# version 05
# baseline (speedup 1.0000x reference)
import numpy as np
import ml_dtypes
import concourse.bass as bass
import concourse.mybir as mybir
from concourse.bass_utils import run_bass_kernel_spmd

F32 = mybir.dt.float32
BF16 = mybir.dt.bfloat16
AF = mybir.ActivationFunctionType
ALU = mybir.AluOpType
AX = mybir.AxisListType.X

T = 2048
D = 2048
E = 16
FF = 768
NH = 64
DH = 32
NTB = 16
NDC = 16
NFB = 6
EPS_LN = 1e-5
EPS_ATT = 1e-15
NEG_BIG_ABS = 1e30


def _ap(x):
    return x if isinstance(x, bass.AP) else x[:]


def cap(x, doff, dims):
    b = _ap(x)
    return bass.AP(tensor=b.tensor, offset=b.offset + doff,
                   ap=[list(q) for q in b.ap[:1]] + [list(d) for d in dims])


class Sem:
    def __init__(self, handle):
        self.h = handle
        self.n = 0

    def inc(self, k=1):
        self.n += k
        return self.n


class Prog:
    ENGINES = ("sync", "gpsimd", "tensor", "vector", "scalar")

    def __init__(self):
        self.ops = {e: [] for e in self.ENGINES}

    def op(self, eng, fn, waits=(), inc=None, inc_k=1):
        val = inc.inc(inc_k) if inc is not None else None

        def emit(h, fn=fn, waits=tuple(waits), inc=inc, inc_k=inc_k):
            for s, v in waits:
                if v > 0:
                    h.wait_ge(s.h, v)
            ins = fn(h)
            if inc is not None:
                assert ins is not None
                ins.then_inc(inc.h, inc_k)
        self.ops[eng].append(emit)
        return val

    def wait(self, eng, sem, val):
        if val and val > 0:
            self.ops[eng].append(lambda h, s=sem.h, v=val: h.wait_ge(s, v))

    def barrier(self, ph):
        target = ph.n + len(self.ENGINES)
        for eng in self.ENGINES:
            def fn(h, s=ph.h, t=target):
                h.nop().then_inc(s, 1)
                h.wait_ge(s, t)
            self.ops[eng].append(fn)
        ph.n = target


def build_nc(max_phase=99, dumps=()):
    nc = bass.Bass("TRN2", target_bir_lowering=False, debug=False)

    x0 = nc.dram_tensor("x0", [T, D], F32, kind="ExternalInput").ap()
    wqkvT = nc.dram_tensor("wqkvT", [D, 3 * D], F32, kind="ExternalInput").ap()
    wprojT = nc.dram_tensor("wprojT", [D, D], F32, kind="ExternalInput").ap()
    bproj = nc.dram_tensor("bproj", [1, D], F32, kind="ExternalInput").ap()
    router = nc.dram_tensor("router", [128, NDC * E], F32, kind="ExternalInput").ap()
    gate_w = nc.dram_tensor("gate_w", [E, D, FF], F32, kind="ExternalInput").ap()
    up_w = nc.dram_tensor("up_w", [E, D, FF], F32, kind="ExternalInput").ap()
    down_w = nc.dram_tensor("down_w", [E, FF, D], F32, kind="ExternalInput").ap()
    ident = nc.dram_tensor("ident", [128, 128], BF16, kind="ExternalInput").ap()
    out = nc.dram_tensor("out", [T, D], F32, kind="ExternalOutput").ap()

    cc_in = nc.dram_tensor("cc_in", [128, NH * 33], F32, kind="Internal")
    cc_out = nc.dram_tensor("cc_out", [128, NH * 33], F32, kind="Internal")

    dump_aps = {}
    for name, shape, dt in [
        ("dbg_xn", [128, NDC * T], BF16),
        ("dbg_comb", [128, NTB * E], F32),
        ("dbg_q", [128, NTB * D], BF16),
        ("dbg_kv", [128, NH * 33], F32),
        ("dbg_attn", [128, NTB * D], BF16),
        ("dbg_h", [128, NFB * T], BF16),
    ]:
        if name in dumps:
            dump_aps[name] = nc.dram_tensor(name, shape, dt, kind="ExternalOutput").ap()

    p = Prog()

    sem_names = ["ph", "s_x", "s_w", "s_wfree", "s_acc", "s_mm", "s_ev", "s_act",
                 "s_gp", "s_init", "s_dump"]
    sem_ctxs = [nc.semaphore(n) for n in sem_names]
    S = {n: Sem(c.__enter__()) for n, c in zip(sem_names, sem_ctxs)}
    ph, s_x, s_w, s_wfree, s_acc, s_mm, s_ev, s_act, s_gp, s_init, s_dump = (
        S[n] for n in sem_names)

    sb_ctx = []

    def sbuf(name, shape, dt):
        c = nc.sbuf_tensor(name, shape, dt)
        t = c.__enter__()
        sb_ctx.append(c)
        return t

    xn_fm = sbuf("xn_fm", [128, NDC * T], BF16)
    xn_tok = sbuf("xn_tok", [128, D], BF16)
    comb = sbuf("comb", [128, NTB * E], F32)
    xs = [sbuf(f"xs{i}", [128, D], F32) for i in range(2)]
    y_sb = [sbuf(f"ysb{i}", [128, D], F32) for i in range(2)]
    ident_bf = sbuf("ident_bf", [128, 128], BF16)
    router_bf = sbuf("router_bf", [128, NDC * E], BF16)
    small = sbuf("small", [128, 32], F32)
    stt = sbuf("stt", [128, 48], F32)
    sm16 = sbuf("sm16", [128, 3 * E], F32)

    p.op("gpsimd", lambda h: h.dma_start(out=ident_bf[:], in_=ident[:, :]), inc=s_w, inc_k=16)
    p.op("vector", lambda h: h.memset(small[:, 21:22], EPS_LN))
    p.op("gpsimd", lambda h: h.dma_start(out=router_bf[:], in_=router[:, :]), inc=s_w, inc_k=16)
    p.wait("tensor", s_w, 32)
    p.barrier(ph)

    def ln_phase(src_ap, init_out, with_router, pt_ps, sc_ps,
                 first_load_waits=(), start_tensor_waits=()):
        res = {'evac': {}, 'comb': {}}
        init_done = {}
        norm_done = {}
        trans_done = {}
        exp_done = {}
        for tb in range(NTB):
            buf = xs[tb % 2]
            r0, r1 = tb * 128, (tb + 1) * 128

            waits = list(first_load_waits) if tb == 0 else []
            if tb >= 2:
                waits = [(s_ev, norm_done[tb - 2])]
                if init_out:
                    waits.append((s_init, init_done[tb - 2]))
            xw = p.op("sync", lambda h, buf=buf, r0=r0, r1=r1:
                      h.dma_start(out=buf[:], in_=src_ap[r0:r1, :]),
                      waits=waits, inc=s_x, inc_k=16)

            if init_out:
                init_done[tb] = p.op("sync", lambda h, buf=buf, r0=r0, r1=r1:
                                     h.dma_start(out=out[r0:r1, :], in_=buf[:]),
                                     waits=[(s_x, xw)], inc=s_init, inc_k=16)

            pa = tb % 2
            sc = 4 * pa
            v1 = None
            for c in range(4):
                v1 = p.op("vector", lambda h, buf=buf, c=c, pa=pa:
                          h.bn_stats(stt[:, 24 * pa + 6 * c:24 * pa + 6 * (c + 1)],
                                     buf[:, 512 * c:512 * (c + 1)]),
                          waits=[(s_x, xw)] if c == 0 else [], inc=s_ev)
            v2 = p.op("vector", lambda h, pa=pa, sc=sc:
                      h.bn_aggr(small[:, sc:sc + 2], stt[:, 24 * pa:24 * (pa + 1)]),
                      waits=[(s_ev, v1)], inc=s_ev)
            a2 = p.op("scalar", lambda h, sc=sc:
                      h.activation(small[:, sc + 2:sc + 3], small[:, sc + 1:sc + 2], AF.Ln, bias=small[:, 21:22]),
                      waits=[(s_ev, v2)], inc=s_act)
            a3 = p.op("scalar", lambda h, sc=sc:
                      h.activation(small[:, sc + 3:sc + 4], small[:, sc + 2:sc + 3], AF.Exp, scale=-0.5),
                      waits=[(s_act, a2)], inc=s_act)
            nwaits = [(s_act, a3)]
            if tb >= 1:
                nwaits.append((s_mm, trans_done[tb - 1]))
            nv = p.op("vector", lambda h, buf=buf, sc=sc:
                      h.tensor_scalar(xn_tok[:], buf[:], small[:, sc:sc + 1], small[:, sc + 3:sc + 4],
                                      ALU.subtract, ALU.mult),
                      waits=nwaits, inc=s_ev)
            norm_done[tb] = nv

            tw = [(s_ev, nv)] + (list(start_tensor_waits) if tb == 0 else [])
            ev_prev = []
            for half in range(2):
                def tmm(h, half=half):
                    ins = None
                    for k in range(8):
                        dc = half * 8 + k
                        ins = h.transpose(pt_ps[half][:, k * 128:(k + 1) * 128],
                                          xn_tok[:, dc * 128:(dc + 1) * 128], ident_bf[:])
                    return ins
                mmv = p.op("tensor", tmm, waits=tw, inc=s_mm)
                tw = []
                if half == 1:
                    trans_done[tb] = mmv
                evv = p.op("vector", lambda h, half=half, tb=tb:
                           h.tensor_copy(
                               xn_fm.rearrange("p (c t) -> p c t", c=NDC)[:, half * 8:(half + 1) * 8, tb * 128:(tb + 1) * 128],
                               pt_ps[half].rearrange("p (c t) -> p c t", c=8)),
                           waits=[(s_mm, mmv)], inc=s_ev)
                ev_prev.append(evv)
            evac_done = ev_prev[-1]
            res['evac'][tb] = evac_done
            p.wait("tensor", s_ev, evac_done)

            if with_router:
                rwaits = []
                if tb >= 2:
                    rwaits.append((s_act, exp_done[tb - 2]))

                def tr(h, tb=tb):
                    ins = None
                    for dc in range(NDC):
                        ins = h.matmul(sc_ps[tb % 2][:, 0:E],
                                       xn_fm[:, dc * T + tb * 128: dc * T + (tb + 1) * 128],
                                       router_bf[:, dc * E:(dc + 1) * E],
                                       start=(dc == 0), stop=(dc == NDC - 1))
                    return ins
                rw = p.op("tensor", tr, waits=rwaits, inc=s_mm)

                c1 = p.op("vector", lambda h, tb=tb:
                          h.reduce_max(small[:, 16:17], sc_ps[tb % 2][:, 0:E], axis=AX, negate=True),
                          waits=[(s_mm, rw)], inc=s_ev)
                a3 = p.op("scalar", lambda h, tb=tb:
                          h.activation(sm16[:, 0:E], sc_ps[tb % 2][:, 0:E], AF.Exp, bias=small[:, 16:17]),
                          waits=[(s_ev, c1)], inc=s_act)
                exp_done[tb] = a3
                c2 = p.op("vector", lambda h: h.reduce_max(small[:, 17:18], sm16[:, 0:E], axis=AX),
                          waits=[(s_act, a3)], inc=s_ev)
                c3 = p.op("vector", lambda h: h.tensor_scalar(sm16[:, E:2 * E], sm16[:, 0:E], small[:, 17:18], None, ALU.is_ge),
                          waits=[(s_ev, c2)], inc=s_ev)
                c4 = p.op("vector", lambda h: h.scalar_tensor_tensor(
                              sm16[:, 2 * E:3 * E], sm16[:, E:2 * E], -NEG_BIG_ABS,
                              sm16[:, 0:E], ALU.mult, ALU.add),
                          waits=[(s_ev, c3)], inc=s_ev)
                c5 = p.op("vector", lambda h: h.reduce_max(small[:, 18:19], sm16[:, 2 * E:3 * E], axis=AX),
                          waits=[(s_ev, c4)], inc=s_ev)
                c6 = p.op("vector", lambda h: h.tensor_tensor(small[:, 19:20], small[:, 17:18], small[:, 18:19], ALU.add),
                          waits=[(s_ev, c5)], inc=s_ev)
                c6b = p.op("vector", lambda h: h.tensor_scalar(sm16[:, E:2 * E], sm16[:, 0:E], small[:, 18:19], None, ALU.is_ge),
                           waits=[(s_ev, c5)], inc=s_ev)
                c7 = p.op("vector", lambda h: h.reciprocal(small[:, 20:21], small[:, 19:20]),
                          waits=[(s_ev, c6)], inc=s_ev)
                c8 = p.op("vector", lambda h: h.tensor_scalar(small[:, 20:21], small[:, 20:21], 0.5, None, ALU.mult),
                          waits=[(s_ev, c7)], inc=s_ev)
                p.op("vector", lambda h, tb=tb:
                     h.scalar_tensor_tensor(comb[:, tb * E:(tb + 1) * E], sm16[:, 0:E],
                                            small[:, 20:21], sm16[:, E:2 * E], ALU.mult, ALU.mult),
                     waits=[(s_ev, max(c8, c6b))], inc=s_ev)
        res['all_ev'] = s_ev.n
        res['all_act'] = s_act.n
        return res

    def ffn_phase(out_ap, wg, wu, wd, h_fm, s_sb, g_ps, u_ps, y_ps,
                  acc_base, init_sem_val, ln_res, dump_h=None):
        silu_done = {}; tt_done = {}; gu_used = {}; d_used = {}
        ts_done = {}
        acc_done = {}
        NU = NFB * 2

        def load_gu(e, waits):
            p.op("gpsimd", lambda h, e=e:
                 h.dma_start(out=wg.rearrange("p (c f) -> p c f", c=NDC),
                             in_=gate_w[e, :, :].rearrange("(c p) f -> p c f", p=128)),
                 waits=waits, inc=s_w, inc_k=16)
            uv_ = p.op("gpsimd", lambda h, e=e:
                       h.dma_start(out=wu.rearrange("p (c f) -> p c f", c=NDC),
                                   in_=up_w[e, :, :].rearrange("(c p) f -> p c f", p=128)),
                       inc=s_w, inc_k=16)
            return uv_

        def gu_chunk(e, j, waits):
            src = gate_w if j < 4 else up_w
            dst = wg if j < 4 else wu
            c0 = (j % 4) * 4
            return p.op("gpsimd", lambda h, e=e, c0=c0, src=src, dst=dst:
                        h.dma_start(
                            out=dst.rearrange("p (c f) -> p c f", c=NDC)[:, c0:c0 + 4, :],
                            in_=src[e, :, :].rearrange("(c p) f -> p c f", p=128)[:, c0:c0 + 4, :]),
                        waits=waits, inc=s_w, inc_k=16)

        w_gu = {0: load_gu(0, [])}
        w_dn = {}
        for e in range(E):
            w_dn[e] = p.op("gpsimd", lambda h, e=e:
                           h.dma_start(out=wd.rearrange("p (c d) -> p c d", c=NFB),
                                       in_=down_w[e, :, :].rearrange("(c p) d -> p c d", p=128)),
                           waits=[] if e == 0 else [(s_mm, d_used[e - 1])], inc=s_w, inc_k=16)
            w_up = w_gu[e]
            w_down = w_dn[e]

            units = [(fb, th) for th in range(2) for fb in range(NFB)]
            for k, (fb, th) in enumerate(units):
                ku = e * NU + k
                waits = []
                if e == 0 and k == 0:
                    waits.append((s_ev, ln_res['evac'][7]))
                if e == 0 and k == NFB:
                    waits.append((s_ev, ln_res['evac'][NTB - 1]))
                if k == 0:
                    waits.append((s_w, w_up))
                if ku - 1 >= 0 and (ku - 1) in silu_done:
                    waits.append((s_act, silu_done[ku - 1]))

                def gmm(h, e=e, fb=fb, th=th):
                    ins = None
                    for dc in range(NDC):
                        for half in range(2):
                            t0 = th * 1024 + half * 512
                            ins = h.matmul(g_ps[:, half * 512:(half + 1) * 512],
                                           wg[:, dc * FF + fb * 128:dc * FF + (fb + 1) * 128],
                                           xn_fm[:, dc * T + t0:dc * T + t0 + 512],
                                           start=(dc == 0), stop=(dc == NDC - 1))
                    return ins
                gv = p.op("tensor", gmm, waits=waits, inc=s_mm)
                uwaits = []
                if (ku - 1) in tt_done:
                    uwaits.append((s_ev, tt_done[ku - 1]))

                def umm(h, e=e, fb=fb, th=th):
                    ins = None
                    for dc in range(NDC):
                        for half in range(2):
                            t0 = th * 1024 + half * 512
                            ins = h.matmul(u_ps[:, half * 512:(half + 1) * 512],
                                           wu[:, dc * FF + fb * 128:dc * FF + (fb + 1) * 128],
                                           xn_fm[:, dc * T + t0:dc * T + t0 + 512],
                                           start=(dc == 0), stop=(dc == NDC - 1))
                    return ins
                uv = p.op("tensor", umm, waits=uwaits, inc=s_mm)
                if k == len(units) - 1:
                    gu_used[e] = uv
                sv = p.op("scalar", lambda h, ku=ku: h.activation(s_sb[ku % 2][:], g_ps[:], AF.Silu),
                          waits=[(s_mm, gv)] + ([(s_ev, tt_done[ku - 2])] if (ku - 2) in tt_done else []),
                          inc=s_act)
                silu_done[ku] = sv
                tv = p.op("vector", lambda h, fb=fb, th=th, ku=ku:
                          h.tensor_tensor(h_fm[:, fb * T + th * 1024:fb * T + (th + 1) * 1024],
                                          s_sb[ku % 2][:], u_ps[:], ALU.mult),
                          waits=[(s_mm, uv), (s_act, sv)], inc=s_ev)
                tt_done[ku] = tv

            if dump_h is not None and e == 0:
                p.wait("sync", s_ev, tt_done[NU - 1])
                p.op("sync", lambda h: h.dma_start(out=dump_h[:, :], in_=h_fm[:]), inc=s_dump, inc_k=16)

            for tb in range(NTB):
                for half in range(2):
                    buf = (tb * 2 + half) % 2
                    waits = []
                    if (tb, half) == (0, 0):
                        waits += [(s_w, w_down), (s_ev, tt_done[e * NU + NU - 1])]
                    if e == 0 and (tb, half) == (0, 0):
                        waits += [(s_ev, ln_res['all_ev']), (s_act, ln_res['all_act'])]
                    key = (e, tb - 1, half) if tb > 0 else (e - 1, NTB - 1, half)
                    if key in ts_done:
                        waits.append((s_ev, ts_done[key]))

                    def dmm(h, e=e, tb=tb, half=half, buf=buf):
                        ins = None
                        for fc in range(NFB):
                            for n2 in range(2):
                                n0 = half * 1024 + n2 * 512
                                ins = h.matmul(y_ps[buf][:, n2 * 512:(n2 + 1) * 512],
                                               h_fm[:, fc * T + tb * 128:fc * T + (tb + 1) * 128],
                                               wd[:, fc * D + n0:fc * D + n0 + 512],
                                               start=(fc == 0), stop=(fc == NFB - 1))
                        return ins
                    yv = p.op("tensor", dmm, waits=waits, inc=s_mm)
                    if (tb, half) == (NTB - 1, 1):
                        d_used[e] = yv
                    ewaits = [(s_mm, yv)]
                    if (e, tb - 2) in acc_done:
                        ewaits.append((s_acc, acc_done[(e, tb - 2)]))
                    elif tb < 2 and (e - 1, NTB - 2 + tb) in acc_done:
                        ewaits.append((s_acc, acc_done[(e - 1, NTB - 2 + tb)]))
                    tsv = p.op("vector", lambda h, e=e, tb=tb, half=half, buf=buf:
                               h.tensor_scalar(y_sb[tb % 2][:, half * 1024:(half + 1) * 1024],
                                               y_ps[buf][:], comb[:, tb * E + e:tb * E + e + 1],
                                               None, ALU.mult),
                               waits=ewaits, inc=s_ev)
                    ts_done[(e, tb, half)] = tsv
                if e + 1 < E and 2 <= tb <= 9:
                    j = tb - 2
                    w_gu[e + 1] = gu_chunk(e + 1, j,
                                           [(s_mm, gu_used[e])] if j == 0 else [])
                awaits = [(s_ev, ts_done[(e, tb, 1)])]
                if e == 0 and tb == 0:
                    awaits.append(init_sem_val)
                if e > 0 and tb == 0:
                    awaits.append((s_acc, acc_base + 256 * e))
                av = p.op("gpsimd", lambda h, tb=tb:
                          h.dma_start(out=out_ap[tb * 128:(tb + 1) * 128, :], in_=y_sb[tb % 2][:],
                                      accum_op=ALU.add),
                          waits=awaits, inc=s_acc, inc_k=16)
                acc_done[(e, tb)] = av

    q_dram = nc.dram_tensor("q_dram", [T, D], BF16, kind="Internal").ap()

    def sweep_phase(sw_w, k_tok, v133, kv_small, q_stage, kq_ps, v_ps, kv_ps, kv_bd,
                    ln_res, zero_done):
        res = {}
        last_kq_user = {}
        last_v_user = {}
        kv_mm_done = {}
        v133_done = {}
        relu_done = {}
        qdma_done = {}
        kv_evac = {}
        scat = {}
        first = True
        wv = {}
        def load_kv_w(g, kwaits, vwaits):
            kw_ = p.op("gpsimd", lambda h, g=g: h.dma_start(
                out=sw_w[0].rearrange("p (c w) -> p c w", c=NDC),
                in_=wqkvT.rearrange("(c p) n -> p c n", p=128)[:, :, D + 512 * g:D + 512 * (g + 1)]),
                waits=kwaits, inc=s_w, inc_k=16)
            vw_ = p.op("gpsimd", lambda h, g=g: h.dma_start(
                out=sw_w[1].rearrange("p (c w) -> p c w", c=NDC),
                in_=wqkvT.rearrange("(c p) n -> p c n", p=128)[:, :, 2 * D + 512 * g:2 * D + 512 * (g + 1)]),
                waits=vwaits, inc=s_w, inc_k=16)
            return kw_, vw_

        kvw_load = {0: load_kv_w(0, [], [])}
        for g in range(4):
            kw, vw = kvw_load[g]

            for tb in range(NTB):
                buf = tb % 2
                waits = [(s_w, kw)] if tb == 0 else []
                if g == 0:
                    waits.append((s_ev, ln_res['evac'][tb]))
                if buf in last_kq_user:
                    waits.append((s_act, last_kq_user[buf]))

                def kmm(h, g=g, tb=tb, buf=buf):
                    ins = None
                    for dc in range(NDC):
                        ins = h.matmul(kq_ps[buf][:], xn_fm[:, dc * T + tb * 128:dc * T + (tb + 1) * 128],
                                       sw_w[0][:, dc * 512:(dc + 1) * 512],
                                       start=(dc == 0), stop=(dc == NDC - 1))
                    return ins
                kmv = p.op("tensor", kmm, waits=waits, inc=s_mm)
                if tb == NTB - 1:
                    wv[('k', g)] = kmv
                rv = p.op("scalar", lambda h, tb=tb, buf=buf:
                          h.activation(k_tok[:, tb * 512:(tb + 1) * 512], kq_ps[buf][:], AF.Relu),
                          waits=[(s_mm, kmv)], inc=s_act)
                relu_done[('k', g, tb)] = rv
                last_kq_user[buf] = rv

            def kvmms(h, g=g, tb=None):
                ins = None
                for hp in range(16):
                    ins = h.matmul(kv_ps[:, 512 * (hp // 8) + 33 * (hp % 8):512 * (hp // 8) + 33 * (hp % 8) + 33],
                                   k_tok[:, tb * 512 + 32 * hp:tb * 512 + 32 * hp + 32],
                                   v133[tb % 2][:, 33 * hp:33 * hp + 33],
                                   start=(tb == 0 and hp % 8 == 0),
                                   stop=(tb == NTB - 1 and hp % 8 == 7))
                return ins

            for tb in range(NTB):
                buf = tb % 2
                waits = [(s_w, vw)] if tb == 0 else []
                if g == 0 and tb == 0:
                    waits += [(s_ev, ln_res['all_ev']), (s_act, ln_res['all_act'])]
                if buf in last_v_user:
                    waits.append((s_ev, last_v_user[buf]))

                def vmm(h, g=g, tb=tb, buf=buf):
                    ins = None
                    for dc in range(NDC):
                        ins = h.matmul(v_ps[buf][:], xn_fm[:, dc * T + tb * 128:dc * T + (tb + 1) * 128],
                                       sw_w[1][:, dc * 512:(dc + 1) * 512],
                                       start=(dc == 0), stop=(dc == NDC - 1))
                    return ins
                vmv = p.op("tensor", vmm, waits=waits, inc=s_mm)
                if tb == NTB - 1:
                    wv[('v', g)] = vmv
                cwaits = [(s_mm, vmv)]
                if (g, tb - 2) in kv_mm_done:
                    cwaits.append((s_mm, kv_mm_done[(g, tb - 2)]))
                elif tb < 2 and (g - 1, NTB - 2 + tb) in kv_mm_done:
                    cwaits.append((s_mm, kv_mm_done[(g - 1, NTB - 2 + tb)]))
                cv = p.op("vector", lambda h, tb=tb, buf=buf:
                          h.tensor_copy(v133[buf].rearrange("p (i j) -> p i j", i=16)[:, :, 0:32],
                                        v_ps[buf].rearrange("p (i j) -> p i j", i=16)),
                          waits=cwaits, inc=s_ev)
                v133_done[(g, tb)] = cv
                last_v_user[buf] = cv
                if tb >= 1:
                    kvw = [(s_ev, v133_done[(g, tb - 1)]), (s_act, relu_done[('k', g, tb - 1)])]
                    if tb == 1 and g > 0:
                        kvw.append((s_ev, kv_evac[g - 1]))
                    kv_mm_done[(g, tb - 1)] = p.op("tensor", lambda h, g=g, tb=tb - 1: kvmms(h, g=g, tb=tb),
                                                   waits=kvw, inc=s_mm)
            kvw = [(s_ev, v133_done[(g, NTB - 1)]), (s_act, relu_done[('k', g, NTB - 1)])]
            kv_mm_done[(g, NTB - 1)] = p.op("tensor", lambda h, g=g: kvmms(h, g=g, tb=NTB - 1),
                                            waits=kvw, inc=s_mm)
            if g + 1 < 4:
                kvw_load[g + 1] = load_kv_w(g + 1, [(s_mm, wv[('k', g)])],
                                            [(s_mm, wv[('v', g)])])
            kv_evac[g] = p.op("vector", lambda h:
                              h.tensor_copy(kv_small[0:32, :].rearrange("p (b w) -> p b w", b=2),
                                            kv_ps[0:32, :].rearrange("p (b w) -> p b w", b=2)[:, :, 0:264]),
                              waits=[(s_mm, kv_mm_done[(g, NTB - 1)])], inc=s_ev)
            for a in range(4):
                scat[g] = p.op("gpsimd", lambda h, g=g, a=a: h.dma_start(
                    out=bass.AP(tensor=cc_in, offset=(32 * a) * 2112 + 528 * g + 33 * a,
                                ap=[[2112, 32], [132, 4], [1, 33]]),
                    in_=cap(kv_small, 33 * a, [[132, 4], [1, 33]])),
                    waits=([(s_ev, kv_evac[g]), (s_init, zero_done[g])] if a == 0 else []),
                    inc=s_gp, inc_k=16)

        def load_q_w(g):
            return p.op("gpsimd", lambda h, g=g: h.dma_start(
                out=sw_w[g % 2].rearrange("p (c w) -> p c w", c=NDC),
                in_=wqkvT.rearrange("(c p) n -> p c n", p=128)[:, :, 512 * g:512 * (g + 1)]),
                waits=[(s_mm, wv[('k', 3) if g == 0 else ('v', 3)])] if g < 2 else [(s_mm, wv[('q', g - 2)])],
                inc=s_w, inc_k=16)

        qw_load = {0: load_q_w(0), 1: load_q_w(1)}
        ccv = p.op("gpsimd", lambda h: h.collective_compute(
            "AllReduce", ALU.add, replica_groups=[list(range(8))],
            ins=[cc_in.ap().opt()], outs=[cc_out.ap().opt()]),
            waits=[(s_gp, scat[3])], inc=s_gp, inc_k=1)

        for g in range(4):
            qw = qw_load[g]
            for tb in range(NTB):
                buf = tb % 2
                waits = [(s_w, qw)] if tb == 0 else []
                if buf in last_kq_user:
                    waits.append((s_act, last_kq_user[buf]))

                def qmm(h, g=g, tb=tb, buf=buf):
                    ins = None
                    for dc in range(NDC):
                        ins = h.matmul(kq_ps[buf][:], xn_fm[:, dc * T + tb * 128:dc * T + (tb + 1) * 128],
                                       sw_w[g % 2][:, dc * 512:(dc + 1) * 512],
                                       start=(dc == 0), stop=(dc == NDC - 1))
                    return ins
                qmv = p.op("tensor", qmm, waits=waits, inc=s_mm)
                if tb == NTB - 1:
                    wv[('q', g)] = qmv
                rwaits = [(s_mm, qmv)]
                if (g, tb - 2) in qdma_done:
                    rwaits.append((s_x, qdma_done[(g, tb - 2)]))
                elif tb < 2 and (g - 1, NTB - 2 + tb) in qdma_done:
                    rwaits.append((s_x, qdma_done[(g - 1, NTB - 2 + tb)]))
                rv = p.op("scalar", lambda h, tb=tb, buf=buf:
                          h.activation(q_stage[buf][:], kq_ps[buf][:], AF.Relu),
                          waits=rwaits, inc=s_act)
                relu_done[('q', g, tb)] = rv
                last_kq_user[buf] = rv
                qdma_done[(g, tb)] = p.op("sync", lambda h, g=g, tb=tb, buf=buf:
                                          h.dma_start(out=q_dram[tb * 128:(tb + 1) * 128, 512 * g:512 * (g + 1)],
                                                      in_=q_stage[buf][:]),
                                          waits=[(s_act, rv)], inc=s_x, inc_k=16)
            if g < 2:
                qw_load[g + 2] = load_q_w(g + 2)
            if g == 0:
                res['kv_bd_w'] = p.op("gpsimd", lambda h: h.dma_start(out=kv_bd[:], in_=cc_out[:, :]),
                                      waits=[(s_gp, ccv)], inc=s_w, inc_k=16)
        res['q_done'] = qdma_done[(3, NTB - 1)]
        res['q_mm_done'] = wv[('q', 3)]
        return res

    def att_phase(wproj_bf, kv_bd, q_tb, q_fm_tb, attn_tok, attn_fm, bias_bc,
                  dens, tq_ps, att_ps, y_ps, sw, ln2_loads, ffn1_acc_end,
                  dump_q=None, dump_attn=None):
        wpv = p.op("gpsimd", lambda h: h.dma_start(
            out=wproj_bf.rearrange("p (c w) -> p c w", c=NDC),
            in_=wprojT.rearrange("(c p) n -> p c n", p=128)),
            waits=[(s_mm, sw['q_mm_done']), (s_x, sw['q_done'])], inc=s_w, inc_k=16)
        bbv = p.op("gpsimd", lambda h: h.dma_start(out=bias_bc[:], in_=bproj.broadcast_to([128, D])),
                   inc=s_w, inc_k=16)
        SUBQ = [(0, 0, 6, 3), (1, 6, 6, 3), (0, 12, 4, 2)]
        qload = {}; qt_done = {}; qfm_ev = {}; att_done = {}; norm_done = {}
        afm_ev = {}; at_done = {}; yev_done = {}; acc_done = {}
        for tb in range(NTB):
            lw = [(s_x, sw['q_done'])] if tb == 0 else []
            if tb >= 2:
                lw.append((s_mm, qt_done[tb - 2]))
            qload[tb] = p.op("sync", lambda h, tb=tb:
                             h.dma_start(out=q_tb[tb % 2][:], in_=q_dram[tb * 128:(tb + 1) * 128, :]),
                             waits=lw, inc=s_x, inc_k=16)
            tw = [(s_x, qload[tb])]
            if tb >= 1:
                tw.append((s_ev, afm_ev[tb - 1]))
            last = None
            for half in range(2):
                def qtr(h, tb=tb, half=half):
                    ins = None
                    for k in range(8):
                        qc = half * 8 + k
                        ins = h.transpose(tq_ps[half][:, k * 128:(k + 1) * 128],
                                         q_tb[tb % 2][:, qc * 128:(qc + 1) * 128], ident_bf[:])
                    return ins
                last = p.op("tensor", qtr, waits=tw if half == 0 else [], inc=s_mm)
            qt_done[tb] = last
            evw0 = [(s_mm, qt_done[tb] - 1)]
            evw1 = [(s_mm, qt_done[tb])]
            if (tb - 2, 2) in att_done:
                evw0.append((s_mm, att_done[(tb - 2, 2)]))
            p.op("vector", lambda h, tb=tb:
                 h.tensor_copy(q_fm_tb[tb % 2][:, 0:1024], tq_ps[0][:]),
                 waits=evw0, inc=s_ev)
            qfm_ev[tb] = p.op("vector", lambda h, tb=tb:
                              h.tensor_copy(q_fm_tb[tb % 2][:, 1024:2048], tq_ps[1][:]),
                              waits=evw1, inc=s_ev)

            for si, (abuf, qc0, n, kdim) in enumerate(SUBQ):
                mwaits = [(s_ev, qfm_ev[tb])]
                if tb == 0 and si == 0:
                    mwaits.append((s_w, sw['kv_bd_w']))
                key = {0: (tb - 1, 2), 1: (tb - 1, 1), 2: (tb, 0)}[si]
                if key in norm_done:
                    mwaits.append((s_ev, norm_done[key]))

                def amms(h, tb=tb, abuf=abuf, qc0=qc0, n=n, kdim=kdim):
                    ins = None
                    for i in range(n):
                        qc = qc0 + i
                        col = 512 * (i // kdim) + 132 * (i % kdim)
                        ins = h.matmul(att_ps[abuf][:, col:col + 132],
                                       q_fm_tb[tb % 2][:, qc * 128:(qc + 1) * 128],
                                       kv_bd[:, qc * 132:(qc + 1) * 132],
                                       start=True, stop=True)
                    return ins
                amv = p.op("tensor", amms, waits=mwaits, inc=s_mm)
                att_done[(tb, si)] = amv
                g1 = p.op("vector", lambda h, abuf=abuf, n=n, kdim=kdim:
                          h.tensor_copy(dens[:, 0:4 * n],
                                        cap(att_ps[abuf], 32, [[512, 2], [132, kdim], [33, 4]])),
                          waits=[(s_mm, amv)], inc=s_ev)
                g2 = p.op("vector", lambda h, n=n:
                          h.tensor_scalar(dens[:, 0:4 * n], dens[:, 0:4 * n], EPS_ATT, None, ALU.add),
                          waits=[(s_ev, g1)], inc=s_ev)
                g3 = p.op("vector", lambda h, n=n: h.reciprocal(dens[:, 32:32 + 4 * n], dens[:, 0:4 * n]),
                          waits=[(s_ev, g2)], inc=s_ev)
                nwaits = [(s_ev, g3)]
                if si == 0 and (tb - 1) in at_done:
                    nwaits.append((s_mm, at_done[tb - 1]))
                nv = p.op("vector", lambda h, tb=tb, abuf=abuf, qc0=qc0, n=n, kdim=kdim:
                          h.tensor_tensor(
                              cap(attn_tok, qc0 * 128, [[128 * kdim, 2], [128, kdim], [32, 4], [1, 32]]),
                              cap(att_ps[abuf], 0, [[512, 2], [132, kdim], [33, 4], [1, 32]]),
                              cap(dens, 32, [[4 * kdim, 2], [4, kdim], [1, 4], [0, 32]]),
                              ALU.mult),
                          waits=nwaits, inc=s_ev)
                norm_done[(tb, si)] = nv

            atw = [(s_ev, norm_done[(tb, 2)]), (s_ev, qfm_ev[tb])]
            last = None
            for half in range(2):
                def atr(h, tb=tb, half=half):
                    ins = None
                    for k in range(8):
                        cc = half * 8 + k
                        ins = h.transpose(tq_ps[half][:, k * 128:(k + 1) * 128],
                                         attn_tok[:, cc * 128:(cc + 1) * 128], ident_bf[:])
                    return ins
                last = p.op("tensor", atr, waits=atw if half == 0 else [], inc=s_mm)
            at_done[tb] = last
            p.op("vector", lambda h, tb=tb: h.tensor_copy(attn_fm[:, 0:1024], tq_ps[0][:]),
                 waits=[(s_mm, at_done[tb] - 1)], inc=s_ev)
            afm_ev[tb] = p.op("vector", lambda h, tb=tb: h.tensor_copy(attn_fm[:, 1024:2048], tq_ps[1][:]),
                              waits=[(s_mm, at_done[tb])], inc=s_ev)

            for sweep in range(2):
                pw = [(s_ev, afm_ev[tb])]
                if tb == 0 and sweep == 0:
                    pw.append((s_w, wpv))
                key = (tb, sweep - 1) if sweep == 1 else (tb - 1, 1)
                if key in yev_done:
                    pw.append((s_ev, yev_done[key]))

                def pmms(h, tb=tb, sweep=sweep):
                    ins = None
                    for cc in range(NDC):
                        for n2 in range(2):
                            n0 = sweep * 1024 + n2 * 512
                            ins = h.matmul(y_ps[:, n2 * 512:(n2 + 1) * 512],
                                           attn_fm[:, cc * 128:(cc + 1) * 128],
                                           wproj_bf[:, cc * D + n0:cc * D + n0 + 512],
                                           start=(cc == 0), stop=(cc == NDC - 1))
                    return ins
                pv = p.op("tensor", pmms, waits=pw, inc=s_mm)
                ew = [(s_mm, pv)]
                if tb == 0 and sweep == 0:
                    ew.append((s_acc, ffn1_acc_end))
                if (tb - 2) in acc_done:
                    ew.append((s_acc, acc_done[tb - 2]))
                yev_done[(tb, sweep)] = p.op("vector", lambda h, tb=tb, sweep=sweep:
                                             h.tensor_tensor(y_sb[tb % 2][:, sweep * 1024:(sweep + 1) * 1024],
                                                             y_ps[:], bias_bc[:, sweep * 1024:(sweep + 1) * 1024],
                                                             ALU.add),
                                             waits=ew, inc=s_ev)
            aw = [(s_ev, yev_done[(tb, 1)])]
            if tb == 0:
                aw.append((s_x, ln2_loads))
            acc_done[tb] = p.op("gpsimd", lambda h, tb=tb:
                                h.dma_start(out=out[tb * 128:(tb + 1) * 128, :], in_=y_sb[tb % 2][:],
                                            accum_op=ALU.add),
                                waits=aw, inc=s_acc, inc_k=16)
            if dump_attn is not None:
                p.wait("sync", s_ev, norm_done[(tb, 2)])
                p.op("sync", lambda h, tb=tb: h.dma_start(out=dump_attn[:, tb * D:(tb + 1) * D], in_=attn_tok[:]),
                     inc=s_dump, inc_k=16)

    def dump(name, src):
        ap = dump_aps.get(name)
        if ap is None:
            return
        p.wait("sync", s_ev, s_ev.n)
        p.wait("sync", s_mm, s_mm.n)
        p.wait("sync", s_act, s_act.n)
        p.op("sync", lambda h: h.dma_start(out=ap[:, :], in_=src[:]), inc=s_dump, inc_k=16)

    zrow = xs[0]
    zv = p.op("vector", lambda h: h.memset(zrow[:, 0:528], 0.0), inc=s_ev)
    zero_done = {}
    for g in range(4):
        zero_done[g] = p.op("sync", lambda h, g=g:
                            h.dma_start(out=cc_in[:, 528 * g:528 * (g + 1)], in_=zrow[:, 0:528]),
                            waits=[(s_ev, zv)], inc=s_init, inc_k=16)

    uid = [0]

    def run_ln(src, init_out, with_router, flw):
        uid[0] += 1
        u = uid[0]
        with (
            nc.psum_tensor(f"pt0_{u}", [128, 1024], BF16, side="right") as _pt0,
            nc.psum_tensor(f"pt1_{u}", [128, 1024], BF16, side="right") as _pt1,
            nc.psum_tensor(f"sc0_{u}", [128, 512], F32, side="right") as _sc0,
            nc.psum_tensor(f"sc1_{u}", [128, 512], F32, side="right") as _sc1,
        ):
            return ln_phase(src, init_out=init_out, with_router=with_router,
                            pt_ps=[_pt0, _pt1], sc_ps=[_sc0, _sc1],
                            first_load_waits=flw,
                            start_tensor_waits=[(s_ev, s_ev.n), (s_act, s_act.n)])

    def run_ffn(init_sem_val, ln_res, dump_h=None):
        uid[0] += 1
        u = uid[0]
        with (
            nc.sbuf_tensor(f"wg{u}", [128, NDC * FF], BF16) as wg,
            nc.sbuf_tensor(f"wu{u}", [128, NDC * FF], BF16) as wu,
            nc.sbuf_tensor(f"wd{u}", [128, NFB * D], BF16) as wd,
            nc.sbuf_tensor(f"h_fm{u}", [128, NFB * T], BF16) as h_fm,
            nc.sbuf_tensor(f"ssb0_{u}", [128, 1024], F32) as _s0,
            nc.sbuf_tensor(f"ssb1_{u}", [128, 1024], F32) as _s1,
            nc.psum_tensor(f"g_ps{u}", [128, 1024], F32, side="left") as g_ps,
            nc.psum_tensor(f"u_ps{u}", [128, 1024], F32, side="left") as u_ps,
            nc.psum_tensor(f"y0_ps{u}", [128, 1024], F32, side="right") as _y0,
            nc.psum_tensor(f"y1_ps{u}", [128, 1024], F32, side="right") as _y1,
        ):
            ffn_phase(out, wg, wu, wd, h_fm, [_s0, _s1], g_ps, u_ps, [_y0, _y1],
                      acc_base=s_acc.n, init_sem_val=init_sem_val,
                      ln_res=ln_res, dump_h=dump_h)

    if max_phase >= 1:
        ln1_res = run_ln(x0, True, True, [(s_init, zero_done[3])])
        dump("dbg_xn", xn_fm)
        dump("dbg_comb", comb)
    if max_phase >= 2:
        run_ffn((s_init, s_init.n), ln1_res, dump_h=dump_aps.get("dbg_h"))
    if max_phase >= 3:
        ffn1_acc_end = s_acc.n
        ln2_res = run_ln(out, False, False, [(s_acc, ffn1_acc_end)])
        ln2_loads = s_x.n
    if max_phase >= 4:
        with nc.sbuf_tensor("kv_bd", [128, NH * 33], BF16) as kv_bd:
            with (
                nc.sbuf_tensor("sww0", [128, NDC * 512], BF16) as _w0,
                nc.sbuf_tensor("sww1", [128, NDC * 512], BF16) as _w1,
                nc.sbuf_tensor("k_tok", [128, NTB * 512], BF16) as k_tok,
                nc.sbuf_tensor("v133a", [128, 528], BF16) as _v0,
                nc.sbuf_tensor("v133b", [128, 528], BF16) as _v1,
                nc.sbuf_tensor("kv_small", [32, 528], F32) as kv_small,
                nc.sbuf_tensor("qst0", [128, 512], BF16) as _q0,
                nc.sbuf_tensor("qst1", [128, 512], BF16) as _q1,
                nc.psum_tensor("kq0", [128, 512], F32, side="left") as _kq0,
                nc.psum_tensor("kq1", [128, 512], F32, side="left") as _kq1,
                nc.psum_tensor("vp0", [128, 512], F32, side="right") as _vp0,
                nc.psum_tensor("vp1", [128, 512], F32, side="right") as _vp1,
                nc.psum_tensor("kv_ps", [32, 1024], F32, side="right") as kv_ps,
            ):
                for v1t in (_v0, _v1):
                    p.op("vector", lambda h, v1t=v1t: h.memset(
                        cap(v1t, 32, [[33, 16], [1, 1]]), 1.0))
                sw = sweep_phase([_w0, _w1], k_tok, [_v0, _v1], kv_small, [_q0, _q1],
                                 [_kq0, _kq1], [_vp0, _vp1], kv_ps, kv_bd,
                                 ln_res=ln2_res, zero_done=zero_done)
            if max_phase >= 5:
                with (
                    nc.sbuf_tensor("wproj_bf", [128, NDC * D], BF16) as wproj_bf,
                    nc.sbuf_tensor("qtb0", [128, D], BF16) as _qt0,
                    nc.sbuf_tensor("qtb1", [128, D], BF16) as _qt1,
                    nc.sbuf_tensor("qfm0", [128, D], BF16) as _qf0,
                    nc.sbuf_tensor("qfm1", [128, D], BF16) as _qf1,
                    nc.sbuf_tensor("attn_tok", [128, D], BF16) as attn_tok,
                    nc.sbuf_tensor("attn_fm", [128, D], BF16) as attn_fm,
                    nc.sbuf_tensor("bias_bc", [128, D], F32) as bias_bc,
                    nc.sbuf_tensor("dens", [128, 64], F32) as dens,
                    nc.psum_tensor("tq0", [128, 1024], BF16) as _tq0,
                    nc.psum_tensor("tq1", [128, 1024], BF16) as _tq1,
                    nc.psum_tensor("attA", [128, 1024], F32) as _aa,
                    nc.psum_tensor("attB", [128, 1024], F32) as _ab,
                    nc.psum_tensor("ypj", [128, 1024], F32) as ypj,
                ):
                    att_phase(wproj_bf, kv_bd, [_qt0, _qt1], [_qf0, _qf1],
                              attn_tok, attn_fm, bias_bc, dens,
                              [_tq0, _tq1], [_aa, _ab], ypj, sw,
                              ln2_loads=ln2_loads, ffn1_acc_end=ffn1_acc_end,
                              dump_attn=dump_aps.get("dbg_attn"))
    if max_phase >= 6:
        att_acc_end = s_acc.n
        ln3_res = run_ln(out, False, True, [(s_acc, att_acc_end)])
        ln3_loads = s_x.n
    if max_phase >= 7:
        run_ffn((s_x, ln3_loads), ln3_res)

    def fin(h):
        for s in (s_x, s_init, s_acc, s_dump):
            if s.n > 0:
                h.wait_ge(s.h, s.n)
    p.ops["sync"].append(fin)
    p.barrier(ph)

    with nc.Block() as block:
        @block.sync
        def _(h):
            for fn in p.ops["sync"]:
                fn(h)

        @block.gpsimd
        def _(h):
            for fn in p.ops["gpsimd"]:
                fn(h)

        @block.tensor
        def _(h):
            for fn in p.ops["tensor"]:
                fn(h)

        @block.vector
        def _(h):
            for fn in p.ops["vector"]:
                fn(h)

        @block.scalar
        def _(h):
            for fn in p.ops["scalar"]:
                fn(h)

    for c in reversed(sb_ctx):
        c.__exit__(None, None, None)
    for c in reversed(sem_ctxs):
        c.__exit__(None, None, None)
    return nc



def _prep_inputs(L0_attn_x, L0_attn_w_qkv, L0_attn_w_proj, L0_attn_b_proj,
                 L0_ffn_router_w, L0_ffn_gate_w, L0_ffn_up_w, L0_ffn_down_w):
    x = np.ascontiguousarray(np.asarray(L0_attn_x, dtype=np.float32)[0])
    shards = np.split(x, 8, axis=0)
    wqkvT = np.ascontiguousarray(np.asarray(L0_attn_w_qkv, dtype=np.float32).T)
    wprojT = np.ascontiguousarray(np.asarray(L0_attn_w_proj, dtype=np.float32).T)
    bproj = np.ascontiguousarray(np.asarray(L0_attn_b_proj, dtype=np.float32)[None, :])
    router = np.ascontiguousarray(
        np.asarray(L0_ffn_router_w, dtype=np.float32).reshape(NDC, 128, E).transpose(1, 0, 2).reshape(128, NDC * E))
    gate = np.ascontiguousarray(np.asarray(L0_ffn_gate_w, dtype=np.float32))
    up = np.ascontiguousarray(np.asarray(L0_ffn_up_w, dtype=np.float32))
    down = np.ascontiguousarray(np.asarray(L0_ffn_down_w, dtype=np.float32))
    ident_np = np.eye(128, dtype=ml_dtypes.bfloat16)
    common = dict(wqkvT=wqkvT, wprojT=wprojT, bproj=bproj, router=router,
                  gate_w=gate, up_w=up, down_w=down, ident=ident_np)
    return [dict(x0=shards[i], **common) for i in range(8)]


def kernel(**inputs):
    in_maps = _prep_inputs(**inputs)
    nc = build_nc()
    res = run_bass_kernel_spmd(nc, in_maps, core_ids=list(range(8)))
    outv = np.concatenate([np.asarray(r["out"]) for r in res.results], axis=0)
    return outv[None, :, :]


# version 06
# speedup vs baseline: 1.0142x; 1.0142x over previous
import numpy as np
import ml_dtypes
import concourse.bass as bass
import concourse.mybir as mybir
from concourse.bass_utils import run_bass_kernel_spmd

F32 = mybir.dt.float32
BF16 = mybir.dt.bfloat16
AF = mybir.ActivationFunctionType
ALU = mybir.AluOpType
AX = mybir.AxisListType.X

T = 2048
D = 2048
E = 16
FF = 768
NH = 64
DH = 32
NTB = 16
NDC = 16
NFB = 6
EPS_LN = 1e-5
EPS_ATT = 1e-15
NEG_BIG_ABS = 1e30


def _ap(x):
    return x if isinstance(x, bass.AP) else x[:]


def cap(x, doff, dims):
    b = _ap(x)
    return bass.AP(tensor=b.tensor, offset=b.offset + doff,
                   ap=[list(q) for q in b.ap[:1]] + [list(d) for d in dims])


class Sem:
    def __init__(self, handle):
        self.h = handle
        self.n = 0

    def inc(self, k=1):
        self.n += k
        return self.n


class Prog:
    ENGINES = ("sync", "gpsimd", "tensor", "vector", "scalar")

    def __init__(self):
        self.ops = {e: [] for e in self.ENGINES}

    def op(self, eng, fn, waits=(), inc=None, inc_k=1):
        val = inc.inc(inc_k) if inc is not None else None

        def emit(h, fn=fn, waits=tuple(waits), inc=inc, inc_k=inc_k):
            for s, v in waits:
                if v > 0:
                    h.wait_ge(s.h, v)
            ins = fn(h)
            if inc is not None:
                assert ins is not None
                ins.then_inc(inc.h, inc_k)
        self.ops[eng].append(emit)
        return val

    def wait(self, eng, sem, val):
        if val and val > 0:
            self.ops[eng].append(lambda h, s=sem.h, v=val: h.wait_ge(s, v))

    def barrier(self, ph):
        target = ph.n + len(self.ENGINES)
        for eng in self.ENGINES:
            def fn(h, s=ph.h, t=target):
                h.nop().then_inc(s, 1)
                h.wait_ge(s, t)
            self.ops[eng].append(fn)
        ph.n = target


def build_nc(max_phase=99, dumps=()):
    nc = bass.Bass("TRN2", target_bir_lowering=False, debug=False)

    x0 = nc.dram_tensor("x0", [T, D], F32, kind="ExternalInput").ap()
    wqkvT = nc.dram_tensor("wqkvT", [D, 3 * D], F32, kind="ExternalInput").ap()
    wprojT = nc.dram_tensor("wprojT", [D, D], F32, kind="ExternalInput").ap()
    bproj = nc.dram_tensor("bproj", [1, D], F32, kind="ExternalInput").ap()
    router = nc.dram_tensor("router", [128, NDC * E], F32, kind="ExternalInput").ap()
    gate_w = nc.dram_tensor("gate_w", [E, D, FF], F32, kind="ExternalInput").ap()
    up_w = nc.dram_tensor("up_w", [E, D, FF], F32, kind="ExternalInput").ap()
    down_w = nc.dram_tensor("down_w", [E, FF, D], F32, kind="ExternalInput").ap()
    ident = nc.dram_tensor("ident", [128, 128], BF16, kind="ExternalInput").ap()
    out = nc.dram_tensor("out", [T, D], F32, kind="ExternalOutput").ap()

    cc_in = nc.dram_tensor("cc_in", [128, NH * 33], F32, kind="Internal")
    cc_out = nc.dram_tensor("cc_out", [128, NH * 33], F32, kind="Internal")

    dump_aps = {}
    for name, shape, dt in [
        ("dbg_xn", [128, NDC * T], BF16),
        ("dbg_comb", [128, NTB * E], F32),
        ("dbg_q", [128, NTB * D], BF16),
        ("dbg_kv", [128, NH * 33], F32),
        ("dbg_attn", [128, NTB * D], BF16),
        ("dbg_h", [128, NFB * T], BF16),
    ]:
        if name in dumps:
            dump_aps[name] = nc.dram_tensor(name, shape, dt, kind="ExternalOutput").ap()

    p = Prog()

    sem_names = ["ph", "s_x", "s_w", "s_wfree", "s_acc", "s_mm", "s_ev", "s_act",
                 "s_gp", "s_init", "s_dump"]
    sem_ctxs = [nc.semaphore(n) for n in sem_names]
    S = {n: Sem(c.__enter__()) for n, c in zip(sem_names, sem_ctxs)}
    ph, s_x, s_w, s_wfree, s_acc, s_mm, s_ev, s_act, s_gp, s_init, s_dump = (
        S[n] for n in sem_names)

    sb_ctx = []

    def sbuf(name, shape, dt):
        c = nc.sbuf_tensor(name, shape, dt)
        t = c.__enter__()
        sb_ctx.append(c)
        return t

    xn_fm = sbuf("xn_fm", [128, NDC * T], BF16)
    xn_tok = sbuf("xn_tok", [128, D], BF16)
    comb = sbuf("comb", [128, NTB * E], F32)
    xs = [sbuf(f"xs{i}", [128, D], F32) for i in range(2)]
    y_sb = [sbuf(f"ysb{i}", [128, D], F32) for i in range(2)]
    ident_bf = sbuf("ident_bf", [128, 128], BF16)
    router_bf = sbuf("router_bf", [128, NDC * E], BF16)
    small = sbuf("small", [128, 32], F32)
    stt = sbuf("stt", [128, 48], F32)
    sm16 = sbuf("sm16", [128, 3 * E], F32)

    p.op("gpsimd", lambda h: h.dma_start(out=ident_bf[:], in_=ident[:, :]), inc=s_w, inc_k=16)
    p.op("vector", lambda h: h.memset(small[:, 21:22], EPS_LN))
    p.op("gpsimd", lambda h: h.dma_start(out=router_bf[:], in_=router[:, :]), inc=s_w, inc_k=16)
    p.wait("tensor", s_w, 32)
    p.barrier(ph)

    def ln_phase(src_ap, init_out, with_router, pt_ps, sc_ps,
                 first_load_waits=(), start_tensor_waits=(), per_tb_load_waits=None):
        res = {'evac': {}, 'comb': {}}
        init_done = {}
        norm_done = {}
        trans_done = {}
        exp_done = {}
        for tb in range(NTB):
            buf = xs[tb % 2]
            r0, r1 = tb * 128, (tb + 1) * 128

            waits = list(first_load_waits) if tb == 0 else []
            if per_tb_load_waits is not None:
                waits = waits + list(per_tb_load_waits[tb])
            if tb >= 2:
                waits = [(s_ev, norm_done[tb - 2])]
                if init_out:
                    waits.append((s_init, init_done[tb - 2]))
            xw = p.op("sync", lambda h, buf=buf, r0=r0, r1=r1:
                      h.dma_start(out=buf[:], in_=src_ap[r0:r1, :]),
                      waits=waits, inc=s_x, inc_k=16)

            if init_out:
                init_done[tb] = p.op("sync", lambda h, buf=buf, r0=r0, r1=r1:
                                     h.dma_start(out=out[r0:r1, :], in_=buf[:]),
                                     waits=[(s_x, xw)], inc=s_init, inc_k=16)

            pa = tb % 2
            sc = 4 * pa
            v1 = None
            for c in range(4):
                v1 = p.op("vector", lambda h, buf=buf, c=c, pa=pa:
                          h.bn_stats(stt[:, 24 * pa + 6 * c:24 * pa + 6 * (c + 1)],
                                     buf[:, 512 * c:512 * (c + 1)]),
                          waits=[(s_x, xw)] if c == 0 else [], inc=s_ev)
            v2 = p.op("vector", lambda h, pa=pa, sc=sc:
                      h.bn_aggr(small[:, sc:sc + 2], stt[:, 24 * pa:24 * (pa + 1)]),
                      waits=[(s_ev, v1)], inc=s_ev)
            a2 = p.op("scalar", lambda h, sc=sc:
                      h.activation(small[:, sc + 2:sc + 3], small[:, sc + 1:sc + 2], AF.Ln, bias=small[:, 21:22]),
                      waits=[(s_ev, v2)], inc=s_act)
            a3 = p.op("scalar", lambda h, sc=sc:
                      h.activation(small[:, sc + 3:sc + 4], small[:, sc + 2:sc + 3], AF.Exp, scale=-0.5),
                      waits=[(s_act, a2)], inc=s_act)
            nwaits = [(s_act, a3)]
            if tb >= 1:
                nwaits.append((s_mm, trans_done[tb - 1]))
            nv = p.op("vector", lambda h, buf=buf, sc=sc:
                      h.tensor_scalar(xn_tok[:], buf[:], small[:, sc:sc + 1], small[:, sc + 3:sc + 4],
                                      ALU.subtract, ALU.mult),
                      waits=nwaits, inc=s_ev)
            norm_done[tb] = nv

            tw = [(s_ev, nv)] + (list(start_tensor_waits) if tb == 0 else [])
            ev_prev = []
            for half in range(2):
                def tmm(h, half=half):
                    ins = None
                    for k in range(8):
                        dc = half * 8 + k
                        ins = h.transpose(pt_ps[half][:, k * 128:(k + 1) * 128],
                                          xn_tok[:, dc * 128:(dc + 1) * 128], ident_bf[:])
                    return ins
                mmv = p.op("tensor", tmm, waits=tw, inc=s_mm)
                tw = []
                if half == 1:
                    trans_done[tb] = mmv
                evv = p.op("vector", lambda h, half=half, tb=tb:
                           h.tensor_copy(
                               xn_fm.rearrange("p (c t) -> p c t", c=NDC)[:, half * 8:(half + 1) * 8, tb * 128:(tb + 1) * 128],
                               pt_ps[half].rearrange("p (c t) -> p c t", c=8)),
                           waits=[(s_mm, mmv)], inc=s_ev)
                ev_prev.append(evv)
            evac_done = ev_prev[-1]
            res['evac'][tb] = evac_done
            p.wait("tensor", s_ev, evac_done)

            if with_router:
                rwaits = []
                if tb >= 2:
                    rwaits.append((s_act, exp_done[tb - 2]))

                def tr(h, tb=tb):
                    ins = None
                    for dc in range(NDC):
                        ins = h.matmul(sc_ps[tb % 2][:, 0:E],
                                       xn_fm[:, dc * T + tb * 128: dc * T + (tb + 1) * 128],
                                       router_bf[:, dc * E:(dc + 1) * E],
                                       start=(dc == 0), stop=(dc == NDC - 1))
                    return ins
                rw = p.op("tensor", tr, waits=rwaits, inc=s_mm)

                c1 = p.op("vector", lambda h, tb=tb:
                          h.reduce_max(small[:, 16:17], sc_ps[tb % 2][:, 0:E], axis=AX, negate=True),
                          waits=[(s_mm, rw)], inc=s_ev)
                a3 = p.op("scalar", lambda h, tb=tb:
                          h.activation(sm16[:, 0:E], sc_ps[tb % 2][:, 0:E], AF.Exp, bias=small[:, 16:17]),
                          waits=[(s_ev, c1)], inc=s_act)
                exp_done[tb] = a3
                c2 = p.op("vector", lambda h: h.reduce_max(small[:, 17:18], sm16[:, 0:E], axis=AX),
                          waits=[(s_act, a3)], inc=s_ev)
                c3 = p.op("vector", lambda h: h.tensor_scalar(sm16[:, E:2 * E], sm16[:, 0:E], small[:, 17:18], None, ALU.is_ge),
                          waits=[(s_ev, c2)], inc=s_ev)
                c4 = p.op("vector", lambda h: h.scalar_tensor_tensor(
                              sm16[:, 2 * E:3 * E], sm16[:, E:2 * E], -NEG_BIG_ABS,
                              sm16[:, 0:E], ALU.mult, ALU.add),
                          waits=[(s_ev, c3)], inc=s_ev)
                c5 = p.op("vector", lambda h: h.reduce_max(small[:, 18:19], sm16[:, 2 * E:3 * E], axis=AX),
                          waits=[(s_ev, c4)], inc=s_ev)
                c6 = p.op("vector", lambda h: h.tensor_tensor(small[:, 19:20], small[:, 17:18], small[:, 18:19], ALU.add),
                          waits=[(s_ev, c5)], inc=s_ev)
                c6b = p.op("vector", lambda h: h.tensor_scalar(sm16[:, E:2 * E], sm16[:, 0:E], small[:, 18:19], None, ALU.is_ge),
                           waits=[(s_ev, c5)], inc=s_ev)
                c7 = p.op("vector", lambda h: h.reciprocal(small[:, 20:21], small[:, 19:20]),
                          waits=[(s_ev, c6)], inc=s_ev)
                c8 = p.op("vector", lambda h: h.tensor_scalar(small[:, 20:21], small[:, 20:21], 0.5, None, ALU.mult),
                          waits=[(s_ev, c7)], inc=s_ev)
                p.op("vector", lambda h, tb=tb:
                     h.scalar_tensor_tensor(comb[:, tb * E:(tb + 1) * E], sm16[:, 0:E],
                                            small[:, 20:21], sm16[:, E:2 * E], ALU.mult, ALU.mult),
                     waits=[(s_ev, max(c8, c6b))], inc=s_ev)
        res['all_ev'] = s_ev.n
        res['all_act'] = s_act.n
        return res

    def ffn_phase(out_ap, wg, wu, wd, h_fm, s_sb, g_ps, u_ps, y_ps,
                  acc_base, init_sem_val, ln_res, dump_h=None):
        silu_done = {}; tt_done = {}; gu_used = {}; d_used = {}
        ts_done = {}
        acc_done = {}
        NU = NFB * 2

        def load_gu(e, waits):
            p.op("gpsimd", lambda h, e=e:
                 h.dma_start(out=wg.rearrange("p (c f) -> p c f", c=NDC),
                             in_=gate_w[e, :, :].rearrange("(c p) f -> p c f", p=128)),
                 waits=waits, inc=s_w, inc_k=16)
            uv_ = p.op("gpsimd", lambda h, e=e:
                       h.dma_start(out=wu.rearrange("p (c f) -> p c f", c=NDC),
                                   in_=up_w[e, :, :].rearrange("(c p) f -> p c f", p=128)),
                       inc=s_w, inc_k=16)
            return uv_

        def gu_chunk(e, j, waits):
            src = gate_w if j < 4 else up_w
            dst = wg if j < 4 else wu
            c0 = (j % 4) * 4
            return p.op("gpsimd", lambda h, e=e, c0=c0, src=src, dst=dst:
                        h.dma_start(
                            out=dst.rearrange("p (c f) -> p c f", c=NDC)[:, c0:c0 + 4, :],
                            in_=src[e, :, :].rearrange("(c p) f -> p c f", p=128)[:, c0:c0 + 4, :]),
                        waits=waits, inc=s_w, inc_k=16)

        w_gu = {0: load_gu(0, [])}
        w_dn = {}
        for e in range(E):
            w_dn[e] = p.op("gpsimd", lambda h, e=e:
                           h.dma_start(out=wd.rearrange("p (c d) -> p c d", c=NFB),
                                       in_=down_w[e, :, :].rearrange("(c p) d -> p c d", p=128)),
                           waits=[] if e == 0 else [(s_mm, d_used[e - 1])], inc=s_w, inc_k=16)
            w_up = w_gu[e]
            w_down = w_dn[e]

            units = [(fb, th) for th in range(2) for fb in range(NFB)]
            for k, (fb, th) in enumerate(units):
                ku = e * NU + k
                waits = []
                if e == 0 and k == 0:
                    waits.append((s_ev, ln_res['evac'][7]))
                if e == 0 and k == NFB:
                    waits.append((s_ev, ln_res['evac'][NTB - 1]))
                if k == 0:
                    waits.append((s_w, w_up))
                if ku - 1 >= 0 and (ku - 1) in silu_done:
                    waits.append((s_act, silu_done[ku - 1]))

                def gmm(h, e=e, fb=fb, th=th):
                    ins = None
                    for dc in range(NDC):
                        for half in range(2):
                            t0 = th * 1024 + half * 512
                            ins = h.matmul(g_ps[:, half * 512:(half + 1) * 512],
                                           wg[:, dc * FF + fb * 128:dc * FF + (fb + 1) * 128],
                                           xn_fm[:, dc * T + t0:dc * T + t0 + 512],
                                           start=(dc == 0), stop=(dc == NDC - 1))
                    return ins
                gv = p.op("tensor", gmm, waits=waits, inc=s_mm)
                uwaits = []
                if (ku - 1) in tt_done:
                    uwaits.append((s_ev, tt_done[ku - 1]))

                def umm(h, e=e, fb=fb, th=th):
                    ins = None
                    for dc in range(NDC):
                        for half in range(2):
                            t0 = th * 1024 + half * 512
                            ins = h.matmul(u_ps[:, half * 512:(half + 1) * 512],
                                           wu[:, dc * FF + fb * 128:dc * FF + (fb + 1) * 128],
                                           xn_fm[:, dc * T + t0:dc * T + t0 + 512],
                                           start=(dc == 0), stop=(dc == NDC - 1))
                    return ins
                uv = p.op("tensor", umm, waits=uwaits, inc=s_mm)
                if k == len(units) - 1:
                    gu_used[e] = uv
                sv = p.op("scalar", lambda h, ku=ku: h.activation(s_sb[ku % 2][:], g_ps[:], AF.Silu),
                          waits=[(s_mm, gv)] + ([(s_ev, tt_done[ku - 2])] if (ku - 2) in tt_done else []),
                          inc=s_act)
                silu_done[ku] = sv
                tv = p.op("vector", lambda h, fb=fb, th=th, ku=ku:
                          h.tensor_tensor(h_fm[:, fb * T + th * 1024:fb * T + (th + 1) * 1024],
                                          s_sb[ku % 2][:], u_ps[:], ALU.mult),
                          waits=[(s_mm, uv), (s_act, sv)], inc=s_ev)
                tt_done[ku] = tv

            if dump_h is not None and e == 0:
                p.wait("sync", s_ev, tt_done[NU - 1])
                p.op("sync", lambda h: h.dma_start(out=dump_h[:, :], in_=h_fm[:]), inc=s_dump, inc_k=16)

            for tb in range(NTB):
                for half in range(2):
                    buf = (tb * 2 + half) % 2
                    waits = []
                    if (tb, half) == (0, 0):
                        waits += [(s_w, w_down), (s_ev, tt_done[e * NU + NU - 1])]
                    if e == 0 and (tb, half) == (0, 0):
                        waits += [(s_ev, ln_res['all_ev']), (s_act, ln_res['all_act'])]
                    key = (e, tb - 1, half) if tb > 0 else (e - 1, NTB - 1, half)
                    if key in ts_done:
                        waits.append((s_ev, ts_done[key]))

                    def dmm(h, e=e, tb=tb, half=half, buf=buf):
                        ins = None
                        for fc in range(NFB):
                            for n2 in range(2):
                                n0 = half * 1024 + n2 * 512
                                ins = h.matmul(y_ps[buf][:, n2 * 512:(n2 + 1) * 512],
                                               h_fm[:, fc * T + tb * 128:fc * T + (tb + 1) * 128],
                                               wd[:, fc * D + n0:fc * D + n0 + 512],
                                               start=(fc == 0), stop=(fc == NFB - 1))
                        return ins
                    yv = p.op("tensor", dmm, waits=waits, inc=s_mm)
                    if (tb, half) == (NTB - 1, 1):
                        d_used[e] = yv
                    ewaits = [(s_mm, yv)]
                    if (e, tb - 2) in acc_done:
                        ewaits.append((s_acc, acc_done[(e, tb - 2)]))
                    elif tb < 2 and (e - 1, NTB - 2 + tb) in acc_done:
                        ewaits.append((s_acc, acc_done[(e - 1, NTB - 2 + tb)]))
                    tsv = p.op("vector", lambda h, e=e, tb=tb, half=half, buf=buf:
                               h.tensor_scalar(y_sb[tb % 2][:, half * 1024:(half + 1) * 1024],
                                               y_ps[buf][:], comb[:, tb * E + e:tb * E + e + 1],
                                               None, ALU.mult),
                               waits=ewaits, inc=s_ev)
                    ts_done[(e, tb, half)] = tsv
                if e + 1 < E and 2 <= tb <= 9:
                    j = tb - 2
                    w_gu[e + 1] = gu_chunk(e + 1, j,
                                           [(s_mm, gu_used[e])] if j == 0 else [])
                awaits = [(s_ev, ts_done[(e, tb, 1)])]
                if e == 0 and tb == 0:
                    awaits.append(init_sem_val)
                if e > 0 and tb == 0:
                    awaits.append((s_acc, acc_base + 256 * e))
                av = p.op("gpsimd", lambda h, tb=tb:
                          h.dma_start(out=out_ap[tb * 128:(tb + 1) * 128, :], in_=y_sb[tb % 2][:],
                                      accum_op=ALU.add),
                          waits=awaits, inc=s_acc, inc_k=16)
                acc_done[(e, tb)] = av
        return acc_done

    q_dram = nc.dram_tensor("q_dram", [T, D], BF16, kind="Internal").ap()

    def sweep_phase(sw_w, k_tok, v133, kv_small, q_stage, kq_ps, v_ps, kv_ps, kv_bd,
                    ln_res, zero_done):
        res = {}
        last_kq_user = {}
        last_v_user = {}
        kv_mm_done = {}
        v133_done = {}
        relu_done = {}
        qdma_done = {}
        kv_evac = {}
        scat = {}
        first = True
        wv = {}
        def load_kv_w(g, kwaits, vwaits):
            kw_ = p.op("gpsimd", lambda h, g=g: h.dma_start(
                out=sw_w[0].rearrange("p (c w) -> p c w", c=NDC),
                in_=wqkvT.rearrange("(c p) n -> p c n", p=128)[:, :, D + 512 * g:D + 512 * (g + 1)]),
                waits=kwaits, inc=s_w, inc_k=16)
            vw_ = p.op("gpsimd", lambda h, g=g: h.dma_start(
                out=sw_w[1].rearrange("p (c w) -> p c w", c=NDC),
                in_=wqkvT.rearrange("(c p) n -> p c n", p=128)[:, :, 2 * D + 512 * g:2 * D + 512 * (g + 1)]),
                waits=vwaits, inc=s_w, inc_k=16)
            return kw_, vw_

        kvw_load = {0: load_kv_w(0, [], [])}
        qw_early = []
        for g in range(4):
            kw, vw = kvw_load[g]

            for tb in range(NTB):
                buf = tb % 2
                waits = [(s_w, kw)] if tb == 0 else []
                if g == 0:
                    waits.append((s_ev, ln_res['evac'][tb]))
                if buf in last_kq_user:
                    waits.append((s_act, last_kq_user[buf]))

                def kmm(h, g=g, tb=tb, buf=buf):
                    ins = None
                    for dc in range(NDC):
                        ins = h.matmul(kq_ps[buf][:], xn_fm[:, dc * T + tb * 128:dc * T + (tb + 1) * 128],
                                       sw_w[0][:, dc * 512:(dc + 1) * 512],
                                       start=(dc == 0), stop=(dc == NDC - 1))
                    return ins
                kmv = p.op("tensor", kmm, waits=waits, inc=s_mm)
                if tb == NTB - 1:
                    wv[('k', g)] = kmv
                rv = p.op("scalar", lambda h, tb=tb, buf=buf:
                          h.activation(k_tok[:, tb * 512:(tb + 1) * 512], kq_ps[buf][:], AF.Relu),
                          waits=[(s_mm, kmv)], inc=s_act)
                relu_done[('k', g, tb)] = rv
                last_kq_user[buf] = rv

            def kvmms(h, g=g, tb=None):
                ins = None
                for hp in range(16):
                    ins = h.matmul(kv_ps[:, 512 * (hp // 8) + 33 * (hp % 8):512 * (hp // 8) + 33 * (hp % 8) + 33],
                                   k_tok[:, tb * 512 + 32 * hp:tb * 512 + 32 * hp + 32],
                                   v133[tb % 2][:, 33 * hp:33 * hp + 33],
                                   start=(tb == 0 and hp % 8 == 0),
                                   stop=(tb == NTB - 1 and hp % 8 == 7))
                return ins

            for tb in range(NTB):
                buf = tb % 2
                waits = [(s_w, vw)] if tb == 0 else []
                if g == 0 and tb == 0:
                    waits += [(s_ev, ln_res['all_ev']), (s_act, ln_res['all_act'])]
                if buf in last_v_user:
                    waits.append((s_ev, last_v_user[buf]))

                def vmm(h, g=g, tb=tb, buf=buf):
                    ins = None
                    for dc in range(NDC):
                        ins = h.matmul(v_ps[buf][:], xn_fm[:, dc * T + tb * 128:dc * T + (tb + 1) * 128],
                                       sw_w[1][:, dc * 512:(dc + 1) * 512],
                                       start=(dc == 0), stop=(dc == NDC - 1))
                    return ins
                vmv = p.op("tensor", vmm, waits=waits, inc=s_mm)
                if tb == NTB - 1:
                    wv[('v', g)] = vmv
                cwaits = [(s_mm, vmv)]
                if (g, tb - 2) in kv_mm_done:
                    cwaits.append((s_mm, kv_mm_done[(g, tb - 2)]))
                elif tb < 2 and (g - 1, NTB - 2 + tb) in kv_mm_done:
                    cwaits.append((s_mm, kv_mm_done[(g - 1, NTB - 2 + tb)]))
                cv = p.op("vector", lambda h, tb=tb, buf=buf:
                          h.tensor_copy(v133[buf].rearrange("p (i j) -> p i j", i=16)[:, :, 0:32],
                                        v_ps[buf].rearrange("p (i j) -> p i j", i=16)),
                          waits=cwaits, inc=s_ev)
                v133_done[(g, tb)] = cv
                last_v_user[buf] = cv
                if tb >= 1:
                    kvw = [(s_ev, v133_done[(g, tb - 1)]), (s_act, relu_done[('k', g, tb - 1)])]
                    if tb == 1 and g > 0:
                        kvw.append((s_ev, kv_evac[g - 1]))
                    kv_mm_done[(g, tb - 1)] = p.op("tensor", lambda h, g=g, tb=tb - 1: kvmms(h, g=g, tb=tb),
                                                   waits=kvw, inc=s_mm)
            kvw = [(s_ev, v133_done[(g, NTB - 1)]), (s_act, relu_done[('k', g, NTB - 1)])]
            kv_mm_done[(g, NTB - 1)] = p.op("tensor", lambda h, g=g: kvmms(h, g=g, tb=NTB - 1),
                                            waits=kvw, inc=s_mm)
            if g + 1 < 4:
                kvw_load[g + 1] = load_kv_w(g + 1, [(s_mm, wv[('k', g)])],
                                            [(s_mm, wv[('v', g)])])
            elif g == 3:
                qw_early.append(p.op("gpsimd", lambda h: h.dma_start(
                    out=sw_w[0].rearrange("p (c w) -> p c w", c=NDC),
                    in_=wqkvT.rearrange("(c p) n -> p c n", p=128)[:, :, 0:512]),
                    waits=[(s_mm, wv[('k', 3)])], inc=s_w, inc_k=16))
                qw_early.append(p.op("gpsimd", lambda h: h.dma_start(
                    out=sw_w[1].rearrange("p (c w) -> p c w", c=NDC),
                    in_=wqkvT.rearrange("(c p) n -> p c n", p=128)[:, :, 512:1024]),
                    waits=[(s_mm, wv[('v', 3)])], inc=s_w, inc_k=16))
            kv_evac[g] = p.op("vector", lambda h:
                              h.tensor_copy(kv_small[0:32, :].rearrange("p (b w) -> p b w", b=2),
                                            kv_ps[0:32, :].rearrange("p (b w) -> p b w", b=2)[:, :, 0:264]),
                              waits=[(s_mm, kv_mm_done[(g, NTB - 1)])], inc=s_ev)
            for a in range(4):
                scat[g] = p.op("gpsimd", lambda h, g=g, a=a: h.dma_start(
                    out=bass.AP(tensor=cc_in, offset=(32 * a) * 2112 + 528 * g + 33 * a,
                                ap=[[2112, 32], [132, 4], [1, 33]]),
                    in_=cap(kv_small, 33 * a, [[132, 4], [1, 33]])),
                    waits=([(s_ev, kv_evac[g]), (s_init, zero_done[g])] if a == 0 else []),
                    inc=s_gp, inc_k=16)

        def load_q_w(g):
            return p.op("gpsimd", lambda h, g=g: h.dma_start(
                out=sw_w[g % 2].rearrange("p (c w) -> p c w", c=NDC),
                in_=wqkvT.rearrange("(c p) n -> p c n", p=128)[:, :, 512 * g:512 * (g + 1)]),
                waits=[(s_mm, wv[('k', 3) if g == 0 else ('v', 3)])] if g < 2 else [(s_mm, wv[('q', g - 2)])],
                inc=s_w, inc_k=16)

        qw_load = {0: qw_early[0], 1: qw_early[1]}
        ccv = p.op("gpsimd", lambda h: h.collective_compute(
            "AllReduce", ALU.add, replica_groups=[list(range(8))],
            ins=[cc_in.ap().opt()], outs=[cc_out.ap().opt()]),
            waits=[(s_gp, scat[3])], inc=s_gp, inc_k=1)

        for g in range(4):
            qw = qw_load[g]
            for tb in range(NTB):
                buf = tb % 2
                waits = [(s_w, qw)] if tb == 0 else []
                if buf in last_kq_user:
                    waits.append((s_act, last_kq_user[buf]))

                def qmm(h, g=g, tb=tb, buf=buf):
                    ins = None
                    for dc in range(NDC):
                        ins = h.matmul(kq_ps[buf][:], xn_fm[:, dc * T + tb * 128:dc * T + (tb + 1) * 128],
                                       sw_w[g % 2][:, dc * 512:(dc + 1) * 512],
                                       start=(dc == 0), stop=(dc == NDC - 1))
                    return ins
                qmv = p.op("tensor", qmm, waits=waits, inc=s_mm)
                if tb == NTB - 1:
                    wv[('q', g)] = qmv
                rwaits = [(s_mm, qmv)]
                if (g, tb - 2) in qdma_done:
                    rwaits.append((s_x, qdma_done[(g, tb - 2)]))
                elif tb < 2 and (g - 1, NTB - 2 + tb) in qdma_done:
                    rwaits.append((s_x, qdma_done[(g - 1, NTB - 2 + tb)]))
                rv = p.op("scalar", lambda h, tb=tb, buf=buf:
                          h.activation(q_stage[buf][:], kq_ps[buf][:], AF.Relu),
                          waits=rwaits, inc=s_act)
                relu_done[('q', g, tb)] = rv
                last_kq_user[buf] = rv
                qdma_done[(g, tb)] = p.op("sync", lambda h, g=g, tb=tb, buf=buf:
                                          h.dma_start(out=q_dram[tb * 128:(tb + 1) * 128, 512 * g:512 * (g + 1)],
                                                      in_=q_stage[buf][:]),
                                          waits=[(s_act, rv)], inc=s_x, inc_k=16)
            if g < 2:
                qw_load[g + 2] = load_q_w(g + 2)
            if g == 0:
                res['kv_bd_w'] = p.op("gpsimd", lambda h: h.dma_start(out=kv_bd[:], in_=cc_out[:, :]),
                                      waits=[(s_gp, ccv)], inc=s_w, inc_k=16)
        res['q_done'] = qdma_done[(3, NTB - 1)]
        res['q_mm_done'] = wv[('q', 3)]
        return res

    def att_phase(wproj_bf, kv_bd, q_tb, q_fm_tb, attn_tok, attn_fm, bias_bc,
                  dens, tq_ps, att_ps, y_ps, sw, ln2_loads, ffn1_acc_end,
                  dump_q=None, dump_attn=None):
        wpv = p.op("gpsimd", lambda h: h.dma_start(
            out=wproj_bf.rearrange("p (c w) -> p c w", c=NDC),
            in_=wprojT.rearrange("(c p) n -> p c n", p=128)),
            waits=[(s_mm, sw['q_mm_done']), (s_x, sw['q_done'])], inc=s_w, inc_k=16)
        bbv = p.op("gpsimd", lambda h: h.dma_start(out=bias_bc[:], in_=bproj.broadcast_to([128, D])),
                   inc=s_w, inc_k=16)
        SUBQ = [(0, 0, 6, 3), (1, 6, 6, 3), (0, 12, 4, 2)]
        qload = {}; qt_done = {}; qfm_ev = {}; att_done = {}; norm_done = {}
        afm_ev = {}; at_done = {}; yev_done = {}; acc_done = {}
        for tb in range(NTB):
            lw = [(s_x, sw['q_done'])] if tb == 0 else []
            if tb >= 2:
                lw.append((s_mm, qt_done[tb - 2]))
            qload[tb] = p.op("sync", lambda h, tb=tb:
                             h.dma_start(out=q_tb[tb % 2][:], in_=q_dram[tb * 128:(tb + 1) * 128, :]),
                             waits=lw, inc=s_x, inc_k=16)
            tw = [(s_x, qload[tb])]
            if tb >= 1:
                tw.append((s_ev, afm_ev[tb - 1]))
            last = None
            for half in range(2):
                def qtr(h, tb=tb, half=half):
                    ins = None
                    for k in range(8):
                        qc = half * 8 + k
                        ins = h.transpose(tq_ps[half][:, k * 128:(k + 1) * 128],
                                         q_tb[tb % 2][:, qc * 128:(qc + 1) * 128], ident_bf[:])
                    return ins
                last = p.op("tensor", qtr, waits=tw if half == 0 else [], inc=s_mm)
            qt_done[tb] = last
            evw0 = [(s_mm, qt_done[tb] - 1)]
            evw1 = [(s_mm, qt_done[tb])]
            if (tb - 2, 2) in att_done:
                evw0.append((s_mm, att_done[(tb - 2, 2)]))
            p.op("vector", lambda h, tb=tb:
                 h.tensor_copy(q_fm_tb[tb % 2][:, 0:1024], tq_ps[0][:]),
                 waits=evw0, inc=s_ev)
            qfm_ev[tb] = p.op("vector", lambda h, tb=tb:
                              h.tensor_copy(q_fm_tb[tb % 2][:, 1024:2048], tq_ps[1][:]),
                              waits=evw1, inc=s_ev)

            for si, (abuf, qc0, n, kdim) in enumerate(SUBQ):
                mwaits = [(s_ev, qfm_ev[tb])]
                if tb == 0 and si == 0:
                    mwaits.append((s_w, sw['kv_bd_w']))
                key = {0: (tb - 1, 2), 1: (tb - 1, 1), 2: (tb, 0)}[si]
                if key in norm_done:
                    mwaits.append((s_ev, norm_done[key]))

                def amms(h, tb=tb, abuf=abuf, qc0=qc0, n=n, kdim=kdim):
                    ins = None
                    for i in range(n):
                        qc = qc0 + i
                        col = 512 * (i // kdim) + 132 * (i % kdim)
                        ins = h.matmul(att_ps[abuf][:, col:col + 132],
                                       q_fm_tb[tb % 2][:, qc * 128:(qc + 1) * 128],
                                       kv_bd[:, qc * 132:(qc + 1) * 132],
                                       start=True, stop=True)
                    return ins
                amv = p.op("tensor", amms, waits=mwaits, inc=s_mm)
                att_done[(tb, si)] = amv
                g1 = p.op("vector", lambda h, abuf=abuf, n=n, kdim=kdim:
                          h.tensor_copy(dens[:, 0:4 * n],
                                        cap(att_ps[abuf], 32, [[512, 2], [132, kdim], [33, 4]])),
                          waits=[(s_mm, amv)], inc=s_ev)
                g2 = p.op("vector", lambda h, n=n:
                          h.tensor_scalar(dens[:, 0:4 * n], dens[:, 0:4 * n], EPS_ATT, None, ALU.add),
                          waits=[(s_ev, g1)], inc=s_ev)
                g3 = p.op("vector", lambda h, n=n: h.reciprocal(dens[:, 32:32 + 4 * n], dens[:, 0:4 * n]),
                          waits=[(s_ev, g2)], inc=s_ev)
                nwaits = [(s_ev, g3)]
                if si == 0 and (tb - 1) in at_done:
                    nwaits.append((s_mm, at_done[tb - 1]))
                nv = p.op("vector", lambda h, tb=tb, abuf=abuf, qc0=qc0, n=n, kdim=kdim:
                          h.tensor_tensor(
                              cap(attn_tok, qc0 * 128, [[128 * kdim, 2], [128, kdim], [32, 4], [1, 32]]),
                              cap(att_ps[abuf], 0, [[512, 2], [132, kdim], [33, 4], [1, 32]]),
                              cap(dens, 32, [[4 * kdim, 2], [4, kdim], [1, 4], [0, 32]]),
                              ALU.mult),
                          waits=nwaits, inc=s_ev)
                norm_done[(tb, si)] = nv

            atw = [(s_ev, norm_done[(tb, 2)]), (s_ev, qfm_ev[tb])]
            last = None
            for half in range(2):
                def atr(h, tb=tb, half=half):
                    ins = None
                    for k in range(8):
                        cc = half * 8 + k
                        ins = h.transpose(tq_ps[half][:, k * 128:(k + 1) * 128],
                                         attn_tok[:, cc * 128:(cc + 1) * 128], ident_bf[:])
                    return ins
                last = p.op("tensor", atr, waits=atw if half == 0 else [], inc=s_mm)
            at_done[tb] = last
            p.op("vector", lambda h, tb=tb: h.tensor_copy(attn_fm[:, 0:1024], tq_ps[0][:]),
                 waits=[(s_mm, at_done[tb] - 1)], inc=s_ev)
            afm_ev[tb] = p.op("vector", lambda h, tb=tb: h.tensor_copy(attn_fm[:, 1024:2048], tq_ps[1][:]),
                              waits=[(s_mm, at_done[tb])], inc=s_ev)

            for sweep in range(2):
                pw = [(s_ev, afm_ev[tb])]
                if tb == 0 and sweep == 0:
                    pw.append((s_w, wpv))
                key = (tb, sweep - 1) if sweep == 1 else (tb - 1, 1)
                if key in yev_done:
                    pw.append((s_ev, yev_done[key]))

                def pmms(h, tb=tb, sweep=sweep):
                    ins = None
                    for cc in range(NDC):
                        for n2 in range(2):
                            n0 = sweep * 1024 + n2 * 512
                            ins = h.matmul(y_ps[:, n2 * 512:(n2 + 1) * 512],
                                           attn_fm[:, cc * 128:(cc + 1) * 128],
                                           wproj_bf[:, cc * D + n0:cc * D + n0 + 512],
                                           start=(cc == 0), stop=(cc == NDC - 1))
                    return ins
                pv = p.op("tensor", pmms, waits=pw, inc=s_mm)
                ew = [(s_mm, pv)]
                if tb == 0 and sweep == 0:
                    ew.append((s_acc, ffn1_acc_end))
                if (tb - 2) in acc_done:
                    ew.append((s_acc, acc_done[tb - 2]))
                yev_done[(tb, sweep)] = p.op("vector", lambda h, tb=tb, sweep=sweep:
                                             h.tensor_tensor(y_sb[tb % 2][:, sweep * 1024:(sweep + 1) * 1024],
                                                             y_ps[:], bias_bc[:, sweep * 1024:(sweep + 1) * 1024],
                                                             ALU.add),
                                             waits=ew, inc=s_ev)
            aw = [(s_ev, yev_done[(tb, 1)])]
            if tb == 0:
                aw.append((s_x, ln2_loads))
            acc_done[tb] = p.op("gpsimd", lambda h, tb=tb:
                                h.dma_start(out=out[tb * 128:(tb + 1) * 128, :], in_=y_sb[tb % 2][:],
                                            accum_op=ALU.add),
                                waits=aw, inc=s_acc, inc_k=16)
            if dump_attn is not None:
                p.wait("sync", s_ev, norm_done[(tb, 2)])
                p.op("sync", lambda h, tb=tb: h.dma_start(out=dump_attn[:, tb * D:(tb + 1) * D], in_=attn_tok[:]),
                     inc=s_dump, inc_k=16)
        return {tb: [(s_acc, acc_done[tb])] for tb in range(NTB)}

    def dump(name, src):
        ap = dump_aps.get(name)
        if ap is None:
            return
        p.wait("sync", s_ev, s_ev.n)
        p.wait("sync", s_mm, s_mm.n)
        p.wait("sync", s_act, s_act.n)
        p.op("sync", lambda h: h.dma_start(out=ap[:, :], in_=src[:]), inc=s_dump, inc_k=16)

    zrow = xs[0]
    zv = p.op("vector", lambda h: h.memset(zrow[:, 0:528], 0.0), inc=s_ev)
    zero_done = {}
    for g in range(4):
        zero_done[g] = p.op("sync", lambda h, g=g:
                            h.dma_start(out=cc_in[:, 528 * g:528 * (g + 1)], in_=zrow[:, 0:528]),
                            waits=[(s_ev, zv)], inc=s_init, inc_k=16)

    uid = [0]

    def run_ln(src, init_out, with_router, flw, per_tb=None):
        uid[0] += 1
        u = uid[0]
        with (
            nc.psum_tensor(f"pt0_{u}", [128, 1024], BF16, side="right") as _pt0,
            nc.psum_tensor(f"pt1_{u}", [128, 1024], BF16, side="right") as _pt1,
            nc.psum_tensor(f"sc0_{u}", [128, 512], F32, side="right") as _sc0,
            nc.psum_tensor(f"sc1_{u}", [128, 512], F32, side="right") as _sc1,
        ):
            return ln_phase(src, init_out=init_out, with_router=with_router,
                            pt_ps=[_pt0, _pt1], sc_ps=[_sc0, _sc1],
                            first_load_waits=flw, per_tb_load_waits=per_tb,
                            start_tensor_waits=[(s_ev, s_ev.n), (s_act, s_act.n)])

    def run_ffn(init_sem_val, ln_res, dump_h=None):
        uid[0] += 1
        u = uid[0]
        with (
            nc.sbuf_tensor(f"wg{u}", [128, NDC * FF], BF16) as wg,
            nc.sbuf_tensor(f"wu{u}", [128, NDC * FF], BF16) as wu,
            nc.sbuf_tensor(f"wd{u}", [128, NFB * D], BF16) as wd,
            nc.sbuf_tensor(f"h_fm{u}", [128, NFB * T], BF16) as h_fm,
            nc.sbuf_tensor(f"ssb0_{u}", [128, 1024], F32) as _s0,
            nc.sbuf_tensor(f"ssb1_{u}", [128, 1024], F32) as _s1,
            nc.psum_tensor(f"g_ps{u}", [128, 1024], F32, side="left") as g_ps,
            nc.psum_tensor(f"u_ps{u}", [128, 1024], F32, side="left") as u_ps,
            nc.psum_tensor(f"y0_ps{u}", [128, 1024], F32, side="right") as _y0,
            nc.psum_tensor(f"y1_ps{u}", [128, 1024], F32, side="right") as _y1,
        ):
            acc_done = ffn_phase(out, wg, wu, wd, h_fm, [_s0, _s1], g_ps, u_ps, [_y0, _y1],
                                 acc_base=s_acc.n, init_sem_val=init_sem_val,
                                 ln_res=ln_res, dump_h=dump_h)
            return {tb: [(s_acc, acc_done[(E - 1, tb)])] for tb in range(NTB)}

    if max_phase >= 1:
        ln1_res = run_ln(x0, True, True, [(s_init, zero_done[3])])
        dump("dbg_xn", xn_fm)
        dump("dbg_comb", comb)
    if max_phase >= 2:
        ffn1_tb_acc = run_ffn((s_init, s_init.n), ln1_res, dump_h=dump_aps.get("dbg_h"))
    if max_phase >= 3:
        ffn1_acc_end = s_acc.n
        ln2_res = run_ln(out, False, False, [], per_tb=ffn1_tb_acc)
        ln2_loads = s_x.n
    if max_phase >= 4:
        with nc.sbuf_tensor("kv_bd", [128, NH * 33], BF16) as kv_bd:
            with (
                nc.sbuf_tensor("sww0", [128, NDC * 512], BF16) as _w0,
                nc.sbuf_tensor("sww1", [128, NDC * 512], BF16) as _w1,
                nc.sbuf_tensor("k_tok", [128, NTB * 512], BF16) as k_tok,
                nc.sbuf_tensor("v133a", [128, 528], BF16) as _v0,
                nc.sbuf_tensor("v133b", [128, 528], BF16) as _v1,
                nc.sbuf_tensor("kv_small", [32, 528], F32) as kv_small,
                nc.sbuf_tensor("qst0", [128, 512], BF16) as _q0,
                nc.sbuf_tensor("qst1", [128, 512], BF16) as _q1,
                nc.psum_tensor("kq0", [128, 512], F32, side="left") as _kq0,
                nc.psum_tensor("kq1", [128, 512], F32, side="left") as _kq1,
                nc.psum_tensor("vp0", [128, 512], F32, side="right") as _vp0,
                nc.psum_tensor("vp1", [128, 512], F32, side="right") as _vp1,
                nc.psum_tensor("kv_ps", [32, 1024], F32, side="right") as kv_ps,
            ):
                for v1t in (_v0, _v1):
                    p.op("vector", lambda h, v1t=v1t: h.memset(
                        cap(v1t, 32, [[33, 16], [1, 1]]), 1.0))
                sw = sweep_phase([_w0, _w1], k_tok, [_v0, _v1], kv_small, [_q0, _q1],
                                 [_kq0, _kq1], [_vp0, _vp1], kv_ps, kv_bd,
                                 ln_res=ln2_res, zero_done=zero_done)
            if max_phase >= 5:
                with (
                    nc.sbuf_tensor("wproj_bf", [128, NDC * D], BF16) as wproj_bf,
                    nc.sbuf_tensor("qtb0", [128, D], BF16) as _qt0,
                    nc.sbuf_tensor("qtb1", [128, D], BF16) as _qt1,
                    nc.sbuf_tensor("qfm0", [128, D], BF16) as _qf0,
                    nc.sbuf_tensor("qfm1", [128, D], BF16) as _qf1,
                    nc.sbuf_tensor("attn_tok", [128, D], BF16) as attn_tok,
                    nc.sbuf_tensor("attn_fm", [128, D], BF16) as attn_fm,
                    nc.sbuf_tensor("bias_bc", [128, D], F32) as bias_bc,
                    nc.sbuf_tensor("dens", [128, 64], F32) as dens,
                    nc.psum_tensor("tq0", [128, 1024], BF16) as _tq0,
                    nc.psum_tensor("tq1", [128, 1024], BF16) as _tq1,
                    nc.psum_tensor("attA", [128, 1024], F32) as _aa,
                    nc.psum_tensor("attB", [128, 1024], F32) as _ab,
                    nc.psum_tensor("ypj", [128, 1024], F32) as ypj,
                ):
                    att_tb_acc = att_phase(wproj_bf, kv_bd, [_qt0, _qt1], [_qf0, _qf1],
                                           attn_tok, attn_fm, bias_bc, dens,
                                           [_tq0, _tq1], [_aa, _ab], ypj, sw,
                                           ln2_loads=ln2_loads, ffn1_acc_end=ffn1_acc_end,
                                           dump_attn=dump_aps.get("dbg_attn"))
    if max_phase >= 6:
        att_acc_end = s_acc.n
        ln3_res = run_ln(out, False, True, [], per_tb=att_tb_acc)
        ln3_loads = s_x.n
    if max_phase >= 7:
        run_ffn((s_x, ln3_loads), ln3_res)

    def fin(h):
        for s in (s_x, s_init, s_acc, s_dump):
            if s.n > 0:
                h.wait_ge(s.h, s.n)
    p.ops["sync"].append(fin)
    p.barrier(ph)

    with nc.Block() as block:
        @block.sync
        def _(h):
            for fn in p.ops["sync"]:
                fn(h)

        @block.gpsimd
        def _(h):
            for fn in p.ops["gpsimd"]:
                fn(h)

        @block.tensor
        def _(h):
            for fn in p.ops["tensor"]:
                fn(h)

        @block.vector
        def _(h):
            for fn in p.ops["vector"]:
                fn(h)

        @block.scalar
        def _(h):
            for fn in p.ops["scalar"]:
                fn(h)

    for c in reversed(sb_ctx):
        c.__exit__(None, None, None)
    for c in reversed(sem_ctxs):
        c.__exit__(None, None, None)
    return nc



def _prep_inputs(L0_attn_x, L0_attn_w_qkv, L0_attn_w_proj, L0_attn_b_proj,
                 L0_ffn_router_w, L0_ffn_gate_w, L0_ffn_up_w, L0_ffn_down_w):
    x = np.ascontiguousarray(np.asarray(L0_attn_x, dtype=np.float32)[0])
    shards = np.split(x, 8, axis=0)
    wqkvT = np.ascontiguousarray(np.asarray(L0_attn_w_qkv, dtype=np.float32).T)
    wprojT = np.ascontiguousarray(np.asarray(L0_attn_w_proj, dtype=np.float32).T)
    bproj = np.ascontiguousarray(np.asarray(L0_attn_b_proj, dtype=np.float32)[None, :])
    router = np.ascontiguousarray(
        np.asarray(L0_ffn_router_w, dtype=np.float32).reshape(NDC, 128, E).transpose(1, 0, 2).reshape(128, NDC * E))
    gate = np.ascontiguousarray(np.asarray(L0_ffn_gate_w, dtype=np.float32))
    up = np.ascontiguousarray(np.asarray(L0_ffn_up_w, dtype=np.float32))
    down = np.ascontiguousarray(np.asarray(L0_ffn_down_w, dtype=np.float32))
    ident_np = np.eye(128, dtype=ml_dtypes.bfloat16)
    common = dict(wqkvT=wqkvT, wprojT=wprojT, bproj=bproj, router=router,
                  gate_w=gate, up_w=up, down_w=down, ident=ident_np)
    return [dict(x0=shards[i], **common) for i in range(8)]


def kernel(**inputs):
    in_maps = _prep_inputs(**inputs)
    nc = build_nc()
    res = run_bass_kernel_spmd(nc, in_maps, core_ids=list(range(8)))
    outv = np.concatenate([np.asarray(r["out"]) for r in res.results], axis=0)
    return outv[None, :, :]


# version 07
# speedup vs baseline: 2.0081x; 1.9799x over previous
import numpy as np
import ml_dtypes
import concourse.bass as bass
import concourse.mybir as mybir
from concourse.bass_utils import run_bass_kernel_spmd

F32 = mybir.dt.float32
BF16 = mybir.dt.bfloat16
AF = mybir.ActivationFunctionType
ALU = mybir.AluOpType
AX = mybir.AxisListType.X

T = 2048
D = 2048
E = 16
FF = 768
NH = 64
DH = 32
NTB = 16
NDC = 16
NFB = 6
EPS_LN = 1e-5
EPS_ATT = 1e-15
NEG_BIG_ABS = 1e30


def _ap(x):
    return x if isinstance(x, bass.AP) else x[:]


def cap(x, doff, dims):
    b = _ap(x)
    return bass.AP(tensor=b.tensor, offset=b.offset + doff,
                   ap=[list(q) for q in b.ap[:1]] + [list(d) for d in dims])


class Sem:
    def __init__(self, handle):
        self.h = handle
        self.n = 0

    def inc(self, k=1):
        self.n += k
        return self.n


class Prog:
    ENGINES = ("sync", "gpsimd", "tensor", "vector", "scalar")

    def __init__(self):
        self.ops = {e: [] for e in self.ENGINES}

    def op(self, eng, fn, waits=(), inc=None, inc_k=1):
        val = inc.inc(inc_k) if inc is not None else None

        def emit(h, fn=fn, waits=tuple(waits), inc=inc, inc_k=inc_k):
            for s, v in waits:
                if v > 0:
                    h.wait_ge(s.h, v)
            ins = fn(h)
            if inc is not None:
                assert ins is not None
                ins.then_inc(inc.h, inc_k)
        self.ops[eng].append(emit)
        return val

    def wait(self, eng, sem, val):
        if val and val > 0:
            self.ops[eng].append(lambda h, s=sem.h, v=val: h.wait_ge(s, v))

    def barrier(self, ph):
        target = ph.n + len(self.ENGINES)
        for eng in self.ENGINES:
            def fn(h, s=ph.h, t=target):
                h.nop().then_inc(s, 1)
                h.wait_ge(s, t)
            self.ops[eng].append(fn)
        ph.n = target


def build_nc(max_phase=99, dumps=(), sstage=3):
    nc = bass.Bass("TRN2", target_bir_lowering=False, debug=False)

    x0 = nc.dram_tensor("x0", [T, D], F32, kind="ExternalInput").ap()
    wqkvT = nc.dram_tensor("wqkvT", [D, 3 * D], F32, kind="ExternalInput").ap()
    wprojT = nc.dram_tensor("wprojT", [D, D], F32, kind="ExternalInput").ap()
    bproj = nc.dram_tensor("bproj", [1, D], F32, kind="ExternalInput").ap()
    router = nc.dram_tensor("router", [128, NDC * E], F32, kind="ExternalInput").ap()
    gate_w = nc.dram_tensor("gate_w", [E, D, FF], F32, kind="ExternalInput").ap()
    up_w = nc.dram_tensor("up_w", [E, D, FF], F32, kind="ExternalInput").ap()
    down_w = nc.dram_tensor("down_w", [E, FF, D], F32, kind="ExternalInput").ap()
    ident = nc.dram_tensor("ident", [128, 128], BF16, kind="ExternalInput").ap()
    ident32 = nc.dram_tensor("ident32", [128, 128], F32, kind="ExternalInput").ap()
    ebase = nc.dram_tensor("ebase", [16, 1], F32, kind="ExternalInput").ap()
    out = nc.dram_tensor("out", [T, D], F32, kind="ExternalOutput").ap()

    cc_in = nc.dram_tensor("cc_in", [128, NH * 33], F32, kind="Internal")
    cc_out = nc.dram_tensor("cc_out", [128, NH * 33], F32, kind="Internal")

    dump_aps = {}
    for name, shape, dt in [
        ("dbg_xn", [128, NDC * T], BF16),
        ("dbg_comb", [128, NTB * E], F32),
        ("dbg_q", [128, NTB * D], BF16),
        ("dbg_kv", [128, NH * 33], F32),
        ("dbg_attn", [128, NTB * D], BF16),
        ("dbg_h", [128, NFB * T], BF16),
        ("dbg_dest", [128, 2 * E], F32),
    ]:
        if name in dumps:
            dump_aps[name] = nc.dram_tensor(name, shape, dt, kind="ExternalOutput").ap()

    p = Prog()

    sem_names = ["ph", "s_x", "s_w", "s_wfree", "s_acc", "s_mm", "s_ev", "s_act",
                 "s_gp", "s_init", "s_dump"]
    sem_ctxs = [nc.semaphore(n) for n in sem_names]
    S = {n: Sem(c.__enter__()) for n, c in zip(sem_names, sem_ctxs)}
    ph, s_x, s_w, s_wfree, s_acc, s_mm, s_ev, s_act, s_gp, s_init, s_dump = (
        S[n] for n in sem_names)

    sb_ctx = []

    def sbuf(name, shape, dt):
        c = nc.sbuf_tensor(name, shape, dt)
        t = c.__enter__()
        sb_ctx.append(c)
        return t

    xn_fm = None
    xn_tok = sbuf("xn_tok", [128, D], BF16)
    comb = sbuf("comb", [128, NTB * E], F32)
    xs = [sbuf(f"xs{i}", [128, D], F32) for i in range(2)]
    y_sb = [sbuf(f"ysb{i}", [128, D], F32) for i in range(2)]
    ident_bf = sbuf("ident_bf", [128, 128], BF16)
    router_bf = sbuf("router_bf", [128, NDC * E], BF16)
    small = sbuf("small", [128, 32], F32)
    stt = sbuf("stt", [128, 48], F32)
    ident_f32s = sbuf("ident_f32s", [128, 128], F32)
    ebase_sb = sbuf("ebase_sb", [16, 1], F32)
    sm16 = sbuf("sm16", [128, 3 * E], F32)

    p.op("gpsimd", lambda h: h.dma_start(out=ident_bf[:], in_=ident[:, :]), inc=s_w, inc_k=16)
    p.op("gpsimd", lambda h: h.dma_start(out=ident_f32s[:], in_=ident32[:, :]), inc=s_w, inc_k=16)
    p.op("gpsimd", lambda h: h.dma_start(out=ebase_sb[:], in_=ebase[:, :]), inc=s_w, inc_k=16)
    p.op("vector", lambda h: h.memset(small[:, 21:22], EPS_LN))
    p.op("vector", lambda h: h.memset(small[:, 22:23], 1.0))
    p.op("gpsimd", lambda h: h.dma_start(out=router_bf[:], in_=router[:, :]), inc=s_w, inc_k=16)
    p.wait("tensor", s_w, 32)
    p.barrier(ph)

    def ln_phase(src_ap, init_out, with_router, pt_ps, sc_ps,
                 first_load_waits=(), start_tensor_waits=(), per_tb_load_waits=None,
                 sparse=None):
        res = {'evac': {}, 'comb': {}, 'xne': {}}
        init_done = {}
        norm_done = {}
        trans_done = {}
        exp_done = {}
        exp_mm_done = {}
        for tb in range(NTB):
            buf = xs[tb % 2]
            r0, r1 = tb * 128, (tb + 1) * 128

            waits = list(first_load_waits) if tb == 0 else []
            if per_tb_load_waits is not None:
                waits = waits + list(per_tb_load_waits[tb])
            if tb >= 2:
                waits = [(s_ev, norm_done[tb - 2])]
                if init_out:
                    waits.append((s_init, init_done[tb - 2]))
            xw = p.op("sync", lambda h, buf=buf, r0=r0, r1=r1:
                      h.dma_start(out=buf[:], in_=src_ap[r0:r1, :]),
                      waits=waits, inc=s_x, inc_k=16)

            if init_out:
                init_done[tb] = p.op("sync", lambda h, buf=buf, r0=r0, r1=r1:
                                     h.dma_start(out=out[r0:r1, :], in_=buf[:]),
                                     waits=[(s_x, xw)], inc=s_init, inc_k=16)

            pa = tb % 2
            sc = 4 * pa
            v1 = None
            for c in range(4):
                v1 = p.op("vector", lambda h, buf=buf, c=c, pa=pa:
                          h.bn_stats(stt[:, 24 * pa + 6 * c:24 * pa + 6 * (c + 1)],
                                     buf[:, 512 * c:512 * (c + 1)]),
                          waits=[(s_x, xw)] if c == 0 else [], inc=s_ev)
            v2 = p.op("vector", lambda h, pa=pa, sc=sc:
                      h.bn_aggr(small[:, sc:sc + 2], stt[:, 24 * pa:24 * (pa + 1)]),
                      waits=[(s_ev, v1)], inc=s_ev)
            a2 = p.op("scalar", lambda h, sc=sc:
                      h.activation(small[:, sc + 2:sc + 3], small[:, sc + 1:sc + 2], AF.Ln, bias=small[:, 21:22]),
                      waits=[(s_ev, v2)], inc=s_act)
            a3 = p.op("scalar", lambda h, sc=sc:
                      h.activation(small[:, sc + 3:sc + 4], small[:, sc + 2:sc + 3], AF.Exp, scale=-0.5),
                      waits=[(s_act, a2)], inc=s_act)
            nwaits = [(s_act, a3)]
            if tb >= 1:
                nwaits.append((s_mm, trans_done[tb - 1]))
            if sparse is not None and tb >= 1:
                nwaits.append((s_x, res['xne'][tb - 1]))
            nv = p.op("vector", lambda h, buf=buf, sc=sc:
                      h.tensor_scalar(xn_tok[:], buf[:], small[:, sc:sc + 1], small[:, sc + 3:sc + 4],
                                      ALU.subtract, ALU.mult),
                      waits=nwaits, inc=s_ev)
            norm_done[tb] = nv

            if sparse is not None:
                res['xne'][tb] = p.op("sync", lambda h, tb=tb:
                                      h.dma_start(out=sparse['xne'][tb * 128:(tb + 1) * 128, :], in_=xn_tok[:]),
                                      waits=[(s_ev, nv)], inc=s_x, inc_k=16)

            tw = [(s_ev, nv)] + (list(start_tensor_waits) if tb == 0 else [])
            ev_prev = []
            for half in range(2):
                def tmm(h, half=half):
                    ins = None
                    for k in range(8):
                        dc = half * 8 + k
                        ins = h.transpose(pt_ps[half][:, k * 128:(k + 1) * 128],
                                          xn_tok[:, dc * 128:(dc + 1) * 128], ident_bf[:])
                    return ins
                mmv = p.op("tensor", tmm, waits=tw, inc=s_mm)
                tw = []
                if half == 1:
                    trans_done[tb] = mmv
                if sparse is None:
                    evv = p.op("vector", lambda h, half=half, tb=tb:
                               h.tensor_copy(
                                   xn_fm.rearrange("p (c t) -> p c t", c=NDC)[:, half * 8:(half + 1) * 8, tb * 128:(tb + 1) * 128],
                                   pt_ps[half].rearrange("p (c t) -> p c t", c=8)),
                               waits=[(s_mm, mmv)], inc=s_ev)
                else:
                    ewaits = [(s_mm, mmv)]
                    if tb >= 2 and half == 0:
                        ewaits.append((s_mm, exp_mm_done[tb - 2]))
                    evv = p.op("vector", lambda h, half=half, tb=tb:
                               h.tensor_copy(sparse['stage'][tb % 2][:, half * 1024:(half + 1) * 1024],
                                             pt_ps[half][:]),
                               waits=ewaits, inc=s_ev)
                ev_prev.append(evv)
            evac_done = ev_prev[-1]
            res['evac'][tb] = evac_done
            p.wait("tensor", s_ev, evac_done)

            if with_router:
                rwaits = []
                if tb >= 2:
                    rwaits.append((s_act, exp_done[tb - 2]))

                def tr(h, tb=tb):
                    ins = None
                    for dc in range(NDC):
                        lhsT = (xn_fm[:, dc * T + tb * 128: dc * T + (tb + 1) * 128]
                                if sparse is None else
                                sparse['stage'][tb % 2][:, dc * 128:(dc + 1) * 128])
                        ins = h.matmul(sc_ps[tb % 2][:, 0:E], lhsT,
                                       router_bf[:, dc * E:(dc + 1) * E],
                                       start=(dc == 0), stop=(dc == NDC - 1))
                    return ins
                rw = p.op("tensor", tr, waits=rwaits, inc=s_mm)
                exp_mm_done[tb] = rw

                c1 = p.op("vector", lambda h, tb=tb:
                          h.reduce_max(small[:, 16:17], sc_ps[tb % 2][:, 0:E], axis=AX, negate=True),
                          waits=[(s_mm, rw)], inc=s_ev)
                a3 = p.op("scalar", lambda h, tb=tb:
                          h.activation(sm16[:, 0:E], sc_ps[tb % 2][:, 0:E], AF.Exp, bias=small[:, 16:17]),
                          waits=[(s_ev, c1)], inc=s_act)
                exp_done[tb] = a3
                c2 = p.op("vector", lambda h: h.reduce_max(small[:, 17:18], sm16[:, 0:E], axis=AX),
                          waits=[(s_act, a3)], inc=s_ev)
                sel1ap = sm16[:, E:2 * E] if sparse is None else sparse['sel1'][:, tb * E:(tb + 1) * E]
                selBap = sm16[:, E:2 * E] if sparse is None else sparse['selB'][:, tb * E:(tb + 1) * E]
                c3 = p.op("vector", lambda h, sel1ap=sel1ap: h.tensor_scalar(sel1ap, sm16[:, 0:E], small[:, 17:18], None, ALU.is_ge),
                          waits=[(s_ev, c2)], inc=s_ev)
                c4 = p.op("vector", lambda h, sel1ap=sel1ap: h.scalar_tensor_tensor(
                              sm16[:, 2 * E:3 * E], sel1ap, -NEG_BIG_ABS,
                              sm16[:, 0:E], ALU.mult, ALU.add),
                          waits=[(s_ev, c3)], inc=s_ev)
                c5 = p.op("vector", lambda h: h.reduce_max(small[:, 18:19], sm16[:, 2 * E:3 * E], axis=AX),
                          waits=[(s_ev, c4)], inc=s_ev)
                c6 = p.op("vector", lambda h: h.tensor_tensor(small[:, 19:20], small[:, 17:18], small[:, 18:19], ALU.add),
                          waits=[(s_ev, c5)], inc=s_ev)
                c6b = p.op("vector", lambda h, selBap=selBap: h.tensor_scalar(selBap, sm16[:, 0:E], small[:, 18:19], None, ALU.is_ge),
                           waits=[(s_ev, c5)], inc=s_ev)
                c7 = p.op("vector", lambda h: h.reciprocal(small[:, 20:21], small[:, 19:20]),
                          waits=[(s_ev, c6)], inc=s_ev)
                c8 = p.op("vector", lambda h: h.tensor_scalar(small[:, 20:21], small[:, 20:21], 0.5, None, ALU.mult),
                          waits=[(s_ev, c7)], inc=s_ev)
                if sparse is None:
                    p.op("vector", lambda h, tb=tb:
                         h.scalar_tensor_tensor(comb[:, tb * E:(tb + 1) * E], sm16[:, 0:E],
                                                small[:, 20:21], selBap, ALU.mult, ALU.mult),
                         waits=[(s_ev, max(c8, c6b))], inc=s_ev)
                else:
                    p.op("vector", lambda h, tb=tb:
                         h.tensor_tensor(sparse['c1a'][:, tb:tb + 1], small[:, 17:18], small[:, 20:21], ALU.mult),
                         waits=[(s_ev, c8)], inc=s_ev)
                    p.op("vector", lambda h, tb=tb:
                         h.tensor_tensor(sparse['c2a'][:, tb:tb + 1], small[:, 18:19], small[:, 20:21], ALU.mult),
                         waits=[(s_ev, c8)], inc=s_ev)
        res['all_ev'] = s_ev.n
        res['all_act'] = s_act.n
        res['rmm'] = exp_mm_done
        return res

    def ffn_phase(out_ap, wg, wu, wd, h_fm, s_sb, g_ps, u_ps, y_ps,
                  acc_base, init_sem_val, ln_res, dump_h=None):
        silu_done = {}; tt_done = {}; gu_used = {}; d_used = {}
        ts_done = {}
        acc_done = {}
        NU = NFB * 2

        def load_gu(e, waits):
            p.op("gpsimd", lambda h, e=e:
                 h.dma_start(out=wg.rearrange("p (c f) -> p c f", c=NDC),
                             in_=gate_w[e, :, :].rearrange("(c p) f -> p c f", p=128)),
                 waits=waits, inc=s_w, inc_k=16)
            uv_ = p.op("gpsimd", lambda h, e=e:
                       h.dma_start(out=wu.rearrange("p (c f) -> p c f", c=NDC),
                                   in_=up_w[e, :, :].rearrange("(c p) f -> p c f", p=128)),
                       inc=s_w, inc_k=16)
            return uv_

        def gu_chunk(e, j, waits):
            src = gate_w if j < 4 else up_w
            dst = wg if j < 4 else wu
            c0 = (j % 4) * 4
            return p.op("gpsimd", lambda h, e=e, c0=c0, src=src, dst=dst:
                        h.dma_start(
                            out=dst.rearrange("p (c f) -> p c f", c=NDC)[:, c0:c0 + 4, :],
                            in_=src[e, :, :].rearrange("(c p) f -> p c f", p=128)[:, c0:c0 + 4, :]),
                        waits=waits, inc=s_w, inc_k=16)

        w_gu = {0: load_gu(0, [])}
        w_dn = {}
        for e in range(E):
            w_dn[e] = p.op("gpsimd", lambda h, e=e:
                           h.dma_start(out=wd.rearrange("p (c d) -> p c d", c=NFB),
                                       in_=down_w[e, :, :].rearrange("(c p) d -> p c d", p=128)),
                           waits=[] if e == 0 else [(s_mm, d_used[e - 1])], inc=s_w, inc_k=16)
            w_up = w_gu[e]
            w_down = w_dn[e]

            units = [(fb, th) for th in range(2) for fb in range(NFB)]
            for k, (fb, th) in enumerate(units):
                ku = e * NU + k
                waits = []
                if e == 0 and k == 0:
                    waits.append((s_ev, ln_res['evac'][7]))
                if e == 0 and k == NFB:
                    waits.append((s_ev, ln_res['evac'][NTB - 1]))
                if k == 0:
                    waits.append((s_w, w_up))
                if ku - 1 >= 0 and (ku - 1) in silu_done:
                    waits.append((s_act, silu_done[ku - 1]))

                def gmm(h, e=e, fb=fb, th=th):
                    ins = None
                    for dc in range(NDC):
                        for half in range(2):
                            t0 = th * 1024 + half * 512
                            ins = h.matmul(g_ps[:, half * 512:(half + 1) * 512],
                                           wg[:, dc * FF + fb * 128:dc * FF + (fb + 1) * 128],
                                           xn_fm[:, dc * T + t0:dc * T + t0 + 512],
                                           start=(dc == 0), stop=(dc == NDC - 1))
                    return ins
                gv = p.op("tensor", gmm, waits=waits, inc=s_mm)
                uwaits = []
                if (ku - 1) in tt_done:
                    uwaits.append((s_ev, tt_done[ku - 1]))

                def umm(h, e=e, fb=fb, th=th):
                    ins = None
                    for dc in range(NDC):
                        for half in range(2):
                            t0 = th * 1024 + half * 512
                            ins = h.matmul(u_ps[:, half * 512:(half + 1) * 512],
                                           wu[:, dc * FF + fb * 128:dc * FF + (fb + 1) * 128],
                                           xn_fm[:, dc * T + t0:dc * T + t0 + 512],
                                           start=(dc == 0), stop=(dc == NDC - 1))
                    return ins
                uv = p.op("tensor", umm, waits=uwaits, inc=s_mm)
                if k == len(units) - 1:
                    gu_used[e] = uv
                sv = p.op("scalar", lambda h, ku=ku: h.activation(s_sb[ku % 2][:], g_ps[:], AF.Silu),
                          waits=[(s_mm, gv)] + ([(s_ev, tt_done[ku - 2])] if (ku - 2) in tt_done else []),
                          inc=s_act)
                silu_done[ku] = sv
                tv = p.op("vector", lambda h, fb=fb, th=th, ku=ku:
                          h.tensor_tensor(h_fm[:, fb * T + th * 1024:fb * T + (th + 1) * 1024],
                                          s_sb[ku % 2][:], u_ps[:], ALU.mult),
                          waits=[(s_mm, uv), (s_act, sv)], inc=s_ev)
                tt_done[ku] = tv

            if dump_h is not None and e == 0:
                p.wait("sync", s_ev, tt_done[NU - 1])
                p.op("sync", lambda h: h.dma_start(out=dump_h[:, :], in_=h_fm[:]), inc=s_dump, inc_k=16)

            for tb in range(NTB):
                for half in range(2):
                    buf = (tb * 2 + half) % 2
                    waits = []
                    if (tb, half) == (0, 0):
                        waits += [(s_w, w_down), (s_ev, tt_done[e * NU + NU - 1])]
                    if e == 0 and (tb, half) == (0, 0):
                        waits += [(s_ev, ln_res['all_ev']), (s_act, ln_res['all_act'])]
                    key = (e, tb - 1, half) if tb > 0 else (e - 1, NTB - 1, half)
                    if key in ts_done:
                        waits.append((s_ev, ts_done[key]))

                    def dmm(h, e=e, tb=tb, half=half, buf=buf):
                        ins = None
                        for fc in range(NFB):
                            for n2 in range(2):
                                n0 = half * 1024 + n2 * 512
                                ins = h.matmul(y_ps[buf][:, n2 * 512:(n2 + 1) * 512],
                                               h_fm[:, fc * T + tb * 128:fc * T + (tb + 1) * 128],
                                               wd[:, fc * D + n0:fc * D + n0 + 512],
                                               start=(fc == 0), stop=(fc == NFB - 1))
                        return ins
                    yv = p.op("tensor", dmm, waits=waits, inc=s_mm)
                    if (tb, half) == (NTB - 1, 1):
                        d_used[e] = yv
                    ewaits = [(s_mm, yv)]
                    if (e, tb - 2) in acc_done:
                        ewaits.append((s_acc, acc_done[(e, tb - 2)]))
                    elif tb < 2 and (e - 1, NTB - 2 + tb) in acc_done:
                        ewaits.append((s_acc, acc_done[(e - 1, NTB - 2 + tb)]))
                    tsv = p.op("vector", lambda h, e=e, tb=tb, half=half, buf=buf:
                               h.tensor_scalar(y_sb[tb % 2][:, half * 1024:(half + 1) * 1024],
                                               y_ps[buf][:], comb[:, tb * E + e:tb * E + e + 1],
                                               None, ALU.mult),
                               waits=ewaits, inc=s_ev)
                    ts_done[(e, tb, half)] = tsv
                if e + 1 < E and 2 <= tb <= 9:
                    j = tb - 2
                    w_gu[e + 1] = gu_chunk(e + 1, j,
                                           [(s_mm, gu_used[e])] if j == 0 else [])
                awaits = [(s_ev, ts_done[(e, tb, 1)])]
                if e == 0 and tb == 0:
                    awaits.append(init_sem_val)
                if e > 0 and tb == 0:
                    awaits.append((s_acc, acc_base + 256 * e))
                av = p.op("gpsimd", lambda h, tb=tb:
                          h.dma_start(out=out_ap[tb * 128:(tb + 1) * 128, :], in_=y_sb[tb % 2][:],
                                      accum_op=ALU.add),
                          waits=awaits, inc=s_acc, inc_k=16)
                acc_done[(e, tb)] = av
        return acc_done

    CAP = 512
    DBIG = float(E * CAP)
    xne_dram = nc.dram_tensor("xne", [T, D], BF16, kind="Internal").ap()
    ebuf = nc.dram_tensor("ebuf", [E * CAP + 128, D], BF16, kind="Internal")
    ybuf = nc.dram_tensor("ybuf", [E * CAP + 128, D], F32, kind="Internal")

    sp_uid = [0]

    def ffn_sparse(out_ap, sp, ln_res, init_sem_val, zero_ebuf_val, desttok, desti):
        sp_uid[0] += 1
        su = sp_uid[0]
        """Top-2 MoE: scatter selected xn rows into per-expert capacity buffers,
        run each expert on its CAP tokens, write unscaled y rows to ybuf, then
        per token-block gather the two expert rows and combine with c1/c2.
        Mathematically identical to the dense reference (zero-weight experts
        contribute exact zeros there)."""
        with (
            nc.sbuf_tensor(f"sel1f{su}", [16, T], F32) as sel1f,
            nc.sbuf_tensor(f"indf{su}", [16, T], F32) as indf,
            nc.sbuf_tensor(f"slotf{su}", [16, T], F32) as slotf,
            nc.sbuf_tensor(f"okf{su}", [16, T], F32) as okf,
            nc.sbuf_tensor(f"t1f{su}", [16, T], F32) as t1f,
            nc.sbuf_tensor(f"t2f{su}", [16, T], F32) as t2f,
            nc.psum_tensor(f"selps{su}", [128, T], F32) as selps,
            nc.psum_tensor(f"destps{su}", [1, T], F32) as destps,
        ):
            ev_tr = {}
            for ri, (src, dst) in enumerate([(sp['sel1'], sel1f), (sp['selB'], indf)]):
                tw = [(s_ev, ln_res['all_ev']), (s_act, ln_res['all_act'])] if ri == 0 else [(s_ev, ev_tr[0])]
                last = None
                for tb in range(NTB):
                    def trm(h, src=src, tb=tb):
                        return h.transpose(selps[0:16, tb * 128:(tb + 1) * 128],
                                           src[:, tb * E:(tb + 1) * E], ident_f32s[:])
                    last = p.op("tensor", trm, waits=tw if tb == 0 else [], inc=s_mm)
                ev_tr[ri] = p.op("vector", lambda h, dst=dst: h.tensor_copy(dst[0:16, :], selps[0:16, :]),
                                 waits=[(s_mm, last)], inc=s_ev)
            w1 = p.op("vector", lambda h: h.tensor_tensor_scan(slotf[0:16, :], indf[0:16, :], indf[0:16, :],
                                                               0.0, ALU.add, ALU.bypass),
                      waits=[(s_ev, ev_tr[1])], inc=s_ev)
            w2 = p.op("vector", lambda h: h.tensor_tensor(slotf[0:16, :], slotf[0:16, :], indf[0:16, :], ALU.subtract),
                      waits=[(s_ev, w1)], inc=s_ev)
            w3 = p.op("vector", lambda h: h.tensor_scalar(okf[0:16, :], slotf[0:16, :], float(CAP), None, ALU.is_lt),
                      waits=[(s_ev, w2)], inc=s_ev)
            w4 = p.op("vector", lambda h: h.tensor_scalar(slotf[0:16, :], slotf[0:16, :], ebase_sb[0:16, 0:1], None, ALU.add),
                      waits=[(s_ev, w3)], inc=s_ev)
            dtok_ev = {}
            prev = w4
            for rank in range(2):
                if rank == 1:
                    prev = p.op("vector", lambda h: h.tensor_tensor(indf[0:16, :], indf[0:16, :], sel1f[0:16, :], ALU.subtract),
                                waits=[(s_ev, prev)], inc=s_ev)
                selfm = sel1f if rank == 0 else indf
                q1 = p.op("vector", lambda h, selfm=selfm: h.tensor_tensor(t1f[0:16, :], selfm[0:16, :], okf[0:16, :], ALU.mult),
                          waits=[(s_ev, prev)], inc=s_ev)
                q2 = p.op("vector", lambda h: h.tensor_tensor(t2f[0:16, :], t1f[0:16, :], slotf[0:16, :], ALU.mult),
                          waits=[(s_ev, q1)], inc=s_ev)

                def mm_reduce(h, srcf):
                    ins = None
                    for j in range(4):
                        ins = h.matmul(destps[0:1, 512 * j:512 * (j + 1)],
                                       small[0:16, 22:23], srcf[0:16, 512 * j:512 * (j + 1)],
                                       start=True, stop=True)
                    return ins
                m1v = p.op("tensor", lambda h: mm_reduce(h, t2f), waits=[(s_ev, q2)], inc=s_mm)
                e0 = p.op("vector", lambda h: h.tensor_copy(t2f[0:1, 0:T], destps[0:1, :]),
                          waits=[(s_mm, m1v)], inc=s_ev)
                m2v = p.op("tensor", lambda h: mm_reduce(h, t1f), waits=[(s_ev, e0)], inc=s_mm)
                e1 = p.op("vector", lambda h: h.tensor_scalar(destps[0:1, :], destps[0:1, :],
                                                              -DBIG, DBIG, ALU.mult, ALU.add),
                          waits=[(s_mm, m2v)], inc=s_ev)
                e2 = p.op("vector", lambda h: h.tensor_tensor(t1f[0:1, 0:T], t2f[0:1, 0:T], destps[0:1, :], ALU.add),
                          waits=[(s_ev, e1)], inc=s_ev)
                last = None
                for tb in range(NTB):
                    def trd(h, tb=tb):
                        return h.transpose(selps[0:128, tb:tb + 1],
                                           t1f[0:1, tb * 128:(tb + 1) * 128], ident_f32s[0:1, 0:1])
                    last = p.op("tensor", trd, waits=[(s_ev, e2)] if tb == 0 else [], inc=s_mm)
                dtok_ev[rank] = p.op("vector", lambda h, rank=rank:
                                     h.tensor_copy(desttok[:, rank * E:(rank + 1) * E], selps[0:128, 0:16]),
                                     waits=[(s_mm, last)], inc=s_ev)
                prev = dtok_ev[rank]
            di = p.op("vector", lambda h: h.tensor_copy(desti[:], desttok[:]),
                      waits=[(s_ev, dtok_ev[1])], inc=s_ev)
            if "dbg_dest" in dump_aps:
                p.wait("sync", s_ev, di)
                p.op("sync", lambda h: h.dma_start(out=dump_aps["dbg_dest"][:, :], in_=desttok[:]),
                     inc=s_dump, inc_k=16)
        if sstage < 1:
            return {tb: [] for tb in range(NTB)}

        scat_done = {}
        stg_free = {}
        for tb in range(NTB):
            lw = [(s_x, ln_res['xne'][tb]), (s_mm, ln_res['rmm'][14 + (tb % 2)])]
            if tb >= 2:
                lw.append((s_gp, scat_done[tb - 2]))
            sl = p.op("sync", lambda h, tb=tb:
                      h.dma_start(out=sp['stage'][tb % 2][:], in_=xne_dram[tb * 128:(tb + 1) * 128, :]),
                      waits=lw, inc=s_x, inc_k=16)
            for rank in range(2):
                sw_ = [(s_x, sl), (s_ev, di)] if rank == 0 else []
                if tb == 0 and rank == 0:
                    sw_.append((s_w, zero_ebuf_val))
                scat_done[tb] = p.op("gpsimd", lambda h, tb=tb, rank=rank:
                                     h.indirect_dma_start(
                                         out=ebuf[:, :], out_offset=bass.IndirectOffsetOnAxis(desti[:, rank * E + tb:rank * E + tb + 1], 0),
                                         in_=sp['stage'][tb % 2][:], in_offset=None),
                                     waits=sw_, inc=s_gp, inc_k=16)
        scat_end = scat_done[NTB - 1]
        if sstage < 2:
            return {tb: [] for tb in range(NTB)}

        with (
            nc.sbuf_tensor(f"wg_s{su}", [128, NDC * FF], BF16) as wg,
            nc.sbuf_tensor(f"wu_s{su}", [128, NDC * FF], BF16) as wu,
            nc.sbuf_tensor(f"wd_s{su}", [128, NFB * D], BF16) as wd,
            nc.sbuf_tensor(f"xng{su}", [128, NDC * CAP], BF16) as xng,
            nc.sbuf_tensor(f"hfm_s{su}", [128, NFB * CAP], BF16) as h_fm,
            nc.sbuf_tensor(f"ebsb{su}", [128, 4 * D], BF16) as ebsb,
            nc.sbuf_tensor(f"ssb0_s{su}", [128, 512], F32) as _ss0,
            nc.sbuf_tensor(f"ssb1_s{su}", [128, 512], F32) as _ss1,
            nc.sbuf_tensor(f"bgat{su}", [128, D], F32) as bgat,
            nc.psum_tensor(f"g_s{su}", [128, 512], F32) as g_ps,
            nc.psum_tensor(f"u_s{su}", [128, 512], F32) as u_ps,
            nc.psum_tensor(f"tr0_s{su}", [128, 1024], BF16) as _tr0,
            nc.psum_tensor(f"tr1_s{su}", [128, 1024], BF16) as _tr1,
            nc.psum_tensor(f"y0_s{su}", [128, 512], F32) as _yy0,
            nc.psum_tensor(f"y1_s{su}", [128, 512], F32) as _yy1,
        ):
            s_sb = [_ss0, _ss1]
            tr_ps = [_tr0, _tr1]
            y_ps = [_yy0, _yy1]
            silu_done = {}; tt_done = {}; gu_used = {}; d_used = {}
            load_done = {}; trans_done = {}; yev = {}; st_done = {}
            prev_evg = {}
            for e in range(E):
                w_gate = p.op("gpsimd", lambda h, e=e:
                              h.dma_start(out=wg.rearrange("p (c f) -> p c f", c=NDC),
                                          in_=gate_w[e, :, :].rearrange("(c p) f -> p c f", p=128)),
                              waits=[] if e == 0 else [(s_mm, gu_used[e - 1])], inc=s_w, inc_k=16)
                w_up = p.op("gpsimd", lambda h, e=e:
                            h.dma_start(out=wu.rearrange("p (c f) -> p c f", c=NDC),
                                        in_=up_w[e, :, :].rearrange("(c p) f -> p c f", p=128)),
                            inc=s_w, inc_k=16)
                w_down = p.op("gpsimd", lambda h, e=e:
                              h.dma_start(out=wd.rearrange("p (c d) -> p c d", c=NFB),
                                          in_=down_w[e, :, :].rearrange("(c p) d -> p c d", p=128)),
                              waits=[] if e == 0 else [(s_mm, d_used[e - 1])], inc=s_w, inc_k=16)
                lw = [(s_gp, scat_end)] if e == 0 else [(s_mm, trans_done[e - 1])]
                load_done[e] = p.op("sync", lambda h, e=e:
                                    h.dma_start(out=ebsb.rearrange("p (c w) -> p c w", c=4),
                                                in_=ebuf[:, :].rearrange("(c p) w -> p c w", p=128)[:, 4 * e:4 * (e + 1), :]),
                                    waits=lw, inc=s_x, inc_k=16)
                ev_last = None
                tlast = None
                evg_done = {}
                for grp in range(8):
                    tw = [(s_x, load_done[e])] if grp == 0 else []
                    if grp >= 2:
                        tw.append((s_ev, evg_done[grp - 2]))
                    elif e > 0:
                        tw.append((s_ev, prev_evg[6 + grp]))

                    def trg(h, e=e, grp=grp):
                        ins = None
                        for k in range(8):
                            idx = grp * 8 + k
                            c, dc = idx // NDC, idx % NDC
                            ins = h.transpose(tr_ps[grp % 2][:, k * 128:(k + 1) * 128],
                                              ebsb[:, c * D + dc * 128:c * D + (dc + 1) * 128], ident_bf[:])
                        return ins
                    tlast = p.op("tensor", trg, waits=tw, inc=s_mm)

                    def evg(h, e=e, grp=grp):
                        c, dc0 = (grp * 8) // NDC, (grp * 8) % NDC
                        return h.tensor_copy(
                            cap(xng, dc0 * CAP + c * 128, [[CAP, 8], [1, 128]]),
                            tr_ps[grp % 2][:])
                    ev_last = p.op("vector", evg, waits=[(s_mm, tlast)], inc=s_ev)
                    evg_done[grp] = ev_last
                trans_done[e] = tlast
                prev_evg = evg_done
                xng_ready = ev_last

                for fb in range(NFB):
                    gw_ = [(s_w, w_up), (s_ev, xng_ready)] if fb == 0 else []
                    kup = (e, fb - 1) if fb > 0 else (e - 1, NFB - 1)
                    if kup in silu_done:
                        gw_.append((s_act, silu_done[kup]))

                    def gmm(h, e=e, fb=fb):
                        ins = None
                        for dc in range(NDC):
                            ins = h.matmul(g_ps[:], wg[:, dc * FF + fb * 128:dc * FF + (fb + 1) * 128],
                                           xng[:, dc * CAP:(dc + 1) * CAP],
                                           start=(dc == 0), stop=(dc == NDC - 1))
                        return ins
                    gv = p.op("tensor", gmm, waits=gw_, inc=s_mm)
                    uw_ = []
                    if kup in tt_done:
                        uw_.append((s_ev, tt_done[kup]))

                    def umm(h, e=e, fb=fb):
                        ins = None
                        for dc in range(NDC):
                            ins = h.matmul(u_ps[:], wu[:, dc * FF + fb * 128:dc * FF + (fb + 1) * 128],
                                           xng[:, dc * CAP:(dc + 1) * CAP],
                                           start=(dc == 0), stop=(dc == NDC - 1))
                        return ins
                    uv = p.op("tensor", umm, waits=uw_, inc=s_mm)
                    if fb == NFB - 1:
                        gu_used[e] = uv
                    sv = p.op("scalar", lambda h, fb=fb: h.activation(s_sb[fb % 2][:], g_ps[:], AF.Silu),
                              waits=[(s_mm, gv)] + ([(s_ev, tt_done[(e, fb - 2)])] if (e, fb - 2) in tt_done else []),
                              inc=s_act)
                    silu_done[(e, fb)] = sv
                    tv = p.op("vector", lambda h, fb=fb:
                              h.tensor_tensor(h_fm[:, fb * CAP:(fb + 1) * CAP], s_sb[fb % 2][:], u_ps[:], ALU.mult),
                              waits=[(s_mm, uv), (s_act, sv)], inc=s_ev)
                    tt_done[(e, fb)] = tv

                for gtb in range(4):
                    for n4 in range(4):
                        buf = (gtb * 4 + n4) % 2
                        dw_ = []
                        if gtb == 0 and n4 == 0:
                            dw_ += [(s_w, w_down), (s_ev, tt_done[(e, NFB - 1)])]
                        kprev = (e, gtb, n4 - 2) if n4 >= 2 else (e, gtb - 1, n4 + 2)
                        if gtb == 0 and n4 < 2:
                            kprev = (e - 1, 3, n4 + 2)
                        if kprev in yev:
                            dw_.append((s_ev, yev[kprev]))

                        def dmm(h, e=e, gtb=gtb, n4=n4, buf=buf):
                            ins = None
                            for fc in range(NFB):
                                ins = h.matmul(y_ps[buf][:],
                                               h_fm[:, fc * CAP + gtb * 128:fc * CAP + (gtb + 1) * 128],
                                               wd[:, fc * D + n4 * 512:fc * D + (n4 + 1) * 512],
                                               start=(fc == 0), stop=(fc == NFB - 1))
                            return ins
                        dv = p.op("tensor", dmm, waits=dw_, inc=s_mm)
                        if (gtb, n4) == (3, 3):
                            d_used[e] = dv
                        ew_ = [(s_mm, dv)]
                        if e == 0 and gtb == 0 and n4 == 0:
                            ew_.append((s_w, zero_ebuf_val))
                        if (e, gtb - 2) in st_done:
                            ew_.append((s_x, st_done[(e, gtb - 2)]))
                        elif gtb < 2 and (e - 1, 2 + gtb) in st_done:
                            ew_.append((s_x, st_done[(e - 1, 2 + gtb)]))
                        yev[(e, gtb, n4)] = p.op("vector", lambda h, gtb=gtb, n4=n4, buf=buf:
                                                 h.tensor_copy(y_sb[gtb % 2][:, n4 * 512:(n4 + 1) * 512], y_ps[buf][:]),
                                                 waits=ew_, inc=s_ev)
                    st_done[(e, gtb)] = p.op("sync", lambda h, e=e, gtb=gtb:
                                             h.dma_start(out=ybuf[e * CAP + gtb * 128:e * CAP + (gtb + 1) * 128, :],
                                                         in_=y_sb[gtb % 2][:]),
                                             waits=[(s_ev, yev[(e, gtb, 3)])], inc=s_x, inc_k=16)
            stores_end = st_done[(E - 1, 3)]
            if sstage < 3:
                return {tb: [] for tb in range(NTB)}

            acc_done = {}
            comb_done = {}
            for tb in range(NTB):
                g1w = [(s_x, stores_end)] if tb == 0 else []
                if (tb - 2) in acc_done:
                    g1w.append((s_acc, acc_done[tb - 2]))
                g1 = p.op("gpsimd", lambda h, tb=tb:
                          h.indirect_dma_start(
                              out=y_sb[tb % 2][:], out_offset=None, in_=ybuf[:, :],
                              in_offset=bass.IndirectOffsetOnAxis(desti[:, tb:tb + 1], 0)),
                          waits=g1w, inc=s_gp, inc_k=16)
                g2w = [(s_gp, g1)]
                if (tb - 1) in comb_done:
                    g2w.append((s_ev, comb_done[tb - 1]))
                g2 = p.op("gpsimd", lambda h, tb=tb:
                          h.indirect_dma_start(
                              out=bgat[:], out_offset=None, in_=ybuf[:, :],
                              in_offset=bass.IndirectOffsetOnAxis(desti[:, E + tb:E + tb + 1], 0)),
                          waits=g2w, inc=s_gp, inc_k=16)
                t1v = p.op("vector", lambda h, tb=tb:
                           h.tensor_scalar(y_sb[tb % 2][:], y_sb[tb % 2][:], sp['c1a'][:, tb:tb + 1], None, ALU.mult),
                           waits=[(s_gp, g1)], inc=s_ev)
                comb_done[tb] = p.op("vector", lambda h, tb=tb:
                                     h.scalar_tensor_tensor(y_sb[tb % 2][:], bgat[:], sp['c2a'][:, tb:tb + 1],
                                                            y_sb[tb % 2][:], ALU.mult, ALU.add),
                                     waits=[(s_gp, g2), (s_ev, t1v)], inc=s_ev)
                aw = [(s_ev, comb_done[tb])]
                if tb == 0:
                    aw.append(init_sem_val)
                acc_done[tb] = p.op("gpsimd", lambda h, tb=tb:
                                    h.dma_start(out=out_ap[tb * 128:(tb + 1) * 128, :], in_=y_sb[tb % 2][:],
                                                accum_op=ALU.add),
                                    waits=aw, inc=s_acc, inc_k=16)
        return {tb: [(s_acc, acc_done[tb])] for tb in range(NTB)}

    q_dram = nc.dram_tensor("q_dram", [T, D], BF16, kind="Internal").ap()

    def sweep_phase(sw_w, k_tok, v133, kv_small, q_stage, kq_ps, v_ps, kv_ps, kv_bd,
                    ln_res, zero_done):
        res = {}
        last_kq_user = {}
        last_v_user = {}
        kv_mm_done = {}
        v133_done = {}
        relu_done = {}
        qdma_done = {}
        kv_evac = {}
        scat = {}
        first = True
        wv = {}
        def load_kv_w(g, kwaits, vwaits):
            kw_ = p.op("gpsimd", lambda h, g=g: h.dma_start(
                out=sw_w[0].rearrange("p (c w) -> p c w", c=NDC),
                in_=wqkvT.rearrange("(c p) n -> p c n", p=128)[:, :, D + 512 * g:D + 512 * (g + 1)]),
                waits=kwaits, inc=s_w, inc_k=16)
            vw_ = p.op("gpsimd", lambda h, g=g: h.dma_start(
                out=sw_w[1].rearrange("p (c w) -> p c w", c=NDC),
                in_=wqkvT.rearrange("(c p) n -> p c n", p=128)[:, :, 2 * D + 512 * g:2 * D + 512 * (g + 1)]),
                waits=vwaits, inc=s_w, inc_k=16)
            return kw_, vw_

        kvw_load = {0: load_kv_w(0, [], [])}
        qw_early = []
        for g in range(4):
            kw, vw = kvw_load[g]

            for tb in range(NTB):
                buf = tb % 2
                waits = [(s_w, kw)] if tb == 0 else []
                if g == 0:
                    waits.append((s_ev, ln_res['evac'][tb]))
                if buf in last_kq_user:
                    waits.append((s_act, last_kq_user[buf]))

                def kmm(h, g=g, tb=tb, buf=buf):
                    ins = None
                    for dc in range(NDC):
                        ins = h.matmul(kq_ps[buf][:], xn_fm[:, dc * T + tb * 128:dc * T + (tb + 1) * 128],
                                       sw_w[0][:, dc * 512:(dc + 1) * 512],
                                       start=(dc == 0), stop=(dc == NDC - 1))
                    return ins
                kmv = p.op("tensor", kmm, waits=waits, inc=s_mm)
                if tb == NTB - 1:
                    wv[('k', g)] = kmv
                rv = p.op("scalar", lambda h, tb=tb, buf=buf:
                          h.activation(k_tok[:, tb * 512:(tb + 1) * 512], kq_ps[buf][:], AF.Relu),
                          waits=[(s_mm, kmv)], inc=s_act)
                relu_done[('k', g, tb)] = rv
                last_kq_user[buf] = rv

            def kvmms(h, g=g, tb=None):
                ins = None
                for hp in range(16):
                    ins = h.matmul(kv_ps[:, 512 * (hp // 8) + 33 * (hp % 8):512 * (hp // 8) + 33 * (hp % 8) + 33],
                                   k_tok[:, tb * 512 + 32 * hp:tb * 512 + 32 * hp + 32],
                                   v133[tb % 2][:, 33 * hp:33 * hp + 33],
                                   start=(tb == 0 and hp % 8 == 0),
                                   stop=(tb == NTB - 1 and hp % 8 == 7))
                return ins

            for tb in range(NTB):
                buf = tb % 2
                waits = [(s_w, vw)] if tb == 0 else []
                if g == 0 and tb == 0:
                    waits += [(s_ev, ln_res['all_ev']), (s_act, ln_res['all_act'])]
                if buf in last_v_user:
                    waits.append((s_ev, last_v_user[buf]))

                def vmm(h, g=g, tb=tb, buf=buf):
                    ins = None
                    for dc in range(NDC):
                        ins = h.matmul(v_ps[buf][:], xn_fm[:, dc * T + tb * 128:dc * T + (tb + 1) * 128],
                                       sw_w[1][:, dc * 512:(dc + 1) * 512],
                                       start=(dc == 0), stop=(dc == NDC - 1))
                    return ins
                vmv = p.op("tensor", vmm, waits=waits, inc=s_mm)
                if tb == NTB - 1:
                    wv[('v', g)] = vmv
                cwaits = [(s_mm, vmv)]
                if (g, tb - 2) in kv_mm_done:
                    cwaits.append((s_mm, kv_mm_done[(g, tb - 2)]))
                elif tb < 2 and (g - 1, NTB - 2 + tb) in kv_mm_done:
                    cwaits.append((s_mm, kv_mm_done[(g - 1, NTB - 2 + tb)]))
                cv = p.op("vector", lambda h, tb=tb, buf=buf:
                          h.tensor_copy(v133[buf].rearrange("p (i j) -> p i j", i=16)[:, :, 0:32],
                                        v_ps[buf].rearrange("p (i j) -> p i j", i=16)),
                          waits=cwaits, inc=s_ev)
                v133_done[(g, tb)] = cv
                last_v_user[buf] = cv
                if tb >= 1:
                    kvw = [(s_ev, v133_done[(g, tb - 1)]), (s_act, relu_done[('k', g, tb - 1)])]
                    if tb == 1 and g > 0:
                        kvw.append((s_ev, kv_evac[g - 1]))
                    kv_mm_done[(g, tb - 1)] = p.op("tensor", lambda h, g=g, tb=tb - 1: kvmms(h, g=g, tb=tb),
                                                   waits=kvw, inc=s_mm)
            kvw = [(s_ev, v133_done[(g, NTB - 1)]), (s_act, relu_done[('k', g, NTB - 1)])]
            kv_mm_done[(g, NTB - 1)] = p.op("tensor", lambda h, g=g: kvmms(h, g=g, tb=NTB - 1),
                                            waits=kvw, inc=s_mm)
            if g + 1 < 4:
                kvw_load[g + 1] = load_kv_w(g + 1, [(s_mm, wv[('k', g)])],
                                            [(s_mm, wv[('v', g)])])
            elif g == 3:
                qw_early.append(p.op("gpsimd", lambda h: h.dma_start(
                    out=sw_w[0].rearrange("p (c w) -> p c w", c=NDC),
                    in_=wqkvT.rearrange("(c p) n -> p c n", p=128)[:, :, 0:512]),
                    waits=[(s_mm, wv[('k', 3)])], inc=s_w, inc_k=16))
                qw_early.append(p.op("gpsimd", lambda h: h.dma_start(
                    out=sw_w[1].rearrange("p (c w) -> p c w", c=NDC),
                    in_=wqkvT.rearrange("(c p) n -> p c n", p=128)[:, :, 512:1024]),
                    waits=[(s_mm, wv[('v', 3)])], inc=s_w, inc_k=16))
            kv_evac[g] = p.op("vector", lambda h:
                              h.tensor_copy(kv_small[0:32, :].rearrange("p (b w) -> p b w", b=2),
                                            kv_ps[0:32, :].rearrange("p (b w) -> p b w", b=2)[:, :, 0:264]),
                              waits=[(s_mm, kv_mm_done[(g, NTB - 1)])], inc=s_ev)
            for a in range(4):
                scat[g] = p.op("gpsimd", lambda h, g=g, a=a: h.dma_start(
                    out=bass.AP(tensor=cc_in, offset=(32 * a) * 2112 + 528 * g + 33 * a,
                                ap=[[2112, 32], [132, 4], [1, 33]]),
                    in_=cap(kv_small, 33 * a, [[132, 4], [1, 33]])),
                    waits=([(s_ev, kv_evac[g]), (s_init, zero_done[g])] if a == 0 else []),
                    inc=s_gp, inc_k=16)

        def load_q_w(g):
            return p.op("gpsimd", lambda h, g=g: h.dma_start(
                out=sw_w[g % 2].rearrange("p (c w) -> p c w", c=NDC),
                in_=wqkvT.rearrange("(c p) n -> p c n", p=128)[:, :, 512 * g:512 * (g + 1)]),
                waits=[(s_mm, wv[('k', 3) if g == 0 else ('v', 3)])] if g < 2 else [(s_mm, wv[('q', g - 2)])],
                inc=s_w, inc_k=16)

        qw_load = {0: qw_early[0], 1: qw_early[1]}
        ccv = p.op("gpsimd", lambda h: h.collective_compute(
            "AllReduce", ALU.add, replica_groups=[list(range(8))],
            ins=[cc_in.ap().opt()], outs=[cc_out.ap().opt()]),
            waits=[(s_gp, scat[3])], inc=s_gp, inc_k=1)

        for g in range(4):
            qw = qw_load[g]
            for tb in range(NTB):
                buf = tb % 2
                waits = [(s_w, qw)] if tb == 0 else []
                if buf in last_kq_user:
                    waits.append((s_act, last_kq_user[buf]))

                def qmm(h, g=g, tb=tb, buf=buf):
                    ins = None
                    for dc in range(NDC):
                        ins = h.matmul(kq_ps[buf][:], xn_fm[:, dc * T + tb * 128:dc * T + (tb + 1) * 128],
                                       sw_w[g % 2][:, dc * 512:(dc + 1) * 512],
                                       start=(dc == 0), stop=(dc == NDC - 1))
                    return ins
                qmv = p.op("tensor", qmm, waits=waits, inc=s_mm)
                if tb == NTB - 1:
                    wv[('q', g)] = qmv
                rwaits = [(s_mm, qmv)]
                if (g, tb - 2) in qdma_done:
                    rwaits.append((s_x, qdma_done[(g, tb - 2)]))
                elif tb < 2 and (g - 1, NTB - 2 + tb) in qdma_done:
                    rwaits.append((s_x, qdma_done[(g - 1, NTB - 2 + tb)]))
                rv = p.op("scalar", lambda h, tb=tb, buf=buf:
                          h.activation(q_stage[buf][:], kq_ps[buf][:], AF.Relu),
                          waits=rwaits, inc=s_act)
                relu_done[('q', g, tb)] = rv
                last_kq_user[buf] = rv
                qdma_done[(g, tb)] = p.op("sync", lambda h, g=g, tb=tb, buf=buf:
                                          h.dma_start(out=q_dram[tb * 128:(tb + 1) * 128, 512 * g:512 * (g + 1)],
                                                      in_=q_stage[buf][:]),
                                          waits=[(s_act, rv)], inc=s_x, inc_k=16)
            if g < 2:
                qw_load[g + 2] = load_q_w(g + 2)
            if g == 0:
                res['kv_bd_w'] = p.op("gpsimd", lambda h: h.dma_start(out=kv_bd[:], in_=cc_out[:, :]),
                                      waits=[(s_gp, ccv)], inc=s_w, inc_k=16)
        res['q_done'] = qdma_done[(3, NTB - 1)]
        res['q_mm_done'] = wv[('q', 3)]
        return res

    def att_phase(wproj_bf, kv_bd, q_tb, q_fm_tb, attn_tok, attn_fm, bias_bc,
                  dens, tq_ps, att_ps, y_ps, sw, ln2_loads, ffn1_acc_end,
                  dump_q=None, dump_attn=None):
        wpv = p.op("gpsimd", lambda h: h.dma_start(
            out=wproj_bf.rearrange("p (c w) -> p c w", c=NDC),
            in_=wprojT.rearrange("(c p) n -> p c n", p=128)),
            waits=[(s_mm, sw['q_mm_done']), (s_x, sw['q_done'])], inc=s_w, inc_k=16)
        bbv = p.op("gpsimd", lambda h: h.dma_start(out=bias_bc[:], in_=bproj.broadcast_to([128, D])),
                   inc=s_w, inc_k=16)
        SUBQ = [(0, 0, 6, 3), (1, 6, 6, 3), (0, 12, 4, 2)]
        qload = {}; qt_done = {}; qfm_ev = {}; att_done = {}; norm_done = {}
        afm_ev = {}; at_done = {}; yev_done = {}; acc_done = {}
        for tb in range(NTB):
            lw = [(s_x, sw['q_done'])] if tb == 0 else []
            if tb >= 2:
                lw.append((s_mm, qt_done[tb - 2]))
            qload[tb] = p.op("sync", lambda h, tb=tb:
                             h.dma_start(out=q_tb[tb % 2][:], in_=q_dram[tb * 128:(tb + 1) * 128, :]),
                             waits=lw, inc=s_x, inc_k=16)
            tw = [(s_x, qload[tb])]
            if tb >= 1:
                tw.append((s_ev, afm_ev[tb - 1]))
            last = None
            for half in range(2):
                def qtr(h, tb=tb, half=half):
                    ins = None
                    for k in range(8):
                        qc = half * 8 + k
                        ins = h.transpose(tq_ps[half][:, k * 128:(k + 1) * 128],
                                         q_tb[tb % 2][:, qc * 128:(qc + 1) * 128], ident_bf[:])
                    return ins
                last = p.op("tensor", qtr, waits=tw if half == 0 else [], inc=s_mm)
            qt_done[tb] = last
            evw0 = [(s_mm, qt_done[tb] - 1)]
            evw1 = [(s_mm, qt_done[tb])]
            if (tb - 2, 2) in att_done:
                evw0.append((s_mm, att_done[(tb - 2, 2)]))
            p.op("vector", lambda h, tb=tb:
                 h.tensor_copy(q_fm_tb[tb % 2][:, 0:1024], tq_ps[0][:]),
                 waits=evw0, inc=s_ev)
            qfm_ev[tb] = p.op("vector", lambda h, tb=tb:
                              h.tensor_copy(q_fm_tb[tb % 2][:, 1024:2048], tq_ps[1][:]),
                              waits=evw1, inc=s_ev)

            for si, (abuf, qc0, n, kdim) in enumerate(SUBQ):
                mwaits = [(s_ev, qfm_ev[tb])]
                if tb == 0 and si == 0:
                    mwaits.append((s_w, sw['kv_bd_w']))
                key = {0: (tb - 1, 2), 1: (tb - 1, 1), 2: (tb, 0)}[si]
                if key in norm_done:
                    mwaits.append((s_ev, norm_done[key]))

                def amms(h, tb=tb, abuf=abuf, qc0=qc0, n=n, kdim=kdim):
                    ins = None
                    for i in range(n):
                        qc = qc0 + i
                        col = 512 * (i // kdim) + 132 * (i % kdim)
                        ins = h.matmul(att_ps[abuf][:, col:col + 132],
                                       q_fm_tb[tb % 2][:, qc * 128:(qc + 1) * 128],
                                       kv_bd[:, qc * 132:(qc + 1) * 132],
                                       start=True, stop=True)
                    return ins
                amv = p.op("tensor", amms, waits=mwaits, inc=s_mm)
                att_done[(tb, si)] = amv
                g1 = p.op("vector", lambda h, abuf=abuf, n=n, kdim=kdim:
                          h.tensor_copy(dens[:, 0:4 * n],
                                        cap(att_ps[abuf], 32, [[512, 2], [132, kdim], [33, 4]])),
                          waits=[(s_mm, amv)], inc=s_ev)
                g2 = p.op("vector", lambda h, n=n:
                          h.tensor_scalar(dens[:, 0:4 * n], dens[:, 0:4 * n], EPS_ATT, None, ALU.add),
                          waits=[(s_ev, g1)], inc=s_ev)
                g3 = p.op("vector", lambda h, n=n: h.reciprocal(dens[:, 32:32 + 4 * n], dens[:, 0:4 * n]),
                          waits=[(s_ev, g2)], inc=s_ev)
                nwaits = [(s_ev, g3)]
                if si == 0 and (tb - 1) in at_done:
                    nwaits.append((s_mm, at_done[tb - 1]))
                nv = p.op("vector", lambda h, tb=tb, abuf=abuf, qc0=qc0, n=n, kdim=kdim:
                          h.tensor_tensor(
                              cap(attn_tok, qc0 * 128, [[128 * kdim, 2], [128, kdim], [32, 4], [1, 32]]),
                              cap(att_ps[abuf], 0, [[512, 2], [132, kdim], [33, 4], [1, 32]]),
                              cap(dens, 32, [[4 * kdim, 2], [4, kdim], [1, 4], [0, 32]]),
                              ALU.mult),
                          waits=nwaits, inc=s_ev)
                norm_done[(tb, si)] = nv

            atw = [(s_ev, norm_done[(tb, 2)]), (s_ev, qfm_ev[tb])]
            last = None
            for half in range(2):
                def atr(h, tb=tb, half=half):
                    ins = None
                    for k in range(8):
                        cc = half * 8 + k
                        ins = h.transpose(tq_ps[half][:, k * 128:(k + 1) * 128],
                                         attn_tok[:, cc * 128:(cc + 1) * 128], ident_bf[:])
                    return ins
                last = p.op("tensor", atr, waits=atw if half == 0 else [], inc=s_mm)
            at_done[tb] = last
            p.op("vector", lambda h, tb=tb: h.tensor_copy(attn_fm[:, 0:1024], tq_ps[0][:]),
                 waits=[(s_mm, at_done[tb] - 1)], inc=s_ev)
            afm_ev[tb] = p.op("vector", lambda h, tb=tb: h.tensor_copy(attn_fm[:, 1024:2048], tq_ps[1][:]),
                              waits=[(s_mm, at_done[tb])], inc=s_ev)

            for sweep in range(2):
                pw = [(s_ev, afm_ev[tb])]
                if tb == 0 and sweep == 0:
                    pw.append((s_w, wpv))
                key = (tb, sweep - 1) if sweep == 1 else (tb - 1, 1)
                if key in yev_done:
                    pw.append((s_ev, yev_done[key]))

                def pmms(h, tb=tb, sweep=sweep):
                    ins = None
                    for cc in range(NDC):
                        for n2 in range(2):
                            n0 = sweep * 1024 + n2 * 512
                            ins = h.matmul(y_ps[:, n2 * 512:(n2 + 1) * 512],
                                           attn_fm[:, cc * 128:(cc + 1) * 128],
                                           wproj_bf[:, cc * D + n0:cc * D + n0 + 512],
                                           start=(cc == 0), stop=(cc == NDC - 1))
                    return ins
                pv = p.op("tensor", pmms, waits=pw, inc=s_mm)
                ew = [(s_mm, pv)]
                if tb == 0 and sweep == 0:
                    ew.append((s_acc, ffn1_acc_end))
                if (tb - 2) in acc_done:
                    ew.append((s_acc, acc_done[tb - 2]))
                yev_done[(tb, sweep)] = p.op("vector", lambda h, tb=tb, sweep=sweep:
                                             h.tensor_tensor(y_sb[tb % 2][:, sweep * 1024:(sweep + 1) * 1024],
                                                             y_ps[:], bias_bc[:, sweep * 1024:(sweep + 1) * 1024],
                                                             ALU.add),
                                             waits=ew, inc=s_ev)
            aw = [(s_ev, yev_done[(tb, 1)])]
            if tb == 0:
                aw.append((s_x, ln2_loads))
            acc_done[tb] = p.op("gpsimd", lambda h, tb=tb:
                                h.dma_start(out=out[tb * 128:(tb + 1) * 128, :], in_=y_sb[tb % 2][:],
                                            accum_op=ALU.add),
                                waits=aw, inc=s_acc, inc_k=16)
            if dump_attn is not None:
                p.wait("sync", s_ev, norm_done[(tb, 2)])
                p.op("sync", lambda h, tb=tb: h.dma_start(out=dump_attn[:, tb * D:(tb + 1) * D], in_=attn_tok[:]),
                     inc=s_dump, inc_k=16)
        return {tb: [(s_acc, acc_done[tb])] for tb in range(NTB)}

    def dump(name, src):
        ap = dump_aps.get(name)
        if ap is None:
            return
        p.wait("sync", s_ev, s_ev.n)
        p.wait("sync", s_mm, s_mm.n)
        p.wait("sync", s_act, s_act.n)
        p.op("sync", lambda h: h.dma_start(out=ap[:, :], in_=src[:]), inc=s_dump, inc_k=16)

    zrow = xs[0]
    zv = p.op("vector", lambda h: h.memset(zrow[:, 0:528], 0.0), inc=s_ev)
    zero_done = {}
    for g in range(4):
        zero_done[g] = p.op("sync", lambda h, g=g:
                            h.dma_start(out=cc_in[:, 528 * g:528 * (g + 1)], in_=zrow[:, 0:528]),
                            waits=[(s_ev, zv)], inc=s_init, inc_k=16)

    uid = [0]

    def run_ln(src, init_out, with_router, flw, per_tb=None, sparse=None):
        uid[0] += 1
        u = uid[0]
        with (
            nc.psum_tensor(f"pt0_{u}", [128, 1024], BF16, side="right") as _pt0,
            nc.psum_tensor(f"pt1_{u}", [128, 1024], BF16, side="right") as _pt1,
            nc.psum_tensor(f"sc0_{u}", [128, 512], F32, side="right") as _sc0,
            nc.psum_tensor(f"sc1_{u}", [128, 512], F32, side="right") as _sc1,
        ):
            return ln_phase(src, init_out=init_out, with_router=with_router,
                            pt_ps=[_pt0, _pt1], sc_ps=[_sc0, _sc1],
                            first_load_waits=flw, per_tb_load_waits=per_tb,
                            start_tensor_waits=[(s_ev, s_ev.n), (s_act, s_act.n)],
                            sparse=sparse)

    def run_ffn(init_sem_val, ln_res, dump_h=None):
        uid[0] += 1
        u = uid[0]
        with (
            nc.sbuf_tensor(f"wg{u}", [128, NDC * FF], BF16) as wg,
            nc.sbuf_tensor(f"wu{u}", [128, NDC * FF], BF16) as wu,
            nc.sbuf_tensor(f"wd{u}", [128, NFB * D], BF16) as wd,
            nc.sbuf_tensor(f"h_fm{u}", [128, NFB * T], BF16) as h_fm,
            nc.sbuf_tensor(f"ssb0_{u}", [128, 1024], F32) as _s0,
            nc.sbuf_tensor(f"ssb1_{u}", [128, 1024], F32) as _s1,
            nc.psum_tensor(f"g_ps{u}", [128, 1024], F32, side="left") as g_ps,
            nc.psum_tensor(f"u_ps{u}", [128, 1024], F32, side="left") as u_ps,
            nc.psum_tensor(f"y0_ps{u}", [128, 1024], F32, side="right") as _y0,
            nc.psum_tensor(f"y1_ps{u}", [128, 1024], F32, side="right") as _y1,
        ):
            acc_done = ffn_phase(out, wg, wu, wd, h_fm, [_s0, _s1], g_ps, u_ps, [_y0, _y1],
                                 acc_base=s_acc.n, init_sem_val=init_sem_val,
                                 ln_res=ln_res, dump_h=dump_h)
            return {tb: [(s_acc, acc_done[(E - 1, tb)])] for tb in range(NTB)}

    zv2 = p.op("vector", lambda h: h.memset(y_sb[0][:], 0.0), inc=s_ev)
    p.op("gpsimd", lambda h: h.dma_start(
        out=ebuf[0:E * CAP, :].rearrange("(c p) w -> p c w", p=128),
        in_=y_sb[0][:].rearrange("p (one w) -> p one w", one=1).broadcast_to([128, 64, D])),
        waits=[(s_ev, zv2)], inc=s_w, inc_k=16)
    p.op("gpsimd", lambda h: h.dma_start(out=ebuf[E * CAP:E * CAP + 128, :], in_=y_sb[0][:]),
         inc=s_w, inc_k=16)
    zero_ebuf_val = p.op("gpsimd", lambda h: h.dma_start(out=ybuf[E * CAP:E * CAP + 128, :], in_=y_sb[0][:]),
                         inc=s_w, inc_k=16)

    def run_sparse_pass(src_ap, init_out, flw, per_tb, init_sem_fn):
        uid[0] += 1
        u = uid[0]
        with (
            nc.sbuf_tensor(f"stg0_{u}", [128, D], BF16) as _st0,
            nc.sbuf_tensor(f"stg1_{u}", [128, D], BF16) as _st1,
            nc.sbuf_tensor(f"sel1_{u}", [128, NTB * E], F32) as sel1,
            nc.sbuf_tensor(f"selB_{u}", [128, NTB * E], F32) as selB,
            nc.sbuf_tensor(f"c1a_{u}", [128, NTB], F32) as c1a,
            nc.sbuf_tensor(f"c2a_{u}", [128, NTB], F32) as c2a,
            nc.sbuf_tensor(f"dtk_{u}", [128, 2 * E], F32) as desttok,
            nc.sbuf_tensor(f"dti_{u}", [128, 2 * E], mybir.dt.int32) as desti,
        ):
            sp = dict(xne=xne_dram, stage=[_st0, _st1], sel1=sel1, selB=selB,
                      c1a=c1a, c2a=c2a)
            ln_res = run_ln(src_ap, init_out, True, flw, per_tb=per_tb, sparse=sp)
            return ffn_sparse(out, sp, ln_res, init_sem_fn(), zero_ebuf_val,
                              desttok, desti)

    if max_phase >= 2:
        ffn1_tb_acc = run_sparse_pass(x0, True, [(s_init, zero_done[3])], None,
                                      lambda: (s_init, s_init.n))
    if max_phase >= 3:
        ffn1_acc_end = s_acc.n
        with nc.sbuf_tensor("kv_bd", [128, NH * 33], BF16) as kv_bd:
          with nc.sbuf_tensor("xn_fm", [128, NDC * T], BF16) as _xnfm:
            xn_fm = _xnfm
            ln2_res = run_ln(out, False, False, [], per_tb=ffn1_tb_acc)
            ln2_loads = s_x.n
            if max_phase >= 4:
              with (
                nc.sbuf_tensor("sww0", [128, NDC * 512], BF16) as _w0,
                nc.sbuf_tensor("sww1", [128, NDC * 512], BF16) as _w1,
                nc.sbuf_tensor("k_tok", [128, NTB * 512], BF16) as k_tok,
                nc.sbuf_tensor("v133a", [128, 528], BF16) as _v0,
                nc.sbuf_tensor("v133b", [128, 528], BF16) as _v1,
                nc.sbuf_tensor("kv_small", [32, 528], F32) as kv_small,
                nc.sbuf_tensor("qst0", [128, 512], BF16) as _q0,
                nc.sbuf_tensor("qst1", [128, 512], BF16) as _q1,
                nc.psum_tensor("kq0", [128, 512], F32, side="left") as _kq0,
                nc.psum_tensor("kq1", [128, 512], F32, side="left") as _kq1,
                nc.psum_tensor("vp0", [128, 512], F32, side="right") as _vp0,
                nc.psum_tensor("vp1", [128, 512], F32, side="right") as _vp1,
                nc.psum_tensor("kv_ps", [32, 1024], F32, side="right") as kv_ps,
            ):
                for v1t in (_v0, _v1):
                    p.op("vector", lambda h, v1t=v1t: h.memset(
                        cap(v1t, 32, [[33, 16], [1, 1]]), 1.0))
                sw = sweep_phase([_w0, _w1], k_tok, [_v0, _v1], kv_small, [_q0, _q1],
                                 [_kq0, _kq1], [_vp0, _vp1], kv_ps, kv_bd,
                                 ln_res=ln2_res, zero_done=zero_done)
          if max_phase >= 5:
                with (
                    nc.sbuf_tensor("wproj_bf", [128, NDC * D], BF16) as wproj_bf,
                    nc.sbuf_tensor("qtb0", [128, D], BF16) as _qt0,
                    nc.sbuf_tensor("qtb1", [128, D], BF16) as _qt1,
                    nc.sbuf_tensor("qfm0", [128, D], BF16) as _qf0,
                    nc.sbuf_tensor("qfm1", [128, D], BF16) as _qf1,
                    nc.sbuf_tensor("attn_tok", [128, D], BF16) as attn_tok,
                    nc.sbuf_tensor("attn_fm", [128, D], BF16) as attn_fm,
                    nc.sbuf_tensor("bias_bc", [128, D], F32) as bias_bc,
                    nc.sbuf_tensor("dens", [128, 64], F32) as dens,
                    nc.psum_tensor("tq0", [128, 1024], BF16) as _tq0,
                    nc.psum_tensor("tq1", [128, 1024], BF16) as _tq1,
                    nc.psum_tensor("attA", [128, 1024], F32) as _aa,
                    nc.psum_tensor("attB", [128, 1024], F32) as _ab,
                    nc.psum_tensor("ypj", [128, 1024], F32) as ypj,
                ):
                    att_tb_acc = att_phase(wproj_bf, kv_bd, [_qt0, _qt1], [_qf0, _qf1],
                                           attn_tok, attn_fm, bias_bc, dens,
                                           [_tq0, _tq1], [_aa, _ab], ypj, sw,
                                           ln2_loads=ln2_loads, ffn1_acc_end=ffn1_acc_end,
                                           dump_attn=dump_aps.get("dbg_attn"))
    if max_phase >= 7:
        run_sparse_pass(out, False, [], att_tb_acc, lambda: (s_x, s_x.n))

    def fin(h):
        for s in (s_x, s_init, s_acc, s_dump):
            if s.n > 0:
                h.wait_ge(s.h, s.n)
    p.ops["sync"].append(fin)
    p.barrier(ph)

    with nc.Block() as block:
        @block.sync
        def _(h):
            for fn in p.ops["sync"]:
                fn(h)

        @block.gpsimd
        def _(h):
            for fn in p.ops["gpsimd"]:
                fn(h)

        @block.tensor
        def _(h):
            for fn in p.ops["tensor"]:
                fn(h)

        @block.vector
        def _(h):
            for fn in p.ops["vector"]:
                fn(h)

        @block.scalar
        def _(h):
            for fn in p.ops["scalar"]:
                fn(h)

    for c in reversed(sb_ctx):
        c.__exit__(None, None, None)
    for c in reversed(sem_ctxs):
        c.__exit__(None, None, None)
    return nc



def _prep_inputs(L0_attn_x, L0_attn_w_qkv, L0_attn_w_proj, L0_attn_b_proj,
                 L0_ffn_router_w, L0_ffn_gate_w, L0_ffn_up_w, L0_ffn_down_w):
    x = np.ascontiguousarray(np.asarray(L0_attn_x, dtype=np.float32)[0])
    shards = np.split(x, 8, axis=0)
    wqkvT = np.ascontiguousarray(np.asarray(L0_attn_w_qkv, dtype=np.float32).T)
    wprojT = np.ascontiguousarray(np.asarray(L0_attn_w_proj, dtype=np.float32).T)
    bproj = np.ascontiguousarray(np.asarray(L0_attn_b_proj, dtype=np.float32)[None, :])
    router = np.ascontiguousarray(
        np.asarray(L0_ffn_router_w, dtype=np.float32).reshape(NDC, 128, E).transpose(1, 0, 2).reshape(128, NDC * E))
    gate = np.ascontiguousarray(np.asarray(L0_ffn_gate_w, dtype=np.float32))
    up = np.ascontiguousarray(np.asarray(L0_ffn_up_w, dtype=np.float32))
    down = np.ascontiguousarray(np.asarray(L0_ffn_down_w, dtype=np.float32))
    ident_np = np.eye(128, dtype=ml_dtypes.bfloat16)
    ident32_np = np.eye(128, dtype=np.float32)
    ebase_np = (np.arange(16, dtype=np.float32) * 512.0).reshape(16, 1)
    common = dict(wqkvT=wqkvT, wprojT=wprojT, bproj=bproj, router=router,
                  gate_w=gate, up_w=up, down_w=down, ident=ident_np,
                  ident32=ident32_np, ebase=ebase_np)
    return [dict(x0=shards[i], **common) for i in range(8)]


def kernel(**inputs):
    in_maps = _prep_inputs(**inputs)
    nc = build_nc()
    res = run_bass_kernel_spmd(nc, in_maps, core_ids=list(range(8)))
    outv = np.concatenate([np.asarray(r["out"]) for r in res.results], axis=0)
    return outv[None, :, :]


# version 08
# speedup vs baseline: 2.0092x; 1.0005x over previous
import numpy as np
import ml_dtypes
import concourse.bass as bass
import concourse.mybir as mybir
from concourse.bass_utils import run_bass_kernel_spmd

F32 = mybir.dt.float32
BF16 = mybir.dt.bfloat16
AF = mybir.ActivationFunctionType
ALU = mybir.AluOpType
AX = mybir.AxisListType.X

T = 2048
D = 2048
E = 16
FF = 768
NH = 64
DH = 32
NTB = 16
NDC = 16
NFB = 6
EPS_LN = 1e-5
EPS_ATT = 1e-15
NEG_BIG_ABS = 1e30


def _ap(x):
    return x if isinstance(x, bass.AP) else x[:]


def cap(x, doff, dims):
    b = _ap(x)
    return bass.AP(tensor=b.tensor, offset=b.offset + doff,
                   ap=[list(q) for q in b.ap[:1]] + [list(d) for d in dims])


class Sem:
    def __init__(self, handle):
        self.h = handle
        self.n = 0

    def inc(self, k=1):
        self.n += k
        return self.n


class Prog:
    ENGINES = ("sync", "gpsimd", "tensor", "vector", "scalar")

    def __init__(self):
        self.ops = {e: [] for e in self.ENGINES}

    def op(self, eng, fn, waits=(), inc=None, inc_k=1):
        val = inc.inc(inc_k) if inc is not None else None

        def emit(h, fn=fn, waits=tuple(waits), inc=inc, inc_k=inc_k):
            for s, v in waits:
                if v > 0:
                    h.wait_ge(s.h, v)
            ins = fn(h)
            if inc is not None:
                assert ins is not None
                ins.then_inc(inc.h, inc_k)
        self.ops[eng].append(emit)
        return val

    def wait(self, eng, sem, val):
        if val and val > 0:
            self.ops[eng].append(lambda h, s=sem.h, v=val: h.wait_ge(s, v))

    def barrier(self, ph):
        target = ph.n + len(self.ENGINES)
        for eng in self.ENGINES:
            def fn(h, s=ph.h, t=target):
                h.nop().then_inc(s, 1)
                h.wait_ge(s, t)
            self.ops[eng].append(fn)
        ph.n = target


def build_nc(max_phase=99, dumps=(), sstage=3):
    nc = bass.Bass("TRN2", target_bir_lowering=False, debug=False)

    x0 = nc.dram_tensor("x0", [T, D], F32, kind="ExternalInput").ap()
    wqkvT = nc.dram_tensor("wqkvT", [D, 3 * D], F32, kind="ExternalInput").ap()
    wprojT = nc.dram_tensor("wprojT", [D, D], F32, kind="ExternalInput").ap()
    bproj = nc.dram_tensor("bproj", [1, D], F32, kind="ExternalInput").ap()
    router = nc.dram_tensor("router", [128, NDC * E], F32, kind="ExternalInput").ap()
    gate_w = nc.dram_tensor("gate_w", [E, D, FF], F32, kind="ExternalInput").ap()
    up_w = nc.dram_tensor("up_w", [E, D, FF], F32, kind="ExternalInput").ap()
    down_w = nc.dram_tensor("down_w", [E, FF, D], F32, kind="ExternalInput").ap()
    ident = nc.dram_tensor("ident", [128, 128], BF16, kind="ExternalInput").ap()
    ident32 = nc.dram_tensor("ident32", [128, 128], F32, kind="ExternalInput").ap()
    ebase = nc.dram_tensor("ebase", [16, 1], F32, kind="ExternalInput").ap()
    out = nc.dram_tensor("out", [T, D], F32, kind="ExternalOutput").ap()

    cc_in = nc.dram_tensor("cc_in", [128, NH * 33], F32, kind="Internal")
    cc_out = nc.dram_tensor("cc_out", [128, NH * 33], F32, kind="Internal")

    dump_aps = {}
    for name, shape, dt in [
        ("dbg_xn", [128, NDC * T], BF16),
        ("dbg_comb", [128, NTB * E], F32),
        ("dbg_q", [128, NTB * D], BF16),
        ("dbg_kv", [128, NH * 33], F32),
        ("dbg_attn", [128, NTB * D], BF16),
        ("dbg_h", [128, NFB * T], BF16),
        ("dbg_dest", [128, 2 * E], F32),
    ]:
        if name in dumps:
            dump_aps[name] = nc.dram_tensor(name, shape, dt, kind="ExternalOutput").ap()

    p = Prog()

    sem_names = ["ph", "s_x", "s_w", "s_wfree", "s_acc", "s_mm", "s_ev", "s_act",
                 "s_gp", "s_init", "s_dump"]
    sem_ctxs = [nc.semaphore(n) for n in sem_names]
    S = {n: Sem(c.__enter__()) for n, c in zip(sem_names, sem_ctxs)}
    ph, s_x, s_w, s_wfree, s_acc, s_mm, s_ev, s_act, s_gp, s_init, s_dump = (
        S[n] for n in sem_names)

    sb_ctx = []

    def sbuf(name, shape, dt):
        c = nc.sbuf_tensor(name, shape, dt)
        t = c.__enter__()
        sb_ctx.append(c)
        return t

    xn_fm = None
    xn_tok = sbuf("xn_tok", [128, D], BF16)
    comb = sbuf("comb", [128, NTB * E], F32)
    xs = [sbuf(f"xs{i}", [128, D], F32) for i in range(2)]
    y_sb = [sbuf(f"ysb{i}", [128, D], F32) for i in range(2)]
    ident_bf = sbuf("ident_bf", [128, 128], BF16)
    router_bf = sbuf("router_bf", [128, NDC * E], BF16)
    small = sbuf("small", [128, 32], F32)
    stt = sbuf("stt", [128, 48], F32)
    ident_f32s = sbuf("ident_f32s", [128, 128], F32)
    ebase_sb = sbuf("ebase_sb", [16, 1], F32)
    sm16 = sbuf("sm16", [128, 3 * E], F32)

    p.op("gpsimd", lambda h: h.dma_start(out=ident_bf[:], in_=ident[:, :]), inc=s_w, inc_k=16)
    p.op("gpsimd", lambda h: h.dma_start(out=ident_f32s[:], in_=ident32[:, :]), inc=s_w, inc_k=16)
    p.op("gpsimd", lambda h: h.dma_start(out=ebase_sb[:], in_=ebase[:, :]), inc=s_w, inc_k=16)
    p.op("vector", lambda h: h.memset(small[:, 21:22], EPS_LN))
    p.op("vector", lambda h: h.memset(small[:, 22:23], 1.0))
    p.op("gpsimd", lambda h: h.dma_start(out=router_bf[:], in_=router[:, :]), inc=s_w, inc_k=16)
    p.wait("tensor", s_w, 32)
    p.barrier(ph)

    def ln_phase(src_ap, init_out, with_router, pt_ps, sc_ps,
                 first_load_waits=(), start_tensor_waits=(), per_tb_load_waits=None,
                 sparse=None):
        res = {'evac': {}, 'comb': {}, 'xne': {}}
        init_done = {}
        norm_done = {}
        trans_done = {}
        exp_done = {}
        exp_mm_done = {}
        for tb in range(NTB):
            buf = xs[tb % 2]
            r0, r1 = tb * 128, (tb + 1) * 128

            waits = list(first_load_waits) if tb == 0 else []
            if per_tb_load_waits is not None:
                waits = waits + list(per_tb_load_waits[tb])
            if tb >= 2:
                waits = [(s_ev, norm_done[tb - 2])]
                if init_out:
                    waits.append((s_init, init_done[tb - 2]))
            xw = p.op("sync", lambda h, buf=buf, r0=r0, r1=r1:
                      h.dma_start(out=buf[:], in_=src_ap[r0:r1, :]),
                      waits=waits, inc=s_x, inc_k=16)

            if init_out:
                init_done[tb] = p.op("sync", lambda h, buf=buf, r0=r0, r1=r1:
                                     h.dma_start(out=out[r0:r1, :], in_=buf[:]),
                                     waits=[(s_x, xw)], inc=s_init, inc_k=16)

            pa = tb % 2
            sc = 4 * pa
            v1 = None
            for c in range(4):
                v1 = p.op("vector", lambda h, buf=buf, c=c, pa=pa:
                          h.bn_stats(stt[:, 24 * pa + 6 * c:24 * pa + 6 * (c + 1)],
                                     buf[:, 512 * c:512 * (c + 1)]),
                          waits=[(s_x, xw)] if c == 0 else [], inc=s_ev)
            v2 = p.op("vector", lambda h, pa=pa, sc=sc:
                      h.bn_aggr(small[:, sc:sc + 2], stt[:, 24 * pa:24 * (pa + 1)]),
                      waits=[(s_ev, v1)], inc=s_ev)
            a2 = p.op("scalar", lambda h, sc=sc:
                      h.activation(small[:, sc + 2:sc + 3], small[:, sc + 1:sc + 2], AF.Ln, bias=small[:, 21:22]),
                      waits=[(s_ev, v2)], inc=s_act)
            a3 = p.op("scalar", lambda h, sc=sc:
                      h.activation(small[:, sc + 3:sc + 4], small[:, sc + 2:sc + 3], AF.Exp, scale=-0.5),
                      waits=[(s_act, a2)], inc=s_act)
            nwaits = [(s_act, a3)]
            if tb >= 1:
                nwaits.append((s_mm, trans_done[tb - 1]))
            if sparse is not None and tb >= 1:
                nwaits.append((s_x, res['xne'][tb - 1]))
            nv = p.op("vector", lambda h, buf=buf, sc=sc:
                      h.tensor_scalar(xn_tok[:], buf[:], small[:, sc:sc + 1], small[:, sc + 3:sc + 4],
                                      ALU.subtract, ALU.mult),
                      waits=nwaits, inc=s_ev)
            norm_done[tb] = nv

            if sparse is not None:
                res['xne'][tb] = p.op("sync", lambda h, tb=tb:
                                      h.dma_start(out=sparse['xne'][tb * 128:(tb + 1) * 128, :], in_=xn_tok[:]),
                                      waits=[(s_ev, nv)], inc=s_x, inc_k=16)

            tw = [(s_ev, nv)] + (list(start_tensor_waits) if tb == 0 else [])
            ev_prev = []
            for half in range(2):
                def tmm(h, half=half):
                    ins = None
                    for k in range(8):
                        dc = half * 8 + k
                        ins = h.transpose(pt_ps[half][:, k * 128:(k + 1) * 128],
                                          xn_tok[:, dc * 128:(dc + 1) * 128], ident_bf[:])
                    return ins
                mmv = p.op("tensor", tmm, waits=tw, inc=s_mm)
                tw = []
                if half == 1:
                    trans_done[tb] = mmv
                if sparse is None:
                    evv = p.op("vector", lambda h, half=half, tb=tb:
                               h.tensor_copy(
                                   xn_fm.rearrange("p (c t) -> p c t", c=NDC)[:, half * 8:(half + 1) * 8, tb * 128:(tb + 1) * 128],
                                   pt_ps[half].rearrange("p (c t) -> p c t", c=8)),
                               waits=[(s_mm, mmv)], inc=s_ev)
                else:
                    ewaits = [(s_mm, mmv)]
                    if tb >= 2 and half == 0:
                        ewaits.append((s_mm, exp_mm_done[tb - 2]))
                    evv = p.op("vector", lambda h, half=half, tb=tb:
                               h.tensor_copy(sparse['stage'][tb % 2][:, half * 1024:(half + 1) * 1024],
                                             pt_ps[half][:]),
                               waits=ewaits, inc=s_ev)
                ev_prev.append(evv)
            evac_done = ev_prev[-1]
            res['evac'][tb] = evac_done
            p.wait("tensor", s_ev, evac_done)

            if with_router:
                rwaits = []
                if tb >= 2:
                    rwaits.append((s_act, exp_done[tb - 2]))

                def tr(h, tb=tb):
                    ins = None
                    for dc in range(NDC):
                        lhsT = (xn_fm[:, dc * T + tb * 128: dc * T + (tb + 1) * 128]
                                if sparse is None else
                                sparse['stage'][tb % 2][:, dc * 128:(dc + 1) * 128])
                        ins = h.matmul(sc_ps[tb % 2][:, 0:E], lhsT,
                                       router_bf[:, dc * E:(dc + 1) * E],
                                       start=(dc == 0), stop=(dc == NDC - 1))
                    return ins
                rw = p.op("tensor", tr, waits=rwaits, inc=s_mm)
                exp_mm_done[tb] = rw

                c1 = p.op("vector", lambda h, tb=tb:
                          h.reduce_max(small[:, 16:17], sc_ps[tb % 2][:, 0:E], axis=AX, negate=True),
                          waits=[(s_mm, rw)], inc=s_ev)
                a3 = p.op("scalar", lambda h, tb=tb:
                          h.activation(sm16[:, 0:E], sc_ps[tb % 2][:, 0:E], AF.Exp, bias=small[:, 16:17]),
                          waits=[(s_ev, c1)], inc=s_act)
                exp_done[tb] = a3
                c2 = p.op("vector", lambda h: h.reduce_max(small[:, 17:18], sm16[:, 0:E], axis=AX),
                          waits=[(s_act, a3)], inc=s_ev)
                sel1ap = sm16[:, E:2 * E] if sparse is None else sparse['sel1'][:, tb * E:(tb + 1) * E]
                selBap = sm16[:, E:2 * E] if sparse is None else sparse['selB'][:, tb * E:(tb + 1) * E]
                c3 = p.op("vector", lambda h, sel1ap=sel1ap: h.tensor_scalar(sel1ap, sm16[:, 0:E], small[:, 17:18], None, ALU.is_ge),
                          waits=[(s_ev, c2)], inc=s_ev)
                c4 = p.op("vector", lambda h, sel1ap=sel1ap: h.scalar_tensor_tensor(
                              sm16[:, 2 * E:3 * E], sel1ap, -NEG_BIG_ABS,
                              sm16[:, 0:E], ALU.mult, ALU.add),
                          waits=[(s_ev, c3)], inc=s_ev)
                c5 = p.op("vector", lambda h: h.reduce_max(small[:, 18:19], sm16[:, 2 * E:3 * E], axis=AX),
                          waits=[(s_ev, c4)], inc=s_ev)
                c6 = p.op("vector", lambda h: h.tensor_tensor(small[:, 19:20], small[:, 17:18], small[:, 18:19], ALU.add),
                          waits=[(s_ev, c5)], inc=s_ev)
                c6b = p.op("vector", lambda h, selBap=selBap: h.tensor_scalar(selBap, sm16[:, 0:E], small[:, 18:19], None, ALU.is_ge),
                           waits=[(s_ev, c5)], inc=s_ev)
                c7 = p.op("vector", lambda h: h.reciprocal(small[:, 20:21], small[:, 19:20]),
                          waits=[(s_ev, c6)], inc=s_ev)
                c8 = p.op("vector", lambda h: h.tensor_scalar(small[:, 20:21], small[:, 20:21], 0.5, None, ALU.mult),
                          waits=[(s_ev, c7)], inc=s_ev)
                if sparse is None:
                    p.op("vector", lambda h, tb=tb:
                         h.scalar_tensor_tensor(comb[:, tb * E:(tb + 1) * E], sm16[:, 0:E],
                                                small[:, 20:21], selBap, ALU.mult, ALU.mult),
                         waits=[(s_ev, max(c8, c6b))], inc=s_ev)
                else:
                    p.op("vector", lambda h, tb=tb:
                         h.tensor_tensor(sparse['c1a'][:, tb:tb + 1], small[:, 17:18], small[:, 20:21], ALU.mult),
                         waits=[(s_ev, c8)], inc=s_ev)
                    p.op("vector", lambda h, tb=tb:
                         h.tensor_tensor(sparse['c2a'][:, tb:tb + 1], small[:, 18:19], small[:, 20:21], ALU.mult),
                         waits=[(s_ev, c8)], inc=s_ev)
        res['all_ev'] = s_ev.n
        res['all_act'] = s_act.n
        res['rmm'] = exp_mm_done
        return res

    def ffn_phase(out_ap, wg, wu, wd, h_fm, s_sb, g_ps, u_ps, y_ps,
                  acc_base, init_sem_val, ln_res, dump_h=None):
        silu_done = {}; tt_done = {}; gu_used = {}; d_used = {}
        ts_done = {}
        acc_done = {}
        NU = NFB * 2

        def load_gu(e, waits):
            p.op("gpsimd", lambda h, e=e:
                 h.dma_start(out=wg.rearrange("p (c f) -> p c f", c=NDC),
                             in_=gate_w[e, :, :].rearrange("(c p) f -> p c f", p=128)),
                 waits=waits, inc=s_w, inc_k=16)
            uv_ = p.op("gpsimd", lambda h, e=e:
                       h.dma_start(out=wu.rearrange("p (c f) -> p c f", c=NDC),
                                   in_=up_w[e, :, :].rearrange("(c p) f -> p c f", p=128)),
                       inc=s_w, inc_k=16)
            return uv_

        def gu_chunk(e, j, waits):
            src = gate_w if j < 4 else up_w
            dst = wg if j < 4 else wu
            c0 = (j % 4) * 4
            return p.op("gpsimd", lambda h, e=e, c0=c0, src=src, dst=dst:
                        h.dma_start(
                            out=dst.rearrange("p (c f) -> p c f", c=NDC)[:, c0:c0 + 4, :],
                            in_=src[e, :, :].rearrange("(c p) f -> p c f", p=128)[:, c0:c0 + 4, :]),
                        waits=waits, inc=s_w, inc_k=16)

        w_gu = {0: load_gu(0, [])}
        w_dn = {}
        for e in range(E):
            w_dn[e] = p.op("gpsimd", lambda h, e=e:
                           h.dma_start(out=wd.rearrange("p (c d) -> p c d", c=NFB),
                                       in_=down_w[e, :, :].rearrange("(c p) d -> p c d", p=128)),
                           waits=[] if e == 0 else [(s_mm, d_used[e - 1])], inc=s_w, inc_k=16)
            w_up = w_gu[e]
            w_down = w_dn[e]

            units = [(fb, th) for th in range(2) for fb in range(NFB)]
            for k, (fb, th) in enumerate(units):
                ku = e * NU + k
                waits = []
                if e == 0 and k == 0:
                    waits.append((s_ev, ln_res['evac'][7]))
                if e == 0 and k == NFB:
                    waits.append((s_ev, ln_res['evac'][NTB - 1]))
                if k == 0:
                    waits.append((s_w, w_up))
                if ku - 1 >= 0 and (ku - 1) in silu_done:
                    waits.append((s_act, silu_done[ku - 1]))

                def gmm(h, e=e, fb=fb, th=th):
                    ins = None
                    for dc in range(NDC):
                        for half in range(2):
                            t0 = th * 1024 + half * 512
                            ins = h.matmul(g_ps[:, half * 512:(half + 1) * 512],
                                           wg[:, dc * FF + fb * 128:dc * FF + (fb + 1) * 128],
                                           xn_fm[:, dc * T + t0:dc * T + t0 + 512],
                                           start=(dc == 0), stop=(dc == NDC - 1))
                    return ins
                gv = p.op("tensor", gmm, waits=waits, inc=s_mm)
                uwaits = []
                if (ku - 1) in tt_done:
                    uwaits.append((s_ev, tt_done[ku - 1]))

                def umm(h, e=e, fb=fb, th=th):
                    ins = None
                    for dc in range(NDC):
                        for half in range(2):
                            t0 = th * 1024 + half * 512
                            ins = h.matmul(u_ps[:, half * 512:(half + 1) * 512],
                                           wu[:, dc * FF + fb * 128:dc * FF + (fb + 1) * 128],
                                           xn_fm[:, dc * T + t0:dc * T + t0 + 512],
                                           start=(dc == 0), stop=(dc == NDC - 1))
                    return ins
                uv = p.op("tensor", umm, waits=uwaits, inc=s_mm)
                if k == len(units) - 1:
                    gu_used[e] = uv
                sv = p.op("scalar", lambda h, ku=ku: h.activation(s_sb[ku % 2][:], g_ps[:], AF.Silu),
                          waits=[(s_mm, gv)] + ([(s_ev, tt_done[ku - 2])] if (ku - 2) in tt_done else []),
                          inc=s_act)
                silu_done[ku] = sv
                tv = p.op("vector", lambda h, fb=fb, th=th, ku=ku:
                          h.tensor_tensor(h_fm[:, fb * T + th * 1024:fb * T + (th + 1) * 1024],
                                          s_sb[ku % 2][:], u_ps[:], ALU.mult),
                          waits=[(s_mm, uv), (s_act, sv)], inc=s_ev)
                tt_done[ku] = tv

            if dump_h is not None and e == 0:
                p.wait("sync", s_ev, tt_done[NU - 1])
                p.op("sync", lambda h: h.dma_start(out=dump_h[:, :], in_=h_fm[:]), inc=s_dump, inc_k=16)

            for tb in range(NTB):
                for half in range(2):
                    buf = (tb * 2 + half) % 2
                    waits = []
                    if (tb, half) == (0, 0):
                        waits += [(s_w, w_down), (s_ev, tt_done[e * NU + NU - 1])]
                    if e == 0 and (tb, half) == (0, 0):
                        waits += [(s_ev, ln_res['all_ev']), (s_act, ln_res['all_act'])]
                    key = (e, tb - 1, half) if tb > 0 else (e - 1, NTB - 1, half)
                    if key in ts_done:
                        waits.append((s_ev, ts_done[key]))

                    def dmm(h, e=e, tb=tb, half=half, buf=buf):
                        ins = None
                        for fc in range(NFB):
                            for n2 in range(2):
                                n0 = half * 1024 + n2 * 512
                                ins = h.matmul(y_ps[buf][:, n2 * 512:(n2 + 1) * 512],
                                               h_fm[:, fc * T + tb * 128:fc * T + (tb + 1) * 128],
                                               wd[:, fc * D + n0:fc * D + n0 + 512],
                                               start=(fc == 0), stop=(fc == NFB - 1))
                        return ins
                    yv = p.op("tensor", dmm, waits=waits, inc=s_mm)
                    if (tb, half) == (NTB - 1, 1):
                        d_used[e] = yv
                    ewaits = [(s_mm, yv)]
                    if (e, tb - 2) in acc_done:
                        ewaits.append((s_acc, acc_done[(e, tb - 2)]))
                    elif tb < 2 and (e - 1, NTB - 2 + tb) in acc_done:
                        ewaits.append((s_acc, acc_done[(e - 1, NTB - 2 + tb)]))
                    tsv = p.op("vector", lambda h, e=e, tb=tb, half=half, buf=buf:
                               h.tensor_scalar(y_sb[tb % 2][:, half * 1024:(half + 1) * 1024],
                                               y_ps[buf][:], comb[:, tb * E + e:tb * E + e + 1],
                                               None, ALU.mult),
                               waits=ewaits, inc=s_ev)
                    ts_done[(e, tb, half)] = tsv
                if e + 1 < E and 2 <= tb <= 9:
                    j = tb - 2
                    w_gu[e + 1] = gu_chunk(e + 1, j,
                                           [(s_mm, gu_used[e])] if j == 0 else [])
                awaits = [(s_ev, ts_done[(e, tb, 1)])]
                if e == 0 and tb == 0:
                    awaits.append(init_sem_val)
                if e > 0 and tb == 0:
                    awaits.append((s_acc, acc_base + 256 * e))
                av = p.op("gpsimd", lambda h, tb=tb:
                          h.dma_start(out=out_ap[tb * 128:(tb + 1) * 128, :], in_=y_sb[tb % 2][:],
                                      accum_op=ALU.add),
                          waits=awaits, inc=s_acc, inc_k=16)
                acc_done[(e, tb)] = av
        return acc_done

    CAP = 512
    DBIG = float(E * CAP)
    xne_dram = nc.dram_tensor("xne", [T, D], BF16, kind="Internal").ap()
    ebuf = nc.dram_tensor("ebuf", [E * CAP + 128, D], BF16, kind="Internal")
    ybuf = nc.dram_tensor("ybuf", [E * CAP + 128, D], F32, kind="Internal")

    sp_uid = [0]

    def ffn_sparse(out_ap, sp, ln_res, init_sem_val, zero_ebuf_val, desttok, desti):
        sp_uid[0] += 1
        su = sp_uid[0]
        """Top-2 MoE: scatter selected xn rows into per-expert capacity buffers,
        run each expert on its CAP tokens, write unscaled y rows to ybuf, then
        per token-block gather the two expert rows and combine with c1/c2.
        Mathematically identical to the dense reference (zero-weight experts
        contribute exact zeros there)."""
        with (
            nc.sbuf_tensor(f"sel1f{su}", [16, T], F32) as sel1f,
            nc.sbuf_tensor(f"indf{su}", [16, T], F32) as indf,
            nc.sbuf_tensor(f"slotf{su}", [16, T], F32) as slotf,
            nc.sbuf_tensor(f"okf{su}", [16, T], F32) as okf,
            nc.sbuf_tensor(f"t1f{su}", [16, T], F32) as t1f,
            nc.sbuf_tensor(f"t2f{su}", [16, T], F32) as t2f,
            nc.psum_tensor(f"selps{su}", [128, T], F32) as selps,
            nc.psum_tensor(f"destps{su}", [1, T], F32) as destps,
        ):
            ev_tr = {}
            for ri, (src, dst) in enumerate([(sp['sel1'], sel1f), (sp['selB'], indf)]):
                tw = [(s_ev, ln_res['all_ev']), (s_act, ln_res['all_act'])] if ri == 0 else [(s_ev, ev_tr[0])]
                last = None
                for tb in range(NTB):
                    def trm(h, src=src, tb=tb):
                        return h.transpose(selps[0:16, tb * 128:(tb + 1) * 128],
                                           src[:, tb * E:(tb + 1) * E], ident_f32s[:])
                    last = p.op("tensor", trm, waits=tw if tb == 0 else [], inc=s_mm)
                ev_tr[ri] = p.op("vector", lambda h, dst=dst: h.tensor_copy(dst[0:16, :], selps[0:16, :]),
                                 waits=[(s_mm, last)], inc=s_ev)
            w1 = p.op("vector", lambda h: h.tensor_tensor_scan(slotf[0:16, :], indf[0:16, :], indf[0:16, :],
                                                               0.0, ALU.add, ALU.bypass),
                      waits=[(s_ev, ev_tr[1])], inc=s_ev)
            w2 = p.op("vector", lambda h: h.tensor_tensor(slotf[0:16, :], slotf[0:16, :], indf[0:16, :], ALU.subtract),
                      waits=[(s_ev, w1)], inc=s_ev)
            w3 = p.op("vector", lambda h: h.tensor_scalar(okf[0:16, :], slotf[0:16, :], float(CAP), None, ALU.is_lt),
                      waits=[(s_ev, w2)], inc=s_ev)
            w4 = p.op("vector", lambda h: h.tensor_scalar(slotf[0:16, :], slotf[0:16, :], ebase_sb[0:16, 0:1], None, ALU.add),
                      waits=[(s_ev, w3)], inc=s_ev)
            dtok_ev = {}
            prev = w4
            for rank in range(2):
                if rank == 1:
                    prev = p.op("vector", lambda h: h.tensor_tensor(indf[0:16, :], indf[0:16, :], sel1f[0:16, :], ALU.subtract),
                                waits=[(s_ev, prev)], inc=s_ev)
                selfm = sel1f if rank == 0 else indf
                q1 = p.op("vector", lambda h, selfm=selfm: h.tensor_tensor(t1f[0:16, :], selfm[0:16, :], okf[0:16, :], ALU.mult),
                          waits=[(s_ev, prev)], inc=s_ev)
                q2 = p.op("vector", lambda h: h.tensor_tensor(t2f[0:16, :], t1f[0:16, :], slotf[0:16, :], ALU.mult),
                          waits=[(s_ev, q1)], inc=s_ev)

                def mm_reduce(h, srcf):
                    ins = None
                    for j in range(4):
                        ins = h.matmul(destps[0:1, 512 * j:512 * (j + 1)],
                                       small[0:16, 22:23], srcf[0:16, 512 * j:512 * (j + 1)],
                                       start=True, stop=True)
                    return ins
                m1v = p.op("tensor", lambda h: mm_reduce(h, t2f), waits=[(s_ev, q2)], inc=s_mm)
                e0 = p.op("vector", lambda h: h.tensor_copy(t2f[0:1, 0:T], destps[0:1, :]),
                          waits=[(s_mm, m1v)], inc=s_ev)
                m2v = p.op("tensor", lambda h: mm_reduce(h, t1f), waits=[(s_ev, e0)], inc=s_mm)
                e1 = p.op("vector", lambda h: h.tensor_scalar(destps[0:1, :], destps[0:1, :],
                                                              -DBIG, DBIG, ALU.mult, ALU.add),
                          waits=[(s_mm, m2v)], inc=s_ev)
                e2 = p.op("vector", lambda h: h.tensor_tensor(t1f[0:1, 0:T], t2f[0:1, 0:T], destps[0:1, :], ALU.add),
                          waits=[(s_ev, e1)], inc=s_ev)
                last = None
                for tb in range(NTB):
                    def trd(h, tb=tb):
                        return h.transpose(selps[0:128, tb:tb + 1],
                                           t1f[0:1, tb * 128:(tb + 1) * 128], ident_f32s[0:1, 0:1])
                    last = p.op("tensor", trd, waits=[(s_ev, e2)] if tb == 0 else [], inc=s_mm)
                dtok_ev[rank] = p.op("vector", lambda h, rank=rank:
                                     h.tensor_copy(desttok[:, rank * E:(rank + 1) * E], selps[0:128, 0:16]),
                                     waits=[(s_mm, last)], inc=s_ev)
                prev = dtok_ev[rank]
            di = p.op("vector", lambda h: h.tensor_copy(desti[:], desttok[:]),
                      waits=[(s_ev, dtok_ev[1])], inc=s_ev)
            if "dbg_dest" in dump_aps:
                p.wait("sync", s_ev, di)
                p.op("sync", lambda h: h.dma_start(out=dump_aps["dbg_dest"][:, :], in_=desttok[:]),
                     inc=s_dump, inc_k=16)
        if sstage < 1:
            return {tb: [] for tb in range(NTB)}

        scat_done = {}
        stg_free = {}
        for tb in range(NTB):
            lw = [(s_x, ln_res['xne'][tb]), (s_mm, ln_res['rmm'][14 + (tb % 2)])]
            if tb >= 2:
                lw.append((s_gp, scat_done[tb - 2]))
            sl = p.op("sync", lambda h, tb=tb:
                      h.dma_start(out=sp['stage'][tb % 2][:], in_=xne_dram[tb * 128:(tb + 1) * 128, :]),
                      waits=lw, inc=s_x, inc_k=16)
            for rank in range(2):
                sw_ = [(s_x, sl), (s_ev, di)] if rank == 0 else []
                if tb == 0 and rank == 0:
                    sw_.append((s_w, zero_ebuf_val))
                scat_done[tb] = p.op("gpsimd", lambda h, tb=tb, rank=rank:
                                     h.indirect_dma_start(
                                         out=ebuf[:, :], out_offset=bass.IndirectOffsetOnAxis(desti[:, rank * E + tb:rank * E + tb + 1], 0),
                                         in_=sp['stage'][tb % 2][:], in_offset=None),
                                     waits=sw_, inc=s_gp, inc_k=16)
        scat_end = scat_done[NTB - 1]
        if sstage < 2:
            return {tb: [] for tb in range(NTB)}

        with (
            nc.sbuf_tensor(f"wg_s{su}", [128, NDC * FF], BF16) as wg,
            nc.sbuf_tensor(f"wu_s{su}", [128, NDC * FF], BF16) as wu,
            nc.sbuf_tensor(f"wd_s{su}", [128, NFB * D], BF16) as wd,
            nc.sbuf_tensor(f"xng{su}", [128, NDC * CAP], BF16) as xng,
            nc.sbuf_tensor(f"hfm_s{su}", [128, NFB * CAP], BF16) as h_fm,
            nc.sbuf_tensor(f"ebsb{su}", [128, 4 * D], BF16) as ebsb,
            nc.sbuf_tensor(f"ssb0_s{su}", [128, 512], F32) as _ss0,
            nc.sbuf_tensor(f"ssb1_s{su}", [128, 512], F32) as _ss1,
            nc.sbuf_tensor(f"bgat{su}", [128, D], F32) as bgat,
            nc.psum_tensor(f"g_s{su}", [128, 512], F32) as g_ps,
            nc.psum_tensor(f"u_s{su}", [128, 512], F32) as u_ps,
            nc.psum_tensor(f"tr0_s{su}", [128, 1024], BF16) as _tr0,
            nc.psum_tensor(f"tr1_s{su}", [128, 1024], BF16) as _tr1,
            nc.psum_tensor(f"y0_s{su}", [128, 512], F32) as _yy0,
            nc.psum_tensor(f"y1_s{su}", [128, 512], F32) as _yy1,
        ):
            s_sb = [_ss0, _ss1]
            tr_ps = [_tr0, _tr1]
            y_ps = [_yy0, _yy1]
            silu_done = {}; tt_done = {}; gu_used = {}; d_used = {}
            load_done = {}; trans_done = {}; yev = {}; st_done = {}
            prev_evg = {}
            for e in range(E):
                w_gate = p.op("gpsimd", lambda h, e=e:
                              h.dma_start(out=wg.rearrange("p (c f) -> p c f", c=NDC),
                                          in_=gate_w[e, :, :].rearrange("(c p) f -> p c f", p=128)),
                              waits=[] if e == 0 else [(s_mm, gu_used[e - 1])], inc=s_w, inc_k=16)
                w_up = p.op("gpsimd", lambda h, e=e:
                            h.dma_start(out=wu.rearrange("p (c f) -> p c f", c=NDC),
                                        in_=up_w[e, :, :].rearrange("(c p) f -> p c f", p=128)),
                            inc=s_w, inc_k=16)
                w_down = p.op("gpsimd", lambda h, e=e:
                              h.dma_start(out=wd.rearrange("p (c d) -> p c d", c=NFB),
                                          in_=down_w[e, :, :].rearrange("(c p) d -> p c d", p=128)),
                              waits=[] if e == 0 else [(s_mm, d_used[e - 1])], inc=s_w, inc_k=16)
                lw = [(s_gp, scat_end)] if e == 0 else [(s_mm, trans_done[e - 1])]
                load_done[e] = p.op("sync", lambda h, e=e:
                                    h.dma_start(out=ebsb.rearrange("p (c w) -> p c w", c=4),
                                                in_=ebuf[:, :].rearrange("(c p) w -> p c w", p=128)[:, 4 * e:4 * (e + 1), :]),
                                    waits=lw, inc=s_x, inc_k=16)
                ev_last = None
                tlast = None
                evg_done = {}
                for grp in range(8):
                    tw = [(s_x, load_done[e])] if grp == 0 else []
                    if grp >= 2:
                        tw.append((s_ev, evg_done[grp - 2]))
                    elif e > 0:
                        tw.append((s_ev, prev_evg[6 + grp]))

                    def trg(h, e=e, grp=grp):
                        ins = None
                        for k in range(8):
                            idx = grp * 8 + k
                            c, dc = idx // NDC, idx % NDC
                            ins = h.transpose(tr_ps[grp % 2][:, k * 128:(k + 1) * 128],
                                              ebsb[:, c * D + dc * 128:c * D + (dc + 1) * 128], ident_bf[:])
                        return ins
                    tlast = p.op("tensor", trg, waits=tw, inc=s_mm)

                    def evg(h, e=e, grp=grp):
                        c, dc0 = (grp * 8) // NDC, (grp * 8) % NDC
                        return h.tensor_copy(
                            cap(xng, dc0 * CAP + c * 128, [[CAP, 8], [1, 128]]),
                            tr_ps[grp % 2][:])
                    ev_last = p.op("vector", evg, waits=[(s_mm, tlast)], inc=s_ev)
                    evg_done[grp] = ev_last
                trans_done[e] = tlast
                prev_evg = evg_done
                xng_ready = ev_last

                for fb in range(NFB):
                    gw_ = [(s_w, w_up), (s_ev, xng_ready)] if fb == 0 else []
                    kup = (e, fb - 1) if fb > 0 else (e - 1, NFB - 1)
                    if kup in silu_done:
                        gw_.append((s_act, silu_done[kup]))

                    def gmm(h, e=e, fb=fb):
                        ins = None
                        for dc in range(NDC):
                            ins = h.matmul(g_ps[:], wg[:, dc * FF + fb * 128:dc * FF + (fb + 1) * 128],
                                           xng[:, dc * CAP:(dc + 1) * CAP],
                                           start=(dc == 0), stop=(dc == NDC - 1))
                        return ins
                    gv = p.op("tensor", gmm, waits=gw_, inc=s_mm)
                    uw_ = []
                    if kup in tt_done:
                        uw_.append((s_ev, tt_done[kup]))

                    def umm(h, e=e, fb=fb):
                        ins = None
                        for dc in range(NDC):
                            ins = h.matmul(u_ps[:], wu[:, dc * FF + fb * 128:dc * FF + (fb + 1) * 128],
                                           xng[:, dc * CAP:(dc + 1) * CAP],
                                           start=(dc == 0), stop=(dc == NDC - 1))
                        return ins
                    uv = p.op("tensor", umm, waits=uw_, inc=s_mm)
                    if fb == NFB - 1:
                        gu_used[e] = uv
                    sv = p.op("scalar", lambda h, fb=fb: h.activation(s_sb[fb % 2][:], g_ps[:], AF.Silu),
                              waits=[(s_mm, gv)] + ([(s_ev, tt_done[(e, fb - 2)])] if (e, fb - 2) in tt_done else []),
                              inc=s_act)
                    silu_done[(e, fb)] = sv
                    tv = p.op("vector", lambda h, fb=fb:
                              h.tensor_tensor(h_fm[:, fb * CAP:(fb + 1) * CAP], s_sb[fb % 2][:], u_ps[:], ALU.mult),
                              waits=[(s_mm, uv), (s_act, sv)], inc=s_ev)
                    tt_done[(e, fb)] = tv

                for gtb in range(4):
                    for n4 in range(4):
                        buf = (gtb * 4 + n4) % 2
                        dw_ = []
                        if gtb == 0 and n4 == 0:
                            dw_ += [(s_w, w_down), (s_ev, tt_done[(e, NFB - 1)])]
                        kprev = (e, gtb, n4 - 2) if n4 >= 2 else (e, gtb - 1, n4 + 2)
                        if gtb == 0 and n4 < 2:
                            kprev = (e - 1, 3, n4 + 2)
                        if kprev in yev:
                            dw_.append((s_ev, yev[kprev]))

                        def dmm(h, e=e, gtb=gtb, n4=n4, buf=buf):
                            ins = None
                            for fc in range(NFB):
                                ins = h.matmul(y_ps[buf][:],
                                               h_fm[:, fc * CAP + gtb * 128:fc * CAP + (gtb + 1) * 128],
                                               wd[:, fc * D + n4 * 512:fc * D + (n4 + 1) * 512],
                                               start=(fc == 0), stop=(fc == NFB - 1))
                            return ins
                        dv = p.op("tensor", dmm, waits=dw_, inc=s_mm)
                        if (gtb, n4) == (3, 3):
                            d_used[e] = dv
                        ew_ = [(s_mm, dv)]
                        if e == 0 and gtb == 0 and n4 == 0:
                            ew_.append((s_w, zero_ebuf_val))
                        if (e, gtb - 2) in st_done:
                            ew_.append((s_x, st_done[(e, gtb - 2)]))
                        elif gtb < 2 and (e - 1, 2 + gtb) in st_done:
                            ew_.append((s_x, st_done[(e - 1, 2 + gtb)]))
                        yev[(e, gtb, n4)] = p.op("vector", lambda h, gtb=gtb, n4=n4, buf=buf:
                                                 h.tensor_copy(y_sb[gtb % 2][:, n4 * 512:(n4 + 1) * 512], y_ps[buf][:]),
                                                 waits=ew_, inc=s_ev)
                    st_done[(e, gtb)] = p.op("sync", lambda h, e=e, gtb=gtb:
                                             h.dma_start(out=ybuf[e * CAP + gtb * 128:e * CAP + (gtb + 1) * 128, :],
                                                         in_=y_sb[gtb % 2][:]),
                                             waits=[(s_ev, yev[(e, gtb, 3)])], inc=s_x, inc_k=16)
            stores_end = st_done[(E - 1, 3)]
            if sstage < 3:
                return {tb: [] for tb in range(NTB)}

            acc_done = {}
            comb_done = {}

            def emit_acc(tb):
                aw = [(s_ev, comb_done[tb])]
                if tb == 0:
                    aw.append(init_sem_val)
                acc_done[tb] = p.op("gpsimd", lambda h, tb=tb:
                                    h.dma_start(out=out_ap[tb * 128:(tb + 1) * 128, :], in_=y_sb[tb % 2][:],
                                                accum_op=ALU.add),
                                    waits=aw, inc=s_acc, inc_k=16)

            for tb in range(NTB):
                g1w = [(s_x, stores_end)] if tb == 0 else []
                if (tb - 2) in acc_done:
                    g1w.append((s_acc, acc_done[tb - 2]))
                g1 = p.op("gpsimd", lambda h, tb=tb:
                          h.indirect_dma_start(
                              out=y_sb[tb % 2][:], out_offset=None, in_=ybuf[:, :],
                              in_offset=bass.IndirectOffsetOnAxis(desti[:, tb:tb + 1], 0)),
                          waits=g1w, inc=s_gp, inc_k=16)
                g2w = [(s_gp, g1)]
                if (tb - 1) in comb_done:
                    g2w.append((s_ev, comb_done[tb - 1]))
                g2 = p.op("gpsimd", lambda h, tb=tb:
                          h.indirect_dma_start(
                              out=bgat[:], out_offset=None, in_=ybuf[:, :],
                              in_offset=bass.IndirectOffsetOnAxis(desti[:, E + tb:E + tb + 1], 0)),
                          waits=g2w, inc=s_gp, inc_k=16)
                t1v = p.op("vector", lambda h, tb=tb:
                           h.tensor_scalar(y_sb[tb % 2][:], y_sb[tb % 2][:], sp['c1a'][:, tb:tb + 1], None, ALU.mult),
                           waits=[(s_gp, g1)], inc=s_ev)
                comb_done[tb] = p.op("vector", lambda h, tb=tb:
                                     h.scalar_tensor_tensor(y_sb[tb % 2][:], bgat[:], sp['c2a'][:, tb:tb + 1],
                                                            y_sb[tb % 2][:], ALU.mult, ALU.add),
                                     waits=[(s_gp, g2), (s_ev, t1v)], inc=s_ev)
                if tb >= 1:
                    emit_acc(tb - 1)
            emit_acc(NTB - 1)
        return {tb: [(s_acc, acc_done[tb])] for tb in range(NTB)}

    q_dram = nc.dram_tensor("q_dram", [T, D], BF16, kind="Internal").ap()

    def sweep_phase(sw_w, k_tok, v133, kv_small, q_stage, kq_ps, v_ps, kv_ps, kv_bd,
                    ln_res, zero_done):
        res = {}
        last_kq_user = {}
        last_v_user = {}
        kv_mm_done = {}
        v133_done = {}
        relu_done = {}
        qdma_done = {}
        kv_evac = {}
        scat = {}
        first = True
        wv = {}
        def load_kv_w(g, kwaits, vwaits):
            kw_ = p.op("gpsimd", lambda h, g=g: h.dma_start(
                out=sw_w[0].rearrange("p (c w) -> p c w", c=NDC),
                in_=wqkvT.rearrange("(c p) n -> p c n", p=128)[:, :, D + 512 * g:D + 512 * (g + 1)]),
                waits=kwaits, inc=s_w, inc_k=16)
            vw_ = p.op("gpsimd", lambda h, g=g: h.dma_start(
                out=sw_w[1].rearrange("p (c w) -> p c w", c=NDC),
                in_=wqkvT.rearrange("(c p) n -> p c n", p=128)[:, :, 2 * D + 512 * g:2 * D + 512 * (g + 1)]),
                waits=vwaits, inc=s_w, inc_k=16)
            return kw_, vw_

        kvw_load = {0: load_kv_w(0, [], [])}
        qw_early = []
        for g in range(4):
            kw, vw = kvw_load[g]

            for tb in range(NTB):
                buf = tb % 2
                waits = [(s_w, kw)] if tb == 0 else []
                if g == 0:
                    waits.append((s_ev, ln_res['evac'][tb]))
                if buf in last_kq_user:
                    waits.append((s_act, last_kq_user[buf]))

                def kmm(h, g=g, tb=tb, buf=buf):
                    ins = None
                    for dc in range(NDC):
                        ins = h.matmul(kq_ps[buf][:], xn_fm[:, dc * T + tb * 128:dc * T + (tb + 1) * 128],
                                       sw_w[0][:, dc * 512:(dc + 1) * 512],
                                       start=(dc == 0), stop=(dc == NDC - 1))
                    return ins
                kmv = p.op("tensor", kmm, waits=waits, inc=s_mm)
                if tb == NTB - 1:
                    wv[('k', g)] = kmv
                rv = p.op("scalar", lambda h, tb=tb, buf=buf:
                          h.activation(k_tok[:, tb * 512:(tb + 1) * 512], kq_ps[buf][:], AF.Relu),
                          waits=[(s_mm, kmv)], inc=s_act)
                relu_done[('k', g, tb)] = rv
                last_kq_user[buf] = rv

            def kvmms(h, g=g, tb=None):
                ins = None
                for hp in range(16):
                    ins = h.matmul(kv_ps[:, 512 * (hp // 8) + 33 * (hp % 8):512 * (hp // 8) + 33 * (hp % 8) + 33],
                                   k_tok[:, tb * 512 + 32 * hp:tb * 512 + 32 * hp + 32],
                                   v133[tb % 2][:, 33 * hp:33 * hp + 33],
                                   start=(tb == 0 and hp % 8 == 0),
                                   stop=(tb == NTB - 1 and hp % 8 == 7))
                return ins

            for tb in range(NTB):
                buf = tb % 2
                waits = [(s_w, vw)] if tb == 0 else []
                if g == 0 and tb == 0:
                    waits += [(s_ev, ln_res['all_ev']), (s_act, ln_res['all_act'])]
                if buf in last_v_user:
                    waits.append((s_ev, last_v_user[buf]))

                def vmm(h, g=g, tb=tb, buf=buf):
                    ins = None
                    for dc in range(NDC):
                        ins = h.matmul(v_ps[buf][:], xn_fm[:, dc * T + tb * 128:dc * T + (tb + 1) * 128],
                                       sw_w[1][:, dc * 512:(dc + 1) * 512],
                                       start=(dc == 0), stop=(dc == NDC - 1))
                    return ins
                vmv = p.op("tensor", vmm, waits=waits, inc=s_mm)
                if tb == NTB - 1:
                    wv[('v', g)] = vmv
                cwaits = [(s_mm, vmv)]
                if (g, tb - 2) in kv_mm_done:
                    cwaits.append((s_mm, kv_mm_done[(g, tb - 2)]))
                elif tb < 2 and (g - 1, NTB - 2 + tb) in kv_mm_done:
                    cwaits.append((s_mm, kv_mm_done[(g - 1, NTB - 2 + tb)]))
                cv = p.op("vector", lambda h, tb=tb, buf=buf:
                          h.tensor_copy(v133[buf].rearrange("p (i j) -> p i j", i=16)[:, :, 0:32],
                                        v_ps[buf].rearrange("p (i j) -> p i j", i=16)),
                          waits=cwaits, inc=s_ev)
                v133_done[(g, tb)] = cv
                last_v_user[buf] = cv
                if tb >= 1:
                    kvw = [(s_ev, v133_done[(g, tb - 1)]), (s_act, relu_done[('k', g, tb - 1)])]
                    if tb == 1 and g > 0:
                        kvw.append((s_ev, kv_evac[g - 1]))
                    kv_mm_done[(g, tb - 1)] = p.op("tensor", lambda h, g=g, tb=tb - 1: kvmms(h, g=g, tb=tb),
                                                   waits=kvw, inc=s_mm)
            kvw = [(s_ev, v133_done[(g, NTB - 1)]), (s_act, relu_done[('k', g, NTB - 1)])]
            kv_mm_done[(g, NTB - 1)] = p.op("tensor", lambda h, g=g: kvmms(h, g=g, tb=NTB - 1),
                                            waits=kvw, inc=s_mm)
            if g + 1 < 4:
                kvw_load[g + 1] = load_kv_w(g + 1, [(s_mm, wv[('k', g)])],
                                            [(s_mm, wv[('v', g)])])
            elif g == 3:
                qw_early.append(p.op("gpsimd", lambda h: h.dma_start(
                    out=sw_w[0].rearrange("p (c w) -> p c w", c=NDC),
                    in_=wqkvT.rearrange("(c p) n -> p c n", p=128)[:, :, 0:512]),
                    waits=[(s_mm, wv[('k', 3)])], inc=s_w, inc_k=16))
                qw_early.append(p.op("gpsimd", lambda h: h.dma_start(
                    out=sw_w[1].rearrange("p (c w) -> p c w", c=NDC),
                    in_=wqkvT.rearrange("(c p) n -> p c n", p=128)[:, :, 512:1024]),
                    waits=[(s_mm, wv[('v', 3)])], inc=s_w, inc_k=16))
            kv_evac[g] = p.op("vector", lambda h:
                              h.tensor_copy(kv_small[0:32, :].rearrange("p (b w) -> p b w", b=2),
                                            kv_ps[0:32, :].rearrange("p (b w) -> p b w", b=2)[:, :, 0:264]),
                              waits=[(s_mm, kv_mm_done[(g, NTB - 1)])], inc=s_ev)
            for a in range(4):
                scat[g] = p.op("gpsimd", lambda h, g=g, a=a: h.dma_start(
                    out=bass.AP(tensor=cc_in, offset=(32 * a) * 2112 + 528 * g + 33 * a,
                                ap=[[2112, 32], [132, 4], [1, 33]]),
                    in_=cap(kv_small, 33 * a, [[132, 4], [1, 33]])),
                    waits=([(s_ev, kv_evac[g]), (s_init, zero_done[g])] if a == 0 else []),
                    inc=s_gp, inc_k=16)

        def load_q_w(g):
            return p.op("gpsimd", lambda h, g=g: h.dma_start(
                out=sw_w[g % 2].rearrange("p (c w) -> p c w", c=NDC),
                in_=wqkvT.rearrange("(c p) n -> p c n", p=128)[:, :, 512 * g:512 * (g + 1)]),
                waits=[(s_mm, wv[('k', 3) if g == 0 else ('v', 3)])] if g < 2 else [(s_mm, wv[('q', g - 2)])],
                inc=s_w, inc_k=16)

        qw_load = {0: qw_early[0], 1: qw_early[1]}
        ccv = p.op("gpsimd", lambda h: h.collective_compute(
            "AllReduce", ALU.add, replica_groups=[list(range(8))],
            ins=[cc_in.ap().opt()], outs=[cc_out.ap().opt()]),
            waits=[(s_gp, scat[3])], inc=s_gp, inc_k=1)

        for g in range(4):
            qw = qw_load[g]
            for tb in range(NTB):
                buf = tb % 2
                waits = [(s_w, qw)] if tb == 0 else []
                if buf in last_kq_user:
                    waits.append((s_act, last_kq_user[buf]))

                def qmm(h, g=g, tb=tb, buf=buf):
                    ins = None
                    for dc in range(NDC):
                        ins = h.matmul(kq_ps[buf][:], xn_fm[:, dc * T + tb * 128:dc * T + (tb + 1) * 128],
                                       sw_w[g % 2][:, dc * 512:(dc + 1) * 512],
                                       start=(dc == 0), stop=(dc == NDC - 1))
                    return ins
                qmv = p.op("tensor", qmm, waits=waits, inc=s_mm)
                if tb == NTB - 1:
                    wv[('q', g)] = qmv
                rwaits = [(s_mm, qmv)]
                if (g, tb - 2) in qdma_done:
                    rwaits.append((s_x, qdma_done[(g, tb - 2)]))
                elif tb < 2 and (g - 1, NTB - 2 + tb) in qdma_done:
                    rwaits.append((s_x, qdma_done[(g - 1, NTB - 2 + tb)]))
                rv = p.op("scalar", lambda h, tb=tb, buf=buf:
                          h.activation(q_stage[buf][:], kq_ps[buf][:], AF.Relu),
                          waits=rwaits, inc=s_act)
                relu_done[('q', g, tb)] = rv
                last_kq_user[buf] = rv
                qdma_done[(g, tb)] = p.op("sync", lambda h, g=g, tb=tb, buf=buf:
                                          h.dma_start(out=q_dram[tb * 128:(tb + 1) * 128, 512 * g:512 * (g + 1)],
                                                      in_=q_stage[buf][:]),
                                          waits=[(s_act, rv)], inc=s_x, inc_k=16)
            if g < 2:
                qw_load[g + 2] = load_q_w(g + 2)
            if g == 0:
                res['kv_bd_w'] = p.op("gpsimd", lambda h: h.dma_start(out=kv_bd[:], in_=cc_out[:, :]),
                                      waits=[(s_gp, ccv)], inc=s_w, inc_k=16)
        res['q_done'] = qdma_done[(3, NTB - 1)]
        res['q_mm_done'] = wv[('q', 3)]
        return res

    def att_phase(wproj_bf, kv_bd, q_tb, q_fm_tb, attn_tok, attn_fm, bias_bc,
                  dens, tq_ps, att_ps, y_ps, sw, ln2_loads, ffn1_acc_end,
                  dump_q=None, dump_attn=None):
        wpv = p.op("gpsimd", lambda h: h.dma_start(
            out=wproj_bf.rearrange("p (c w) -> p c w", c=NDC),
            in_=wprojT.rearrange("(c p) n -> p c n", p=128)),
            waits=[(s_mm, sw['q_mm_done']), (s_x, sw['q_done'])], inc=s_w, inc_k=16)
        bbv = p.op("gpsimd", lambda h: h.dma_start(out=bias_bc[:], in_=bproj.broadcast_to([128, D])),
                   inc=s_w, inc_k=16)
        SUBQ = [(0, 0, 6, 3), (1, 6, 6, 3), (0, 12, 4, 2)]
        qload = {}; qt_done = {}; qfm_ev = {}; att_done = {}; norm_done = {}
        afm_ev = {}; at_done = {}; yev_done = {}; acc_done = {}
        for tb in range(NTB):
            lw = [(s_x, sw['q_done'])] if tb == 0 else []
            if tb >= 2:
                lw.append((s_mm, qt_done[tb - 2]))
            qload[tb] = p.op("sync", lambda h, tb=tb:
                             h.dma_start(out=q_tb[tb % 2][:], in_=q_dram[tb * 128:(tb + 1) * 128, :]),
                             waits=lw, inc=s_x, inc_k=16)
            tw = [(s_x, qload[tb])]
            if tb >= 1:
                tw.append((s_ev, afm_ev[tb - 1]))
            last = None
            for half in range(2):
                def qtr(h, tb=tb, half=half):
                    ins = None
                    for k in range(8):
                        qc = half * 8 + k
                        ins = h.transpose(tq_ps[half][:, k * 128:(k + 1) * 128],
                                         q_tb[tb % 2][:, qc * 128:(qc + 1) * 128], ident_bf[:])
                    return ins
                last = p.op("tensor", qtr, waits=tw if half == 0 else [], inc=s_mm)
            qt_done[tb] = last
            evw0 = [(s_mm, qt_done[tb] - 1)]
            evw1 = [(s_mm, qt_done[tb])]
            if (tb - 2, 2) in att_done:
                evw0.append((s_mm, att_done[(tb - 2, 2)]))
            p.op("vector", lambda h, tb=tb:
                 h.tensor_copy(q_fm_tb[tb % 2][:, 0:1024], tq_ps[0][:]),
                 waits=evw0, inc=s_ev)
            qfm_ev[tb] = p.op("vector", lambda h, tb=tb:
                              h.tensor_copy(q_fm_tb[tb % 2][:, 1024:2048], tq_ps[1][:]),
                              waits=evw1, inc=s_ev)

            for si, (abuf, qc0, n, kdim) in enumerate(SUBQ):
                mwaits = [(s_ev, qfm_ev[tb])]
                if tb == 0 and si == 0:
                    mwaits.append((s_w, sw['kv_bd_w']))
                key = {0: (tb - 1, 2), 1: (tb - 1, 1), 2: (tb, 0)}[si]
                if key in norm_done:
                    mwaits.append((s_ev, norm_done[key]))

                def amms(h, tb=tb, abuf=abuf, qc0=qc0, n=n, kdim=kdim):
                    ins = None
                    for i in range(n):
                        qc = qc0 + i
                        col = 512 * (i // kdim) + 132 * (i % kdim)
                        ins = h.matmul(att_ps[abuf][:, col:col + 132],
                                       q_fm_tb[tb % 2][:, qc * 128:(qc + 1) * 128],
                                       kv_bd[:, qc * 132:(qc + 1) * 132],
                                       start=True, stop=True)
                    return ins
                amv = p.op("tensor", amms, waits=mwaits, inc=s_mm)
                att_done[(tb, si)] = amv
                g1 = p.op("vector", lambda h, abuf=abuf, n=n, kdim=kdim:
                          h.tensor_copy(dens[:, 0:4 * n],
                                        cap(att_ps[abuf], 32, [[512, 2], [132, kdim], [33, 4]])),
                          waits=[(s_mm, amv)], inc=s_ev)
                g2 = p.op("vector", lambda h, n=n:
                          h.tensor_scalar(dens[:, 0:4 * n], dens[:, 0:4 * n], EPS_ATT, None, ALU.add),
                          waits=[(s_ev, g1)], inc=s_ev)
                g3 = p.op("vector", lambda h, n=n: h.reciprocal(dens[:, 32:32 + 4 * n], dens[:, 0:4 * n]),
                          waits=[(s_ev, g2)], inc=s_ev)
                nwaits = [(s_ev, g3)]
                if si == 0 and (tb - 1) in at_done:
                    nwaits.append((s_mm, at_done[tb - 1]))
                nv = p.op("vector", lambda h, tb=tb, abuf=abuf, qc0=qc0, n=n, kdim=kdim:
                          h.tensor_tensor(
                              cap(attn_tok, qc0 * 128, [[128 * kdim, 2], [128, kdim], [32, 4], [1, 32]]),
                              cap(att_ps[abuf], 0, [[512, 2], [132, kdim], [33, 4], [1, 32]]),
                              cap(dens, 32, [[4 * kdim, 2], [4, kdim], [1, 4], [0, 32]]),
                              ALU.mult),
                          waits=nwaits, inc=s_ev)
                norm_done[(tb, si)] = nv

            atw = [(s_ev, norm_done[(tb, 2)]), (s_ev, qfm_ev[tb])]
            last = None
            for half in range(2):
                def atr(h, tb=tb, half=half):
                    ins = None
                    for k in range(8):
                        cc = half * 8 + k
                        ins = h.transpose(tq_ps[half][:, k * 128:(k + 1) * 128],
                                         attn_tok[:, cc * 128:(cc + 1) * 128], ident_bf[:])
                    return ins
                last = p.op("tensor", atr, waits=atw if half == 0 else [], inc=s_mm)
            at_done[tb] = last
            p.op("vector", lambda h, tb=tb: h.tensor_copy(attn_fm[:, 0:1024], tq_ps[0][:]),
                 waits=[(s_mm, at_done[tb] - 1)], inc=s_ev)
            afm_ev[tb] = p.op("vector", lambda h, tb=tb: h.tensor_copy(attn_fm[:, 1024:2048], tq_ps[1][:]),
                              waits=[(s_mm, at_done[tb])], inc=s_ev)

            for sweep in range(2):
                pw = [(s_ev, afm_ev[tb])]
                if tb == 0 and sweep == 0:
                    pw.append((s_w, wpv))
                key = (tb, sweep - 1) if sweep == 1 else (tb - 1, 1)
                if key in yev_done:
                    pw.append((s_ev, yev_done[key]))

                def pmms(h, tb=tb, sweep=sweep):
                    ins = None
                    for cc in range(NDC):
                        for n2 in range(2):
                            n0 = sweep * 1024 + n2 * 512
                            ins = h.matmul(y_ps[:, n2 * 512:(n2 + 1) * 512],
                                           attn_fm[:, cc * 128:(cc + 1) * 128],
                                           wproj_bf[:, cc * D + n0:cc * D + n0 + 512],
                                           start=(cc == 0), stop=(cc == NDC - 1))
                    return ins
                pv = p.op("tensor", pmms, waits=pw, inc=s_mm)
                ew = [(s_mm, pv)]
                if tb == 0 and sweep == 0:
                    ew.append((s_acc, ffn1_acc_end))
                if (tb - 2) in acc_done:
                    ew.append((s_acc, acc_done[tb - 2]))
                yev_done[(tb, sweep)] = p.op("vector", lambda h, tb=tb, sweep=sweep:
                                             h.tensor_tensor(y_sb[tb % 2][:, sweep * 1024:(sweep + 1) * 1024],
                                                             y_ps[:], bias_bc[:, sweep * 1024:(sweep + 1) * 1024],
                                                             ALU.add),
                                             waits=ew, inc=s_ev)
            aw = [(s_ev, yev_done[(tb, 1)])]
            if tb == 0:
                aw.append((s_x, ln2_loads))
            acc_done[tb] = p.op("gpsimd", lambda h, tb=tb:
                                h.dma_start(out=out[tb * 128:(tb + 1) * 128, :], in_=y_sb[tb % 2][:],
                                            accum_op=ALU.add),
                                waits=aw, inc=s_acc, inc_k=16)
            if dump_attn is not None:
                p.wait("sync", s_ev, norm_done[(tb, 2)])
                p.op("sync", lambda h, tb=tb: h.dma_start(out=dump_attn[:, tb * D:(tb + 1) * D], in_=attn_tok[:]),
                     inc=s_dump, inc_k=16)
        return {tb: [(s_acc, acc_done[tb])] for tb in range(NTB)}

    def dump(name, src):
        ap = dump_aps.get(name)
        if ap is None:
            return
        p.wait("sync", s_ev, s_ev.n)
        p.wait("sync", s_mm, s_mm.n)
        p.wait("sync", s_act, s_act.n)
        p.op("sync", lambda h: h.dma_start(out=ap[:, :], in_=src[:]), inc=s_dump, inc_k=16)

    zrow = xs[0]
    zv = p.op("vector", lambda h: h.memset(zrow[:, 0:528], 0.0), inc=s_ev)
    zero_done = {}
    for g in range(4):
        zero_done[g] = p.op("sync", lambda h, g=g:
                            h.dma_start(out=cc_in[:, 528 * g:528 * (g + 1)], in_=zrow[:, 0:528]),
                            waits=[(s_ev, zv)], inc=s_init, inc_k=16)

    uid = [0]

    def run_ln(src, init_out, with_router, flw, per_tb=None, sparse=None):
        uid[0] += 1
        u = uid[0]
        with (
            nc.psum_tensor(f"pt0_{u}", [128, 1024], BF16, side="right") as _pt0,
            nc.psum_tensor(f"pt1_{u}", [128, 1024], BF16, side="right") as _pt1,
            nc.psum_tensor(f"sc0_{u}", [128, 512], F32, side="right") as _sc0,
            nc.psum_tensor(f"sc1_{u}", [128, 512], F32, side="right") as _sc1,
        ):
            return ln_phase(src, init_out=init_out, with_router=with_router,
                            pt_ps=[_pt0, _pt1], sc_ps=[_sc0, _sc1],
                            first_load_waits=flw, per_tb_load_waits=per_tb,
                            start_tensor_waits=[(s_ev, s_ev.n), (s_act, s_act.n)],
                            sparse=sparse)

    def run_ffn(init_sem_val, ln_res, dump_h=None):
        uid[0] += 1
        u = uid[0]
        with (
            nc.sbuf_tensor(f"wg{u}", [128, NDC * FF], BF16) as wg,
            nc.sbuf_tensor(f"wu{u}", [128, NDC * FF], BF16) as wu,
            nc.sbuf_tensor(f"wd{u}", [128, NFB * D], BF16) as wd,
            nc.sbuf_tensor(f"h_fm{u}", [128, NFB * T], BF16) as h_fm,
            nc.sbuf_tensor(f"ssb0_{u}", [128, 1024], F32) as _s0,
            nc.sbuf_tensor(f"ssb1_{u}", [128, 1024], F32) as _s1,
            nc.psum_tensor(f"g_ps{u}", [128, 1024], F32, side="left") as g_ps,
            nc.psum_tensor(f"u_ps{u}", [128, 1024], F32, side="left") as u_ps,
            nc.psum_tensor(f"y0_ps{u}", [128, 1024], F32, side="right") as _y0,
            nc.psum_tensor(f"y1_ps{u}", [128, 1024], F32, side="right") as _y1,
        ):
            acc_done = ffn_phase(out, wg, wu, wd, h_fm, [_s0, _s1], g_ps, u_ps, [_y0, _y1],
                                 acc_base=s_acc.n, init_sem_val=init_sem_val,
                                 ln_res=ln_res, dump_h=dump_h)
            return {tb: [(s_acc, acc_done[(E - 1, tb)])] for tb in range(NTB)}

    zv2 = p.op("vector", lambda h: h.memset(y_sb[0][:], 0.0), inc=s_ev)
    p.op("gpsimd", lambda h: h.dma_start(
        out=ebuf[0:E * CAP, :].rearrange("(c p) w -> p c w", p=128),
        in_=y_sb[0][:].rearrange("p (one w) -> p one w", one=1).broadcast_to([128, 64, D])),
        waits=[(s_ev, zv2)], inc=s_w, inc_k=16)
    p.op("gpsimd", lambda h: h.dma_start(out=ebuf[E * CAP:E * CAP + 128, :], in_=y_sb[0][:]),
         inc=s_w, inc_k=16)
    zero_ebuf_val = p.op("gpsimd", lambda h: h.dma_start(out=ybuf[E * CAP:E * CAP + 128, :], in_=y_sb[0][:]),
                         inc=s_w, inc_k=16)

    def run_sparse_pass(src_ap, init_out, flw, per_tb, init_sem_fn):
        uid[0] += 1
        u = uid[0]
        with (
            nc.sbuf_tensor(f"stg0_{u}", [128, D], BF16) as _st0,
            nc.sbuf_tensor(f"stg1_{u}", [128, D], BF16) as _st1,
            nc.sbuf_tensor(f"sel1_{u}", [128, NTB * E], F32) as sel1,
            nc.sbuf_tensor(f"selB_{u}", [128, NTB * E], F32) as selB,
            nc.sbuf_tensor(f"c1a_{u}", [128, NTB], F32) as c1a,
            nc.sbuf_tensor(f"c2a_{u}", [128, NTB], F32) as c2a,
            nc.sbuf_tensor(f"dtk_{u}", [128, 2 * E], F32) as desttok,
            nc.sbuf_tensor(f"dti_{u}", [128, 2 * E], mybir.dt.int32) as desti,
        ):
            sp = dict(xne=xne_dram, stage=[_st0, _st1], sel1=sel1, selB=selB,
                      c1a=c1a, c2a=c2a)
            ln_res = run_ln(src_ap, init_out, True, flw, per_tb=per_tb, sparse=sp)
            return ffn_sparse(out, sp, ln_res, init_sem_fn(), zero_ebuf_val,
                              desttok, desti)

    if max_phase >= 2:
        ffn1_tb_acc = run_sparse_pass(x0, True, [(s_init, zero_done[3])], None,
                                      lambda: (s_init, s_init.n))
    if max_phase >= 3:
        ffn1_acc_end = s_acc.n
        with nc.sbuf_tensor("kv_bd", [128, NH * 33], BF16) as kv_bd:
          with nc.sbuf_tensor("xn_fm", [128, NDC * T], BF16) as _xnfm:
            xn_fm = _xnfm
            ln2_res = run_ln(out, False, False, [], per_tb=ffn1_tb_acc)
            ln2_loads = s_x.n
            if max_phase >= 4:
              with (
                nc.sbuf_tensor("sww0", [128, NDC * 512], BF16) as _w0,
                nc.sbuf_tensor("sww1", [128, NDC * 512], BF16) as _w1,
                nc.sbuf_tensor("k_tok", [128, NTB * 512], BF16) as k_tok,
                nc.sbuf_tensor("v133a", [128, 528], BF16) as _v0,
                nc.sbuf_tensor("v133b", [128, 528], BF16) as _v1,
                nc.sbuf_tensor("kv_small", [32, 528], F32) as kv_small,
                nc.sbuf_tensor("qst0", [128, 512], BF16) as _q0,
                nc.sbuf_tensor("qst1", [128, 512], BF16) as _q1,
                nc.psum_tensor("kq0", [128, 512], F32, side="left") as _kq0,
                nc.psum_tensor("kq1", [128, 512], F32, side="left") as _kq1,
                nc.psum_tensor("vp0", [128, 512], F32, side="right") as _vp0,
                nc.psum_tensor("vp1", [128, 512], F32, side="right") as _vp1,
                nc.psum_tensor("kv_ps", [32, 1024], F32, side="right") as kv_ps,
            ):
                for v1t in (_v0, _v1):
                    p.op("vector", lambda h, v1t=v1t: h.memset(
                        cap(v1t, 32, [[33, 16], [1, 1]]), 1.0))
                sw = sweep_phase([_w0, _w1], k_tok, [_v0, _v1], kv_small, [_q0, _q1],
                                 [_kq0, _kq1], [_vp0, _vp1], kv_ps, kv_bd,
                                 ln_res=ln2_res, zero_done=zero_done)
          if max_phase >= 5:
                with (
                    nc.sbuf_tensor("wproj_bf", [128, NDC * D], BF16) as wproj_bf,
                    nc.sbuf_tensor("qtb0", [128, D], BF16) as _qt0,
                    nc.sbuf_tensor("qtb1", [128, D], BF16) as _qt1,
                    nc.sbuf_tensor("qfm0", [128, D], BF16) as _qf0,
                    nc.sbuf_tensor("qfm1", [128, D], BF16) as _qf1,
                    nc.sbuf_tensor("attn_tok", [128, D], BF16) as attn_tok,
                    nc.sbuf_tensor("attn_fm", [128, D], BF16) as attn_fm,
                    nc.sbuf_tensor("bias_bc", [128, D], F32) as bias_bc,
                    nc.sbuf_tensor("dens", [128, 64], F32) as dens,
                    nc.psum_tensor("tq0", [128, 1024], BF16) as _tq0,
                    nc.psum_tensor("tq1", [128, 1024], BF16) as _tq1,
                    nc.psum_tensor("attA", [128, 1024], F32) as _aa,
                    nc.psum_tensor("attB", [128, 1024], F32) as _ab,
                    nc.psum_tensor("ypj", [128, 1024], F32) as ypj,
                ):
                    att_tb_acc = att_phase(wproj_bf, kv_bd, [_qt0, _qt1], [_qf0, _qf1],
                                           attn_tok, attn_fm, bias_bc, dens,
                                           [_tq0, _tq1], [_aa, _ab], ypj, sw,
                                           ln2_loads=ln2_loads, ffn1_acc_end=ffn1_acc_end,
                                           dump_attn=dump_aps.get("dbg_attn"))
    if max_phase >= 7:
        run_sparse_pass(out, False, [], att_tb_acc, lambda: (s_x, s_x.n))

    def fin(h):
        for s in (s_x, s_init, s_acc, s_dump):
            if s.n > 0:
                h.wait_ge(s.h, s.n)
    p.ops["sync"].append(fin)
    p.barrier(ph)

    with nc.Block() as block:
        @block.sync
        def _(h):
            for fn in p.ops["sync"]:
                fn(h)

        @block.gpsimd
        def _(h):
            for fn in p.ops["gpsimd"]:
                fn(h)

        @block.tensor
        def _(h):
            for fn in p.ops["tensor"]:
                fn(h)

        @block.vector
        def _(h):
            for fn in p.ops["vector"]:
                fn(h)

        @block.scalar
        def _(h):
            for fn in p.ops["scalar"]:
                fn(h)

    for c in reversed(sb_ctx):
        c.__exit__(None, None, None)
    for c in reversed(sem_ctxs):
        c.__exit__(None, None, None)
    return nc



def _prep_inputs(L0_attn_x, L0_attn_w_qkv, L0_attn_w_proj, L0_attn_b_proj,
                 L0_ffn_router_w, L0_ffn_gate_w, L0_ffn_up_w, L0_ffn_down_w):
    x = np.ascontiguousarray(np.asarray(L0_attn_x, dtype=np.float32)[0])
    shards = np.split(x, 8, axis=0)
    wqkvT = np.ascontiguousarray(np.asarray(L0_attn_w_qkv, dtype=np.float32).T)
    wprojT = np.ascontiguousarray(np.asarray(L0_attn_w_proj, dtype=np.float32).T)
    bproj = np.ascontiguousarray(np.asarray(L0_attn_b_proj, dtype=np.float32)[None, :])
    router = np.ascontiguousarray(
        np.asarray(L0_ffn_router_w, dtype=np.float32).reshape(NDC, 128, E).transpose(1, 0, 2).reshape(128, NDC * E))
    gate = np.ascontiguousarray(np.asarray(L0_ffn_gate_w, dtype=np.float32))
    up = np.ascontiguousarray(np.asarray(L0_ffn_up_w, dtype=np.float32))
    down = np.ascontiguousarray(np.asarray(L0_ffn_down_w, dtype=np.float32))
    ident_np = np.eye(128, dtype=ml_dtypes.bfloat16)
    ident32_np = np.eye(128, dtype=np.float32)
    ebase_np = (np.arange(16, dtype=np.float32) * 512.0).reshape(16, 1)
    common = dict(wqkvT=wqkvT, wprojT=wprojT, bproj=bproj, router=router,
                  gate_w=gate, up_w=up, down_w=down, ident=ident_np,
                  ident32=ident32_np, ebase=ebase_np)
    return [dict(x0=shards[i], **common) for i in range(8)]


def kernel(**inputs):
    in_maps = _prep_inputs(**inputs)
    nc = build_nc()
    res = run_bass_kernel_spmd(nc, in_maps, core_ids=list(range(8)))
    outv = np.concatenate([np.asarray(r["out"]) for r in res.results], axis=0)
    return outv[None, :, :]
